# version 16 (final)
# baseline (speedup 1.0000x reference)
import contextlib
import numpy as np
import concourse.bass as bass
import concourse.mybir as mybir
from concourse.bass_utils import run_bass_kernel_spmd

F32 = mybir.dt.float32
BF16 = mybir.dt.bfloat16
AF = mybir.ActivationFunctionType
ALU = mybir.AluOpType
AX = mybir.AxisListType

T = 4096
D = 1024
NT = T // 128
EPS = 1e-5

ALL_PASSES = ("l0attn_a", "l0ffn_1", "l0attn_b", "l0ffn_2",
              "l1attn_a", "l1ffn_1", "l1attn_b", "l1ffn_2")


class Ctx:

    def __init__(self, nc):
        self.nc = nc
        self._guards = []
        self._frames = []
        self._frame_sems = []
        self.sems = {}
        self.counts = {}

    def frame(self):
        self._frames.append(contextlib.ExitStack())
        self._frame_sems.append([])

    def tframe(self):
        self._tframes = getattr(self, "_tframes", [])
        self._tframes.append(contextlib.ExitStack())

    def end_tframe(self):
        self._tframes.pop().close()

    def end_frame(self):
        self.drain_all()
        self.nc.all_engine_barrier()
        self._frames.pop().close()
        names = self._frame_sems.pop()
        if names:
            self.nc.clear_and_free_semaphores([self.sems[n] for n in names])
            self.nc.all_engine_barrier()
        for name in names:
            del self.sems[name]
            del self.counts[name]

    def sbuf(self, name, shape, dtype):
        stack = (self._tframes[-1] if getattr(self, "_tframes", None)
                 else self._frames[-1])
        return stack.enter_context(self.nc.sbuf_tensor(name, shape, dtype))

    def psum(self, name, shape, dtype):
        return self._frames[-1].enter_context(self.nc.psum_tensor(name, shape, dtype))

    def sem(self, name):
        if name not in self.sems:
            h = self.nc.alloc_semaphore(name=name)
            if self._frames:
                self._frame_sems[-1].append(name)
            self.sems[name] = h
            self.counts[name] = 0
        return self.sems[name]

    def inc(self, inst, name, by=1):
        inst.then_inc(self.sem(name), by)
        self.counts[name] += by
        return self.counts[name]

    def wait(self, eng, name, val=None):
        eng.wait_ge(self.sem(name), self.counts[name] if val is None else val)

    def count(self, name):
        self.sem(name)
        return self.counts[name]

    def drain_all(self):
        if self._frames:
            names = self._frame_sems[-1]
        else:
            names = list(self.counts)
        for name in names:
            if self.counts.get(name, 0) > 0:
                self.nc.sync.wait_ge(self.sems[name], self.counts[name])



def emit_ln_tile(C, xt, ln_bf, stats, scr, want_f32=None, sem="lnp", eng_sem_val=None):
    nc = C.nc
    v, s = nc.vector, nc.scalar
    v.bn_stats(stats[:, 0:6], xt[:, 0:512])
    C.inc(v.bn_stats(stats[:, 6:12], xt[:, 512:1024]), sem)
    C.wait(v, sem)
    C.inc(v.bn_aggr(stats[:, 12:14], stats[:, 0:12]), sem)
    C.wait(v, sem)
    C.inc(v.tensor_scalar(stats[:, 14:15], stats[:, 12:13], -1.0, None, ALU.mult), sem)
    C.wait(s, sem)
    C.inc(s.activation(scr[:, 0:1], stats[:, 13:14], AF.Ln, bias=C.eps_col[:, 0:1]), sem)
    C.wait(s, sem)
    C.inc(s.activation(stats[:, 15:16], scr[:, 0:1], AF.Exp, scale=-0.5), sem)
    C.wait(v, sem)
    C.inc(v.tensor_scalar(ln_bf[:, :], xt[:, :], stats[:, 14:15], stats[:, 15:16],
                          ALU.add, ALU.mult), sem)
    if want_f32 is not None:
        C.inc(v.tensor_scalar(want_f32[:, :], xt[:, :], stats[:, 14:15], stats[:, 15:16],
                              ALU.add, ALU.mult), sem)
    return C.count(sem)


def emit_transpose_tile(C, src_bf, dst_psum_bf, n=8, src_off=0):
    nc = C.nc
    t = None
    for j in range(n):
        t = nc.tensor.transpose(dst_psum_bf[:, j * 128:(j + 1) * 128],
                                src_bf[:, src_off + j * 128:src_off + (j + 1) * 128],
                                C.ident_bf[:, :])
    return t


def load_weight_fm(C, dst_sbuf, src_hbm, din, dout, sem=None, queue=None):
    nc = C.nc
    q = queue or nc.gpsimd
    nchunk = din // 128
    ap = src_hbm.rearrange("(c p) o -> p c o", p=128)
    dst = dst_sbuf[:, 0:nchunk * dout].rearrange("p (c o) -> p c o", c=nchunk)
    C._gld = getattr(C, "_gld", 0) + 1
    name = f"gld{C._gld}"
    C.inc(q.dma_start(dst, ap), name, 16)
    return name


def prep_weight_T(C, p, wT, src_ap, sem, prev_cp=None, stage=None, prev_tp=None):
    nc = C.nc
    v, te = nc.vector, nc.tensor
    if stage is None:
        stage = C.sbuf(p + "_stage", [128, 8 * 1024], BF16)
    ap = src_ap.rearrange("(c p) o -> p c o", p=128)
    if prev_tp is not None:
        C.wait(nc.gpsimd, prev_tp)
    C.inc(nc.gpsimd.dma_start(stage[:, :].rearrange("p (c o) -> p c o", c=8), ap),
          sem + "ld", 16)
    C.wait(te, sem + "ld")
    for j in range(8):
        if j > 0:
            C.wait(te, sem + "cp")
        elif prev_cp is not None:
            C.wait(te, prev_cp)
        C.inc(emit_transpose_tile(C, stage, C.psT[0], src_off=j * 1024), sem + "tp")
        C.wait(v, sem + "tp")
        for i in range(8):
            C.inc(v.tensor_copy(wT[:, i * 1024 + j * 128:i * 1024 + (j + 1) * 128],
                                C.psT[0][:, i * 128:(i + 1) * 128]), sem + "cp")



def emit_l1ffn(C, x_in, x_out, first, final_out=None):
    nc = C.nc
    v, s, te = nc.vector, nc.scalar, nc.tensor
    p = "tf1" if first else "tf2"

    wT = C.sbuf(p + "_wT", [128, 8 * 1024], BF16)
    if first:
        wload = C.sbuf(p + "_wload", [128, 8 * 1024], F32)
        sgn = C.sbuf(p + "_sgn", [128, 3 * 1024], F32)
        wt_bf = C.sbuf(p + "_wtbf", [128, 1024], BF16)
        asum = C.sbuf(p + "_asum", [128, 9], F32)
        dcol = C.sbuf(p + "_dcol", [128, 2], F32)
        ones_col = C.sbuf(p + "_ones", [128, 1], F32)
        dps = C.psF[2][0:1, 0:1]
        bps = C.psF[2][:, 512:513]
        tps = C.psT[0]

        ap = C.in_ap["L1_ffn_weight"].rearrange("(c p) o -> p c o", p=128)
        C.inc(nc.sync.dma_start(wload[:, :].rearrange("p (c o) -> p c o", c=8), ap),
              p + "w", 16)
        C.wait(v, p + "w")
        for j in range(8):
            C.inc(v.tensor_reduce(asum[:, j:j + 1], wload[:, j * 1024:(j + 1) * 1024],
                                  AX.X, ALU.add, apply_absolute_value=True), p + "a")
        C.wait(v, p + "a")
        C.inc(v.tensor_reduce(asum[:, 8:9], asum[:, 0:8], AX.X, ALU.add), p + "a")
        C.inc(v.memset(ones_col[:, :], 1.0), p + "a")
        C.wait(te, p + "a")
        C.inc(te.matmul(dps, asum[:, 8:9], ones_col[:, :], start=True, stop=True),
              p + "a")
        C.wait(s, p + "a")
        C.inc(s.activation(dcol[0:1, 0:1], dps, AF.Identity,
                           scale=0.7 / (1024.0 * 1024.0)), p + "a")
        C.wait(te, p + "a")
        C.inc(te.matmul(bps, C.ones_f32[0:1, 0:128], dcol[0:1, 0:1],
                        start=True, stop=True), p + "a")
        C.wait(v, p + "a")
        C.inc(v.tensor_copy(dcol[:, 0:1], bps), p + "a")
        C.wait(v, p + "a")
        C.inc(v.tensor_scalar(dcol[:, 1:2], dcol[:, 0:1], -1.0, None, ALU.mult), p + "a")
        C.wait(s, p + "a")
        for j in range(8):
            wj = wload[:, j * 1024:(j + 1) * 1024]
            if j > 0:
                C.wait(s, p + "t")
            C.inc(s.sign(sgn[:, 0:1024], wj, bias=dcol[:, 1:2]), p + "s")
            C.inc(s.sign(sgn[:, 1024:2048], wj, bias=dcol[:, 0:1]), p + "s")
            C.wait(v, p + "s")
            if j > 0:
                C.wait(v, p + "tp", j)
            C.inc(v.tensor_tensor(sgn[:, 2048:3072], sgn[:, 0:1024], sgn[:, 1024:2048],
                                  ALU.add), p + "g")
            C.inc(v.tensor_scalar(wt_bf[:, :], sgn[:, 2048:3072], 0.5, None, ALU.mult),
                  p + "g")
            C.wait(te, p + "g")
            if j > 0:
                C.wait(te, p + "t")
            C.inc(emit_transpose_tile(C, wt_bf, tps), p + "tp")
            C.wait(v, p + "tp")
            for i in range(8):
                C.inc(v.tensor_copy(wT[:, i * 1024 + j * 128:i * 1024 + (j + 1) * 128],
                                    tps[:, i * 128:(i + 1) * 128]), p + "t")
        C.wait(nc.sync, p + "t")
        C.inc(nc.sync.dma_start(C.tern_scratch.ap()[:, :], wT[:, :]), p + "w", 16)
        if getattr(C, "dbg_wT", None) is not None:
            C.inc(nc.sync.dma_start(C.dbg_wT[:, :], wT[:, :]), p + "w", 16)
            C.inc(nc.sync.dma_start(C.dbg_dcol[:, :], dcol[:, :]), p + "w", 16)
            C.inc(nc.sync.dma_start(C.dbg_asum[:, :], asum[:, :]), p + "w", 16)
            C.inc(nc.sync.dma_start(C.dbg_ones[:, :], C.ones_f32[:, :]), p + "w", 16)
    else:
        C.inc(nc.sync.dma_start(wT[:, :], C.tern_scratch.ap()[:, :]), p + "w", 16)

    alpha_col = C.sbuf(p + "_alpha", [128, 1], F32)
    bias_rep = C.sbuf(p + "_bias", [128, 1024], F32)
    C.inc(nc.sync.dma_start(alpha_col[:, :], C.in_ap["c_alpha_rep"][:, :]), p + "w", 16)
    C.inc(nc.sync.dma_start(bias_rep[:, :], C.in_ap["c_l1ffn_bias_rep"][:, :]), p + "w", 16)

    xt = [C.sbuf(f"{p}_xt{i}", [128, D], F32) for i in range(3)]
    lnb = [C.sbuf(f"{p}_ln{i}", [128, D], BF16) for i in range(2)]
    fm = [C.sbuf(f"{p}_fm{i}", [128, D], BF16) for i in range(2)]
    xo = [C.sbuf(f"{p}_xo{i}", [128, D], F32) for i in range(2)]
    stats = C.sbuf(p + "_st", [128, 16], F32)
    scr = C.sbuf(p + "_scr", [128, 1], F32)
    psT = C.psT
    pso = C.psF

    gp = nc.gpsimd

    def ldx(i):
        if i >= 3:
            C.wait(nc.sync, p + "ep", 2 * (i - 3) + 2)
        C.inc(nc.sync.dma_start(xt[i % 3][:, :], x_in[i * 128:(i + 1) * 128, :]),
              p + "ld", 16)

    def ln_tile(i):
        b = i % 2
        C.wait(v, p + "ld", 16 * (i + 1))
        C.wait(s, p + "ld", 16 * (i + 1))
        C.wait(v, p + "w")
        if i >= 2:
            C.wait(v, p + "stp", i - 1)
        emit_ln_tile(C, xt[i % 3], lnb[b], stats, scr, sem=p + "ln")
        C.wait(te, p + "ln", 6 * (i + 1))
        if i >= 2:
            C.wait(te, p + "ev", i - 1)
        elif first and i == 0:
            C.wait(te, p + "t")
        C.inc(emit_transpose_tile(C, lnb[b], psT[b]), p + "stp")
        C.wait(v, p + "stp", i + 1)
        if i >= 2:
            C.wait(v, p + "mm", i - 1)
        C.inc(v.tensor_copy(fm[b][:, :], psT[b][:, :]), p + "ev")

    ldx(0)
    ldx(1)
    ln_tile(0)
    for i in range(NT):
        b = i % 2
        if i + 2 < NT:
            ldx(i + 2)
        if i + 1 < NT:
            ln_tile(i + 1)
        C.wait(te, p + "ev", i + 1)
        C.wait(te, p + "w")
        if i >= 2:
            C.wait(te, p + "ep", 2 * (i - 2) + 1)
        mm = None
        for half in range(2):
            for j in range(8):
                mm = te.matmul(pso[b][:, half * 512:(half + 1) * 512],
                               fm[b][:, j * 128:(j + 1) * 128],
                               wT[:, j * 1024 + half * 512:j * 1024 + (half + 1) * 512],
                               start=(j == 0), stop=(j == 7))
        C.inc(mm, p + "mm")
        C.wait(v, p + "mm", i + 1)
        if i >= 2:
            C.wait(v, p + "res", 16 * (i - 1))
        C.inc(v.scalar_tensor_tensor(xo[b][:, :], pso[b][:, :], alpha_col[:, 0:1],
                                     bias_rep[:, :], ALU.mult, ALU.add), p + "ep")
        C.inc(v.tensor_tensor(xo[b][:, :], xo[b][:, :], xt[i % 3][:, :], ALU.add),
              p + "ep")
        C.wait(gp, p + "ep", 2 * (i + 1))
        C.inc(gp.dma_start(x_out[i * 128:(i + 1) * 128, :], xo[b][:, :]),
              p + "res", 16)
    C.wait(nc.sync, p + "res")



def emit_l0ffn(C, x_in, x_out, first):
    nc = C.nc
    v, s, te = nc.vector, nc.scalar, nc.tensor
    p = "cf1" if first else "cf2"

    wrT = C.sbuf(p + "_wrT", [128, 8 * 1024], BF16)
    wkT = C.sbuf(p + "_wkT", [128, 8 * 1024], BF16)
    wvT = C.sbuf(p + "_wvT", [128, 8 * 1024], BF16)
    mu = C.sbuf(p + "_mu", [128, 16], F32)
    C.inc(nc.sync.dma_start(mu[:, :], C.in_ap["c_mu_fm"][:, :]), p + "w", 16)
    if first:
        prep_weight_T(C, p + "wr", wrT, C.in_ap["L0_ffn_W_r"], p + "r")
        prep_weight_T(C, p + "wk", wkT, C.in_ap["L0_ffn_W_k"], p + "k",
                      prev_cp=p + "rcp")
        prep_weight_T(C, p + "wv", wvT, C.in_ap["L0_ffn_W_v"], p + "v",
                      prev_cp=p + "kcp")
        for nm, wt in (("r", wrT), ("k", wkT), ("v", wvT)):
            C.wait(nc.sync, p + nm + "cp")
            C.inc(nc.sync.dma_start(C.l0ffn_scratch[nm].ap()[:, :], wt[:, :]),
                  p + "w", 16)
    else:
        for nm, wt in (("r", wrT), ("k", wkT), ("v", wvT)):
            C.inc(nc.sync.dma_start(wt[:, :], C.l0ffn_scratch[nm].ap()[:, :]),
                  p + "w", 16)

    xt = [C.sbuf(f"{p}_xt{i}", [128, D], F32) for i in range(2)]
    lnb = [C.sbuf(f"{p}_ln{i}", [128, D], BF16) for i in range(2)]
    fm = [C.sbuf(f"{p}_fm{i}", [128, D], BF16) for i in range(2)]
    shf = C.sbuf(p + "_shf", [128, D], BF16)
    dfm = C.sbuf(p + "_dfm", [128, D], BF16)
    xr = C.sbuf(p + "_xr", [128, D], BF16)
    xk = C.sbuf(p + "_xk", [128, D], BF16)
    rr = C.sbuf(p + "_rr", [128, D], F32)
    krel = C.sbuf(p + "_krel", [128, D], F32)
    kact = C.sbuf(p + "_kact", [128, D], BF16)
    kfm = C.sbuf(p + "_kfm", [128, D], BF16)
    xo = [C.sbuf(f"{p}_xo{i}", [128, D], F32) for i in range(2)]
    stats = C.sbuf(p + "_st", [128, 16], F32)
    scr = C.sbuf(p + "_scr", [128, 1], F32)

    def chunked(t, off, w):
        return t[:, 0:D].rearrange("p (c q) -> p c q", c=8)[:, :, off:off + w]

    for i in range(NT):
        b = i % 2
        if i >= 2:
            C.wait(nc.sync, p + "ep", 2 * (i - 2) + 2)
        C.inc(nc.sync.dma_start(xt[b][:, :], x_in[i * 128:(i + 1) * 128, :]),
              p + "ld", 16)
        C.wait(v, p + "ld")
        C.wait(s, p + "ld")
        C.wait(v, p + "w")
        if i >= 2:
            C.wait(v, p + "stp", i - 1)
        emit_ln_tile(C, xt[b], lnb[b], stats, scr, sem=p + "ln")
        C.wait(te, p + "ln")
        if i >= 2:
            C.wait(te, p + "kev", i - 1)
        elif first and i == 0 and C.count(p + "vcp") > 0:
            C.wait(te, p + "vcp")
        C.inc(emit_transpose_tile(C, lnb[b], psTb(C, b)), p + "stp")
        C.wait(v, p + "stp")
        if i >= 1:
            C.wait(v, p + "mm", 3 * i - 2)
        C.inc(v.tensor_copy(fm[b][:, :], psTb(C, b)[:, :]), p + "ev")
        C.wait(v, p + "ev")
        C.inc(v.tensor_copy(chunked(shf, 1, 127), chunked(fm[b], 0, 127)), p + "sh")
        if i == 0:
            C.inc(v.memset(chunked(shf, 0, 1), 0.0), p + "sh")
        else:
            C.inc(v.tensor_copy(chunked(shf, 0, 1), chunked(fm[1 - b], 127, 1)),
                  p + "sh")
        C.wait(v, p + "sh")
        C.inc(v.tensor_tensor(dfm[:, :], fm[b][:, :], shf[:, :], ALU.subtract), p + "mx")
        C.wait(v, p + "mx")
        if i >= 1:
            C.wait(v, p + "mm", 3 * (i - 1) + 2)
        for j in range(8):
            sl = slice(j * 128, (j + 1) * 128)
            C.inc(v.scalar_tensor_tensor(xr[:, sl], dfm[:, sl], mu[:, j:j + 1],
                                         shf[:, sl], ALU.mult, ALU.add), p + "mxr")
        for j in range(8):
            sl = slice(j * 128, (j + 1) * 128)
            C.inc(v.scalar_tensor_tensor(xk[:, sl], dfm[:, sl], mu[:, 8 + j:9 + j],
                                         shf[:, sl], ALU.mult, ALU.add), p + "mxk")
        C.wait(te, p + "mxr")
        C.wait(te, p + "w")
        if i >= 1:
            C.wait(te, p + "sg", i)
            C.wait(te, p + "kv", 2 * i - 1)
        mm = None
        for half in range(2):
            for j in range(8):
                mm = te.matmul(C.psF[0][:, half * 512:(half + 1) * 512],
                               xr[:, j * 128:(j + 1) * 128],
                               wrT[:, j * 1024 + half * 512:j * 1024 + (half + 1) * 512],
                               start=(j == 0), stop=(j == 7))
        C.inc(mm, p + "mm")
        C.wait(te, p + "mxk")
        for half in range(2):
            for j in range(8):
                mm = te.matmul(C.psF[1][:, half * 512:(half + 1) * 512],
                               xk[:, j * 128:(j + 1) * 128],
                               wkT[:, j * 1024 + half * 512:j * 1024 + (half + 1) * 512],
                               start=(j == 0), stop=(j == 7))
        C.inc(mm, p + "mm")
        C.wait(s, p + "mm", C.count(p + "mm") - 1)
        if i >= 1:
            C.wait(s, p + "ep", 2 * (i - 1) + 1)
        C.inc(s.activation(rr[:, :], C.psF[0][:, :], AF.Tanh, scale=0.5), p + "sg")
        C.wait(s, p + "mm")
        C.inc(s.activation(krel[:, :], C.psF[1][:, :], AF.Relu), p + "kv")
        if i >= 1:
            C.wait(s, p + "ktp", i)
        C.inc(s.activation(kact[:, :], krel[:, :], AF.Square), p + "kv")
        C.wait(te, p + "kv")
        C.wait(te, p + "ev", C.count(p + "ev"))
        C.inc(emit_transpose_tile(C, kact, psTb(C, b)), p + "ktp")
        C.wait(v, p + "ktp")
        if i >= 1:
            C.wait(v, p + "mm", 3 * (i - 1) + 3)
        C.inc(v.tensor_copy(kfm[:, :], psTb(C, b)[:, :]), p + "kev")
        C.wait(te, p + "kev")
        if i >= 1:
            C.wait(te, p + "ep", 2 * (i - 1) + 1)
        for half in range(2):
            for j in range(8):
                mm = te.matmul(C.psF[2][:, half * 512:(half + 1) * 512],
                               kfm[:, j * 128:(j + 1) * 128],
                               wvT[:, j * 1024 + half * 512:j * 1024 + (half + 1) * 512],
                               start=(j == 0), stop=(j == 7))
        C.inc(mm, p + "mm")
        C.wait(v, p + "mm")
        C.wait(v, p + "sg")
        if i >= 2:
            C.wait(v, p + "res", 16 * (i - 1))
        C.inc(v.tensor_scalar(rr[:, :], rr[:, :], 0.5, 0.5, ALU.mult, ALU.add), p + "sg2")
        C.wait(v, p + "sg2")
        C.inc(v.tensor_tensor(xo[b][:, :], C.psF[2][:, :], rr[:, :], ALU.mult), p + "ep")
        C.inc(v.tensor_tensor(xo[b][:, :], xo[b][:, :], xt[b][:, :], ALU.add), p + "ep")
        C.wait(nc.gpsimd, p + "ep", C.count(p + "ep"))
        C.inc(nc.gpsimd.dma_start(x_out[i * 128:(i + 1) * 128, :], xo[b][:, :]),
              p + "res", 16)
    C.wait(nc.sync, p + "res")


def psTb(C, b):
    return C.psT[b]



def emit_l0attn_a(C, x_in, x_out):
    nc = C.nc
    v, s, te = nc.vector, nc.scalar, nc.tensor
    p = "aa"

    K_fm = C.sbuf(p + "_Kfm", [128, 8 * T], BF16)
    V_tm = C.sbuf(p + "_Vtm", [128, NT * 1024], BF16)

    C.frame()
    wkT = C.sbuf(p + "_wkT", [128, 8 * 1024], BF16)
    wvT = C.sbuf(p + "_wvT", [128, 8 * 1024], BF16)
    stageA = C.sbuf(p + "_stageA", [128, 8 * 1024], BF16)
    prep_weight_T(C, p + "wk", wkT, C.in_ap["L0_attn_a_Wk"], p + "k", stage=stageA)
    prep_weight_T(C, p + "wv", wvT, C.in_ap["L0_attn_a_Wv"], p + "v",
                  prev_cp=p + "kcp", stage=stageA, prev_tp=p + "ktp")
    xt = [C.sbuf(f"{p}_xt{i}", [128, D], F32) for i in range(2)]
    lnb = [C.sbuf(f"{p}_ln{i}", [128, D], BF16) for i in range(2)]
    fm = [C.sbuf(f"{p}_fm{i}", [128, D], BF16) for i in range(2)]
    stats = C.sbuf(p + "_st", [128, 16], F32)
    scr = C.sbuf(p + "_scr", [128, 1], F32)

    for i in range(NT):
        b = i % 2
        if i >= 2:
            C.wait(nc.sync, p + "aln", 6 * (i - 1))
        C.inc(nc.sync.dma_start(xt[b][:, :], x_in[i * 128:(i + 1) * 128, :]),
              p + "ald", 16)
        C.wait(v, p + "ald")
        C.wait(s, p + "ald")
        if i >= 2:
            C.wait(v, p + "astp", i - 1)
        emit_ln_tile(C, xt[b], lnb[b], stats, scr, sem=p + "aln")
        C.wait(te, p + "aln")
        if i >= 2:
            C.wait(te, p + "aev", i - 1)
        elif i == 0:
            C.wait(te, p + "vcp")
        C.inc(emit_transpose_tile(C, lnb[b], C.psT[b]), p + "astp")
        C.wait(v, p + "astp")
        if i >= 2:
            C.wait(v, p + "avm", i - 1)
        C.inc(v.tensor_copy(fm[b][:, :], C.psT[b][:, :]), p + "aev")
        C.wait(te, p + "aev")
        C.wait(te, p + "kld")
        C.wait(te, p + "vld")
        if i >= 1:
            C.wait(te, p + "akc", i)
        mm = None
        for dblk in range(8):
            for j in range(8):
                mm = te.matmul(C.psF[0][:, dblk * 128:(dblk + 1) * 128],
                               wkT[:, j * 1024 + dblk * 128:j * 1024 + (dblk + 1) * 128],
                               fm[b][:, j * 128:(j + 1) * 128],
                               start=(j == 0), stop=(j == 7))
        C.inc(mm, p + "akm")
        C.wait(v, p + "akm")
        kdst = K_fm[:, 0:8 * T].rearrange("q (c t) -> q c t", c=8)[:, :, i * 128:(i + 1) * 128]
        ksrc = C.psF[0][:, 0:1024].rearrange("q (c t) -> q c t", c=8)
        C.inc(v.tensor_copy(kdst, ksrc), p + "akc")
        if i >= 1:
            C.wait(te, p + "avc", i)
        for half in range(2):
            for j in range(8):
                mm = te.matmul(C.psF[1][:, half * 512:(half + 1) * 512],
                               fm[b][:, j * 128:(j + 1) * 128],
                               wvT[:, j * 1024 + half * 512:j * 1024 + (half + 1) * 512],
                               start=(j == 0), stop=(j == 7))
        C.inc(mm, p + "avm")
        C.wait(v, p + "avm")
        C.inc(v.tensor_copy(V_tm[:, i * 1024:(i + 1) * 1024], C.psF[1][:, :]), p + "avc")
    nc.all_engine_barrier()
    C.end_frame()

    C.frame()
    wqT = C.sbuf(p + "_wqT", [128, 8 * 1024], BF16)
    C.tframe()
    prep_weight_T(C, p + "wq", wqT, C.in_ap["L0_attn_a_Wq"], p + "q")
    C.end_tframe()
    xt2 = [C.sbuf(f"{p}_x2{i}", [128, D], F32) for i in range(2)]
    lnb2 = [C.sbuf(f"{p}_l2{i}", [128, D], BF16) for i in range(2)]
    fm2 = [C.sbuf(f"{p}_f2{i}", [128, D], BF16) for i in range(2)]
    qfm = C.sbuf(p + "_qfm", [128, 1024], BF16)
    Ssc = C.sbuf(p + "_Ssc", [128, 4096], F32)
    Pb = C.sbuf(p + "_P", [128, 4096], BF16)
    PT = C.sbuf(p + "_PT", [128, 4096], BF16)
    mx = C.sbuf(p + "_mx", [128, 12], F32)
    lsum = C.sbuf(p + "_ls", [128, 10], F32)
    xo = [C.sbuf(f"{p}_xo{i}", [128, D], F32) for i in range(2)]
    stats2 = C.sbuf(p + "_st2", [128, 16], F32)
    scr2 = C.sbuf(p + "_sc2", [128, 1], F32)

    for i in range(NT):
        b = i % 2
        if i >= 2:
            C.wait(nc.sync, p + "bep", i - 1)
        elif i == 0:
            C.wait(nc.sync, p + "qtp")
        C.inc(nc.sync.dma_start(xt2[b][:, :], x_in[i * 128:(i + 1) * 128, :]),
              p + "bld", 16)
        C.wait(v, p + "bld")
        C.wait(s, p + "bld")
        if i >= 2:
            C.wait(v, p + "bstp", i - 1)
        emit_ln_tile(C, xt2[b], lnb2[b], stats2, scr2, sem=p + "bln")
        C.wait(te, p + "bln")
        if i >= 2:
            C.wait(te, p + "bev", i - 1)
        elif i == 0:
            C.wait(te, p + "qcp")
        if i >= 1:
            C.wait(te, p + "bptc", 4 * i)
        C.inc(emit_transpose_tile(C, lnb2[b], C.psT[b]), p + "bstp")
        C.wait(v, p + "bstp")
        if i >= 2:
            C.wait(v, p + "bqm", i - 1)
        C.inc(v.tensor_copy(fm2[b][:, :], C.psT[b][:, :]), p + "bev")
        C.wait(te, p + "bev")
        C.wait(te, p + "qld")
        if i >= 1:
            C.wait(te, p + "bqc", i)
        for dblk in range(8):
            for j in range(8):
                mm = te.matmul(C.psF[2][:, dblk * 128:(dblk + 1) * 128],
                               wqT[:, j * 1024 + dblk * 128:j * 1024 + (dblk + 1) * 128],
                               fm2[b][:, j * 128:(j + 1) * 128],
                               start=(j == 0), stop=(j == 7))
        C.inc(mm, p + "bqm")
        C.wait(v, p + "bqm")
        if i >= 1:
            C.wait(v, p + "bsm", 8 * i)
        C.inc(v.tensor_copy(qfm[:, :], C.psF[2][:, :]), p + "bqc")
        C.wait(te, p + "bqc")
        for kb in range(8):
            half = kb % 2
            C.wait(te, p + "bse", max(0, 8 * i + kb - 1))
            for j in range(8):
                mm = te.matmul(C.psF[0][:, half * 512:(half + 1) * 512],
                               qfm[:, j * 128:(j + 1) * 128],
                               K_fm[:, j * T + kb * 512:j * T + (kb + 1) * 512],
                               start=(j == 0), stop=(j == 7))
            C.inc(mm, p + "bsm")
            C.wait(s, p + "bsm")
            if i >= 1 and kb == 0:
                C.wait(s, p + "bmx", 10 * i)
                C.wait(s, p + "bpt", 4 * i)
            C.inc(s.copy(Ssc[:, kb * 512:(kb + 1) * 512],
                         C.psF[0][:, half * 512:(half + 1) * 512]), p + "bse")
            C.wait(v, p + "bse")
            C.inc(v.reduce_max(mx[:, kb:kb + 1], Ssc[:, kb * 512:(kb + 1) * 512],
                               AX.X), p + "bmx")
        C.wait(v, p + "bmx")
        C.inc(v.reduce_max(mx[:, 9:10], mx[:, 0:8], AX.X), p + "bmx")
        C.wait(v, p + "bmx")
        C.inc(v.tensor_scalar(mx[:, 10:11], mx[:, 9:10], -1.0, None, ALU.mult),
              p + "bmx")
        C.wait(s, p + "bmx")
        for kb in range(8):
            C.inc(s.activation(Pb[:, kb * 512:(kb + 1) * 512],
                               Ssc[:, kb * 512:(kb + 1) * 512], AF.Exp,
                               bias=mx[:, 10:11],
                               accum_out=lsum[:, kb:kb + 1]), p + "bex")
        C.wait(v, p + "bex")
        C.inc(v.tensor_reduce(mx[:, 11:12], lsum[:, 0:8], AX.X, ALU.add), p + "blr")
        C.wait(v, p + "blr")
        C.inc(v.tensor_scalar(lsum[:, 8:9], mx[:, 11:12], 32.0, None, ALU.mult),
              p + "blr")
        C.wait(v, p + "blr")
        C.inc(v.reciprocal(lsum[:, 9:10], lsum[:, 8:9]), p + "blr")
        C.wait(te, p + "bex")
        for n in range(4):
            bank = C.psT[n % 2]
            if n == 0:
                C.wait(te, p + "bev", C.count(p + "bev"))
                if i >= 1:
                    C.wait(te, p + "bptc", 4 * i)
            elif n >= 2:
                C.wait(te, p + "bptc", 4 * i + n - 1)
            C.inc(emit_transpose_tile(C, Pb, bank, src_off=n * 1024), p + "bpt")
            C.wait(v, p + "bpt")
            C.inc(v.tensor_copy(PT[:, n * 1024:(n + 1) * 1024], bank[:, :]), p + "bptc")
        C.wait(te, p + "bptc")
        if i >= 1:
            C.wait(te, p + "bep", i)
        for half in range(2):
            for kk in range(32):
                mm = te.matmul(C.psF[1][:, half * 512:(half + 1) * 512],
                               PT[:, kk * 128:(kk + 1) * 128],
                               V_tm[:, kk * 1024 + half * 512:kk * 1024 + (half + 1) * 512],
                               start=(kk == 0), stop=(kk == 31))
        C.inc(mm, p + "bvm")
        C.wait(v, p + "bvm")
        C.wait(v, p + "blr")
        if i >= 2:
            C.wait(v, p + "bres", 16 * (i - 1))
        C.inc(v.scalar_tensor_tensor(xo[b][:, :], C.psF[1][:, :], lsum[:, 9:10],
                                     xt2[b][:, :], ALU.mult, ALU.add), p + "bep")
        C.wait(nc.gpsimd, p + "bep", C.count(p + "bep"))
        C.inc(nc.gpsimd.dma_start(x_out[i * 128:(i + 1) * 128, :], xo[b][:, :]),
              p + "bres", 16)
    C.wait(nc.sync, p + "bres")
    C.end_frame()



def emit_rope(C, p, qk_tm, out_bf, cs, sem):
    v = C.nc.vector
    scr = C.rp_scr
    def hv(t, off, width, stride, base):
        stop = off + (width - 1) * stride + 1
        return t[:, base:base + 1024].rearrange(
            "p (h d) -> p h d", h=16)[:, :, off:stop:stride]
    x1 = hv(qk_tm, 0, 32, 2, 0)
    x2 = hv(qk_tm, 1, 32, 2, 0)
    cos = cs[:, 0:512].rearrange("p (h j) -> p h j", h=16)
    sin = cs[:, 512:1024].rearrange("p (h j) -> p h j", h=16)
    s1 = scr[:, 0:512].rearrange("p (h j) -> p h j", h=16)
    s2 = scr[:, 512:1024].rearrange("p (h j) -> p h j", h=16)
    o_lo = hv(out_bf, 0, 32, 1, 0)
    o_hi = hv(out_bf, 32, 32, 1, 0)
    v.tensor_tensor(s1, x1, cos, ALU.mult)
    v.tensor_tensor(s2, x2, sin, ALU.mult)
    C.inc(v.tensor_tensor(o_lo, s1, s2, ALU.subtract), sem)
    v.tensor_tensor(s1, x2, cos, ALU.mult)
    v.tensor_tensor(s2, x1, sin, ALU.mult)
    C.inc(v.tensor_tensor(o_hi, s1, s2, ALU.add), sem)


def emit_l1attn_a(C, x_in, x_out):
    nc = C.nc
    v, s, te = nc.vector, nc.scalar, nc.tensor
    p = "la"
    NSUP = 8
    pT = C.psT[0]
    pA = [C.psF[0], C.psF[1]]
    pB = C.psF[2]

    V_tm = C.sbuf(p + "_Vtm", [128, NT * 16 * 65], BF16)
    C.rp_scr = C.sbuf(p + "_rps", [128, 1024], F32)
    qkvb_q = C.sbuf(p + "_qkvbq", [128, D], F32)
    C.inc(nc.sync.dma_start(qkvb_q[:, :], C.in_ap["c_l1a_qkvb_rep"][:, 0:1024]),
          p + "w", 16)
    von = V_tm[:, 0:NT * 16 * 65].rearrange("p (t h e) -> p t h e", t=NT, h=16)
    C.inc(v.memset(von[:, :, :, 64:65], 1.0), p + "w", 1)

    C.frame()
    wkv = C.sbuf(p + "_wkv", [128, 8 * 2048], BF16)
    wkv_sem = load_weight_fm(C, wkv, C.in_ap["L1_attn_a_qkv_w"][:, 1024:3072], D, 2048)
    qkvb_kv = C.sbuf(p + "_qkvkv", [128, 2 * D], F32)
    C.inc(nc.sync.dma_start(qkvb_kv[:, :], C.in_ap["c_l1a_qkvb_rep"][:, 1024:3072]),
          p + "w", 16)
    xt = [C.sbuf(f"{p}_xt{i}", [128, D], F32) for i in range(2)]
    lnb = [C.sbuf(f"{p}_ln{i}", [128, D], BF16) for i in range(2)]
    fm = [C.sbuf(f"{p}_fm{i}", [128, D], BF16) for i in range(2)]
    ktm = C.sbuf(p + "_ktm", [128, D], F32)
    krb = C.sbuf(p + "_krb", [128, D], BF16)
    krs = C.sbuf(p + "_krs", [128, D], BF16)
    cs = [C.sbuf(f"{p}_cs{i}", [128, 1024], F32) for i in range(2)]
    stats = C.sbuf(p + "_st", [128, 16], F32)
    scr = C.sbuf(p + "_scr", [128, 1], F32)

    cos_rep = C.in_ap["c_rope_cos_rep"]
    sin_rep = C.in_ap["c_rope_sin_rep"]
    for i in range(NT):
        b = i % 2
        if i >= 2:
            C.wait(nc.sync, p + "aln", 6 * (i - 1))
        C.inc(nc.sync.dma_start(xt[b][:, :], x_in[i * 128:(i + 1) * 128, :]),
              p + "ald", 16)
        if i >= 2:
            C.wait(nc.sync, p + "arp", 2 * (i - 1))
        C.inc(nc.sync.dma_start(cs[b][:, 0:512], cos_rep[i * 128:(i + 1) * 128, :]),
              p + "acs", 16)
        C.inc(nc.sync.dma_start(cs[b][:, 512:1024], sin_rep[i * 128:(i + 1) * 128, :]),
              p + "acs", 16)
        C.wait(v, p + "ald")
        C.wait(s, p + "ald")
        if i >= 2:
            C.wait(v, p + "astp", i - 1)
        emit_ln_tile(C, xt[b], lnb[b], stats, scr, sem=p + "aln")
        C.wait(te, p + "aln")
        if i >= 1:
            C.wait(te, p + "akr", i)
        C.inc(emit_transpose_tile(C, lnb[b], pT), p + "astp")
        C.wait(v, p + "astp")
        if i >= 2:
            C.wait(v, p + "avm", 2 * (i - 1))
        C.inc(v.tensor_copy(fm[b][:, :], pT[:, :]), p + "aev")
        C.wait(te, p + "aev")
        C.wait(te, p + "w")
        C.wait(te, wkv_sem)
        if i >= 1:
            C.wait(te, p + "ake", i)
        mm = None
        for j in range(8):
            for half in range(2):
                mm = te.matmul(pA[0][:, half * 512:(half + 1) * 512],
                               fm[b][:, j * 128:(j + 1) * 128],
                               wkv[:, j * 2048 + half * 512:j * 2048 + (half + 1) * 512],
                               start=(j == 0), stop=(j == 7))
        C.inc(mm, p + "avm")
        if i >= 1:
            C.wait(te, p + "ave", i)
        for j in range(8):
            for half in range(2):
                mm = te.matmul(pA[1][:, half * 512:(half + 1) * 512],
                               fm[b][:, j * 128:(j + 1) * 128],
                               wkv[:, j * 2048 + 1024 + half * 512:
                                   j * 2048 + 1024 + (half + 1) * 512],
                               start=(j == 0), stop=(j == 7))
        C.inc(mm, p + "avm")
        C.wait(v, p + "avm", C.count(p + "avm") - 1)
        C.inc(v.tensor_tensor(ktm[:, :], pA[0][:, :], qkvb_kv[:, 0:1024], ALU.add),
              p + "ake")
        C.wait(v, p + "acs", 32 * (i + 1))
        C.wait(v, p + "ake")
        emit_rope(C, p, ktm, krb, cs[b], p + "arp")
        C.wait(te, p + "arp")
        C.inc(emit_transpose_tile(C, krb, pT), p + "akt")
        C.wait(v, p + "akt")
        if i >= 1:
            C.wait(v, p + "akw", 16 * i)
        C.inc(v.tensor_copy(krs[:, :], pT[:, :]), p + "akr")
        C.wait(nc.gpsimd, p + "akr", C.count(p + "akr"))
        kdst = C.krs_hbm.ap()[:, :, i * 128:(i + 1) * 128].rearrange("h p k -> p h k")
        C.inc(nc.gpsimd.dma_start(kdst, krs[:, :].rearrange("p (h k) -> p h k", h=8)),
              p + "akw", 16)
        C.wait(v, p + "avm")
        vdst = V_tm[:, i * 1040:(i + 1) * 1040].rearrange(
            "p (h e) -> p h e", h=16)[:, :, 0:64]
        vsrc = pA[1][:, :].rearrange("p (h e) -> p h e", h=16)
        bsr = qkvb_kv[:, 1024:2048].rearrange("p (h e) -> p h e", h=16)
        C.inc(v.tensor_tensor(vdst, vsrc, bsr, ALU.add), p + "ave")
    nc.all_engine_barrier()
    C.end_frame()

    C.frame()
    wq = C.sbuf(p + "_wq", [128, 8 * 1024], BF16)
    wo = C.sbuf(p + "_wo", [128, 8 * 1024], BF16)
    wq_sem = load_weight_fm(C, wq, C.in_ap["L1_attn_a_qkv_w"][:, 0:1024], D, 1024)
    wo_sem = load_weight_fm(C, wo, C.in_ap["L1_attn_a_out_w"], D, 1024)
    outb = C.sbuf(p + "_outb", [128, D], F32)
    C.inc(nc.sync.dma_start(outb[:, :], C.in_ap["c_l1a_outb_rep"][:, :]), p + "bw", 16)
    xt4 = [C.sbuf(f"{p}_x2{i}", [128, D], F32) for i in range(4)]
    lnb2 = [C.sbuf(f"{p}_l2{i}", [128, D], BF16) for i in range(2)]
    fm2 = [C.sbuf(f"{p}_f2{i}", [128, D], BF16) for i in range(2)]
    qtm = C.sbuf(p + "_qtm", [128, D], F32)
    qrb = C.sbuf(p + "_qrb", [128, D], BF16)
    Qs = C.sbuf(p + "_Qs", [128, 8 * 512], BF16)
    Kr = [C.sbuf(f"{p}_Kr{i}", [128, 4096], BF16) for i in range(2)]
    Pb = [C.sbuf(f"{p}_Pb{i}", [128, 1024], BF16) for i in range(2)]
    lrow = C.sbuf(p + "_lr", [1, 2048], F32)
    Osb = C.sbuf(p + "_Osb", [128, 512], F32)
    cs2 = [C.sbuf(f"{p}_c2{i}", [128, 1024], F32) for i in range(2)]
    xo2 = [C.sbuf(f"{p}_xo{i}", [128, D], F32) for i in range(2)]
    stats2 = C.sbuf(p + "_st2", [128, 16], F32)
    scr2 = C.sbuf(p + "_sc2", [128, 1], F32)

    for sg in range(NSUP):
        if sg > 0:
            C.wait(nc.sync, p + "bep", 4 * sg)
        for tau in range(4):
            i = sg * 4 + tau
            C.inc(nc.sync.dma_start(xt4[tau][:, :], x_in[i * 128:(i + 1) * 128, :]),
                  p + "bld", 16)
        for tau in range(4):
            i = sg * 4 + tau
            b = i % 2
            if i >= 2:
                C.wait(nc.sync, p + "brp", 2 * (i - 1))
            C.inc(nc.sync.dma_start(cs2[b][:, 0:512],
                                    cos_rep[i * 128:(i + 1) * 128, :]), p + "bcs", 16)
            C.inc(nc.sync.dma_start(cs2[b][:, 512:1024],
                                    sin_rep[i * 128:(i + 1) * 128, :]), p + "bcs", 16)
            C.wait(v, p + "bld", 16 * (4 * sg + tau + 1))
            C.wait(s, p + "bld", 16 * (4 * sg + tau + 1))
            if i >= 2:
                C.wait(v, p + "bstp", i - 1)
            emit_ln_tile(C, xt4[tau], lnb2[b], stats2, scr2, sem=p + "bln")
            C.wait(te, p + "bln")
            if i >= 1:
                C.wait(te, p + "bqc", i)
            C.inc(emit_transpose_tile(C, lnb2[b], pT), p + "bstp")
            C.wait(v, p + "bstp")
            if i >= 2:
                C.wait(v, p + "bqm", i - 1)
            C.inc(v.tensor_copy(fm2[b][:, :], pT[:, :]), p + "bev")
            C.wait(te, p + "bev")
            C.wait(te, p + "bw")
            C.wait(te, wq_sem)
            C.wait(te, wo_sem)
            if sg > 0 and tau == 0:
                C.wait(te, p + "laos", 16 * sg)
                C.wait(te, p + "laex", 256 * sg)
                C.wait(te, p + "bep0", 4 * sg)
            if i >= 1:
                C.wait(te, p + "bqe", i)
            for j in range(8):
                for half in range(2):
                    mm = te.matmul(pA[0][:, half * 512:(half + 1) * 512],
                                   fm2[b][:, j * 128:(j + 1) * 128],
                                   wq[:, j * 1024 + half * 512:j * 1024 + (half + 1) * 512],
                                   start=(j == 0), stop=(j == 7))
            C.inc(mm, p + "bqm")
            C.wait(v, p + "bqm")
            C.inc(v.tensor_tensor(qtm[:, :], pA[0][:, :], qkvb_q[:, :], ALU.add),
                  p + "bqe")
            C.wait(v, p + "bcs", 32 * (4 * sg + tau + 1))
            C.wait(v, p + "bqe")
            emit_rope(C, p, qtm, qrb, cs2[b], p + "brp")
            C.wait(te, p + "brp")
            C.wait(te, p + "bev", C.count(p + "bev"))
            C.inc(emit_transpose_tile(C, qrb, pT), p + "bqt")
            C.wait(v, p + "bqt")
            if sg > 0 and tau == 0:
                C.wait(v, p + "bom", 4 * sg)
            qdst = Qs[:, 0:8 * 512].rearrange("p (h q) -> p h q", h=8)[
                :, :, tau * 128:(tau + 1) * 128]
            C.inc(v.tensor_copy(qdst, pT[:, :].rearrange("p (h q) -> p h q", h=8)),
                  p + "bqc")
        if sg == 0:
            C.inc(nc.sync.dma_start(Kr[0][:, :], C.krs_hbm.ap()[0, :, :]),
                  p + "lakl", 16)
        for hp in range(8):
            ghp = sg * 8 + hp
            kb = ghp % 2
            if ghp + 1 < 8 * NSUP:
                C.wait(nc.sync, p + "lasm", max(0, 32 * ghp))
                C.inc(nc.sync.dma_start(Kr[1 - kb][:, :],
                                        C.krs_hbm.ap()[(hp + 1) % 8, :, :]),
                      p + "lakl", 16)
            C.wait(te, p + "lakl", 16 * (ghp + 1))
            C.wait(te, p + "bqc", C.count(p + "bqc"))
            for k in range(33):
                if k < 32:
                    bank = pA[k % 2]
                    C.wait(te, p + "laex", max(0, 32 * ghp + k - 1))
                    m1 = te.matmul(bank[:, 0:512], Kr[kb][0:64, k * 128:(k + 1) * 128],
                                   Qs[0:64, hp * 512:(hp + 1) * 512],
                                   start=True, stop=True, tile_position=(0, 0))
                    m2 = te.matmul(bank[:, 512:1024], Kr[kb][64:128, k * 128:(k + 1) * 128],
                                   Qs[64:128, hp * 512:(hp + 1) * 512],
                                   start=True, stop=True, tile_position=(64, 0))
                    C.inc(m2, p + "lasm")
                    C.wait(s, p + "lasm")
                    if 32 * ghp + k >= 2:
                        C.wait(s, p + "lapv", 32 * ghp + k - 1)
                    C.inc(s.activation(Pb[k % 2][:, :], bank[:, :], AF.Exp,
                                       bias=C.negc[:, 0:1], scale=0.125), p + "laex")
                if k > 0:
                    kk = k - 1
                    pb = Pb[kk % 2]
                    C.wait(te, p + "laex", 32 * ghp + kk + 1)
                    ve = V_tm[:, kk * 1040 + (2 * hp) * 65:kk * 1040 + (2 * hp) * 65 + 65]
                    vo = V_tm[:, kk * 1040 + (2 * hp + 1) * 65:
                              kk * 1040 + (2 * hp + 1) * 65 + 65]
                    if kk == 0 and ghp > 0:
                        C.wait(te, p + "laos", 2 * ghp)
                        C.wait(te, p + "lalb", ghp)
                    te.matmul(pB[0:65, 0:512], ve, pb[:, 0:512],
                              start=(kk == 0), stop=(kk == 31))
                    m4 = te.matmul(pB[0:65, 512:1024], vo, pb[:, 512:1024],
                                   start=(kk == 0), stop=(kk == 31))
                    C.inc(m4, p + "lapv")
            C.wait(v, p + "lapv")
            C.inc(v.tensor_copy(lrow[0:1, 0:512], pB[64:65, 0:512]), p + "lalr")
            C.inc(v.tensor_copy(lrow[0:1, 512:1024], pB[64:65, 512:1024]), p + "lalr")
            C.wait(v, p + "lalr")
            C.inc(v.reciprocal(lrow[0:1, 1024:2048], lrow[0:1, 0:1024]), p + "lalr")
            C.wait(v, p + "lalr")
            if ghp > 0:
                C.wait(v, p + "laos", 2 * ghp)
            C.inc(v.tensor_copy(Osb[0:64, 0:512], pB[0:64, 0:512]), p + "laoc")
            C.inc(v.tensor_copy(Osb[64:128, 0:512], pB[0:64, 512:1024]), p + "laoc")
            C.wait(te, p + "laoc")
            m5 = te.matmul(pB[0:64, 0:512], C.ones_f32[0:1, 0:64],
                           lrow[0:1, 1024:1536], start=True, stop=True)
            m6 = te.matmul(pB[0:64, 512:1024], C.ones_f32[0:1, 0:64],
                           lrow[0:1, 1536:2048], start=True, stop=True)
            C.inc(m6, p + "lalb")
            C.wait(v, p + "lalb")
            C.inc(v.tensor_tensor(Qs[0:64, hp * 512:(hp + 1) * 512], Osb[0:64, 0:512],
                                  pB[0:64, 0:512], ALU.mult), p + "laos")
            C.inc(v.tensor_tensor(Qs[64:128, hp * 512:(hp + 1) * 512], Osb[64:128, 0:512],
                                  pB[0:64, 512:1024], ALU.mult), p + "laos")
        C.wait(te, p + "laos")
        for tau in range(4):
            i = sg * 4 + tau
            bank = pA[tau % 2]
            if tau >= 2:
                C.wait(te, p + "bep0", 4 * sg + tau - 1)
            for j in range(8):
                for half in range(2):
                    mm = te.matmul(bank[:, half * 512:(half + 1) * 512],
                                   Qs[:, j * 512 + tau * 128:j * 512 + (tau + 1) * 128],
                                   wo[:, j * 1024 + half * 512:j * 1024 + (half + 1) * 512],
                                   start=(j == 0), stop=(j == 7))
            C.inc(mm, p + "bom")
            C.wait(v, p + "bom")
            if tau >= 2 or sg > 0:
                C.wait(v, p + "bres", 16 * (4 * sg + tau - 1))
            C.inc(v.tensor_tensor(xo2[tau % 2][:, :], bank[:, :], outb[:, :],
                                  ALU.add), p + "bep0")
            C.wait(v, p + "bep0")
            C.inc(v.tensor_tensor(xo2[tau % 2][:, :], xo2[tau % 2][:, :],
                                  xt4[tau][:, :], ALU.add), p + "bep")
            C.wait(nc.gpsimd, p + "bep", C.count(p + "bep"))
            C.inc(nc.gpsimd.dma_start(x_out[i * 128:(i + 1) * 128, :],
                                      xo2[tau % 2][:, :]), p + "bres", 16)
    C.wait(nc.sync, p + "bres")
    C.end_frame()



def emit_l1attn_b(C, x_in, x_out):
    nc = C.nc
    v, s, te, gp = nc.vector, nc.scalar, nc.tensor, nc.gpsimd
    p = "lb"

    wqT = C.sbuf(p + "_wqT", [128, 8 * 1024], BF16)
    wkT = C.sbuf(p + "_wkT", [128, 8 * 1024], BF16)
    wvT = C.sbuf(p + "_wvT", [128, 8 * 1024], BF16)
    stage = C.sbuf(p + "_stage", [128, 8 * 1024], BF16)
    W = C.in_ap["L1_attn_b_W_qkv"]
    prep_weight_T(C, p + "wq", wqT, W[0:1024, :], p + "q", stage=stage)
    prep_weight_T(C, p + "wk", wkT, W[1024:2048, :], p + "k", stage=stage,
                  prev_tp=p + "qtp", prev_cp=p + "qcp")
    prep_weight_T(C, p + "wv", wvT, W[2048:3072, :], p + "v", stage=stage,
                  prev_tp=p + "ktp", prev_cp=p + "kcp")

    bq = C.sbuf(p + "_bq", [128, 3 * D], F32)
    qdec = C.sbuf(p + "_qd", [128, 1], F32)
    kdec = C.sbuf(p + "_kd", [128, 1], F32)
    mloc = C.sbuf(p + "_ml", [128, 128], F32)
    bqs = C.sbuf(p + "_bqs", [128, D], F32)
    bks = C.sbuf(p + "_bks", [128, D], F32)
    C.inc(nc.sync.dma_start(bq[:, :], C.in_ap["c_l1b_bqkv_rep"][:, :]), p + "w", 16)
    C.inc(nc.sync.dma_start(qdec[:, :], C.in_ap["c_qdec"][:, :]), p + "w", 16)
    C.inc(nc.sync.dma_start(kdec[:, :], C.in_ap["c_kdec"][:, :]), p + "w", 16)
    C.inc(nc.sync.dma_start(mloc[:, :], C.in_ap["c_mloc"][:, :]), p + "w", 16)
    C.wait(v, p + "w")
    C.inc(v.tensor_scalar(bqs[:, :], bq[:, 0:1024], qdec[:, 0:1], None, ALU.mult),
          p + "w", 1)
    C.inc(v.tensor_scalar(bks[:, :], bq[:, 1024:2048], kdec[:, 0:1], None, ALU.mult),
          p + "w", 1)

    Sst = C.sbuf(p + "_Sst", [128, 8 * 1024], F32)
    Ssh = C.sbuf(p + "_Ssh", [128, 8 * 1024], BF16)
    C.inc(v.memset(Sst[:, :], 0.0), p + "w", 1)
    C.inc(gp.memset(Ssh[:, :], 0.0), p + "w", 1)

    xt = [C.sbuf(f"{p}_xt{i}", [128, D], F32) for i in range(3)]
    lnb = [C.sbuf(f"{p}_ln{i}", [128, D], BF16) for i in range(2)]
    fm = [C.sbuf(f"{p}_fm{i}", [128, D], BF16) for i in range(2)]
    qpt = C.sbuf(p + "_qpt", [128, D], BF16)
    qst = C.sbuf(p + "_qst", [128, D], BF16)
    kpt = C.sbuf(p + "_kpt", [128, D], BF16)
    kst = C.sbuf(p + "_kst", [128, D], BF16)
    vt = C.sbuf(p + "_vt", [128, D], BF16)
    qfm = C.sbuf(p + "_qfm", [128, D], BF16)
    qsfm = C.sbuf(p + "_qsfm", [128, D], BF16)
    kfm = C.sbuf(p + "_kfm", [128, D], BF16)
    Pb = C.sbuf(p + "_P", [128, 128], BF16)
    PT = C.sbuf(p + "_PT", [128, 128], BF16)
    xo = [C.sbuf(f"{p}_xo{i}", [128, D], F32) for i in range(2)]
    stats = C.sbuf(p + "_st", [128, 16], F32)
    scr = C.sbuf(p + "_scr", [128, 1], F32)

    def ldx7(j):
        if j >= 3:
            C.wait(nc.sync, p + "ep", (j - 3) + 1)
        C.inc(nc.sync.dma_start(xt[j % 3][:, :], x_in[j * 128:(j + 1) * 128, :]),
              p + "ld", 16)

    ldx7(0)
    ldx7(1)
    for c in range(NT):
        b = c % 2
        pa, pb_ = C.psT[0], C.psT[1]
        if c + 2 < NT:
            ldx7(c + 2)
        C.wait(v, p + "ld", 16 * (c + 1))
        C.wait(s, p + "ld", 16 * (c + 1))
        C.wait(v, p + "w")
        if c >= 2:
            C.wait(v, p + "stp", c - 1)
        emit_ln_tile(C, xt[c % 3], lnb[b], stats, scr, sem=p + "ln")
        C.wait(te, p + "ln")
        if c >= 1:
            C.wait(te, p + "ev", c)
            C.wait(te, p + "qsc", c)
        elif c == 0:
            C.wait(te, p + "vcp")
        C.inc(emit_transpose_tile(C, lnb[b], pa), p + "stp")
        C.wait(v, p + "stp")
        if c >= 2:
            C.wait(v, p + "vmm", c - 1)
        C.inc(v.tensor_copy(fm[b][:, :], pa[:, :]), p + "ev")
        C.wait(te, p + "ev")
        if c >= 1:
            C.wait(te, p + "qe", 2 * c)
            C.wait(te, p + "pmk", c)
        mm = None
        for half in range(2):
            for j in range(8):
                mm = te.matmul(C.psF[0][:, half * 512:(half + 1) * 512],
                               fm[b][:, j * 128:(j + 1) * 128],
                               wqT[:, j * 1024 + half * 512:j * 1024 + (half + 1) * 512],
                               start=(j == 0), stop=(j == 7))
        C.inc(mm, p + "qmm")
        C.wait(v, p + "qmm")
        if c >= 1:
            C.wait(v, p + "qtp", 2 * c)
        C.inc(v.tensor_tensor(qpt[:, :], C.psF[0][:, :], bq[:, 0:1024], ALU.add),
              p + "qe")
        C.inc(v.scalar_tensor_tensor(qst[:, :], C.psF[0][:, :], qdec[:, 0:1],
                                     bqs[:, :], ALU.mult, ALU.add), p + "qe")
        if c >= 1:
            C.wait(te, p + "sup", 8 * c)
        for half in range(2):
            for j in range(8):
                mm = te.matmul(C.psF[2][:, half * 512:(half + 1) * 512],
                               fm[b][:, j * 128:(j + 1) * 128],
                               wkT[:, j * 1024 + half * 512:j * 1024 + (half + 1) * 512],
                               start=(j == 0), stop=(j == 7))
        C.inc(mm, p + "kmm")
        C.wait(v, p + "kmm")
        if c >= 1:
            C.wait(v, p + "ktp", c)
            C.wait(v, p + "dsm", 16 * c)
        C.inc(v.tensor_tensor(kpt[:, :], C.psF[2][:, :], bq[:, 1024:2048], ALU.add),
              p + "ke")
        C.inc(v.scalar_tensor_tensor(kst[:, :], C.psF[2][:, :], kdec[:, 0:1],
                                     bks[:, :], ALU.mult, ALU.add), p + "ke")
        if c >= 1:
            C.wait(te, p + "ep", c)
        for half in range(2):
            for j in range(8):
                mm = te.matmul(C.psF[1][:, half * 512:(half + 1) * 512],
                               fm[b][:, j * 128:(j + 1) * 128],
                               wvT[:, j * 1024 + half * 512:j * 1024 + (half + 1) * 512],
                               start=(j == 0), stop=(j == 7))
        C.inc(mm, p + "vmm")
        C.wait(v, p + "vmm")
        if c >= 1:
            C.wait(v, p + "imm", c)
            C.wait(v, p + "dsm", 16 * c)
        C.inc(v.tensor_tensor(vt[:, :], C.psF[1][:, :], bq[:, 2048:3072], ALU.add),
              p + "ve")
        C.wait(te, p + "qe")
        if c >= 1:
            C.wait(te, p + "ptc", c)
        C.inc(emit_transpose_tile(C, qpt, pb_), p + "qtp")
        C.wait(v, p + "qtp")
        if c >= 1:
            C.wait(v, p + "smm", c)
            C.wait(v, p + "imm", c)
        C.inc(v.tensor_copy(qfm[:, :], pb_[:, :]), p + "qpc")
        C.inc(emit_transpose_tile(C, qst, pa), p + "qtp")
        C.wait(v, p + "qtp")
        C.inc(v.tensor_copy(qsfm[:, :], pa[:, :]), p + "qsc")
        C.wait(te, p + "qpc")
        C.wait(te, p + "ve")
        C.inc(emit_transpose_tile(C, kpt, pb_), p + "ktp")
        C.wait(v, p + "ktp")
        C.inc(v.tensor_copy(kfm[:, :], pb_[:, :]), p + "kfc")
        C.wait(te, p + "kfc")
        C.wait(te, p + "qe", C.count(p + "qe"))
        for j in range(8):
            mm = te.matmul(C.psF[0][:, 0:128], qfm[:, j * 128:(j + 1) * 128],
                           kfm[:, j * 128:(j + 1) * 128],
                           start=(j == 0), stop=(j == 7))
        C.inc(mm, p + "smm")
        C.wait(v, p + "smm")
        if c >= 1:
            C.wait(v, p + "ptp", c)
        C.inc(v.tensor_tensor(Pb[:, :], C.psF[0][:, 0:128], mloc[:, :], ALU.mult),
              p + "pmk")
        C.wait(te, p + "pmk")
        C.inc(te.transpose(pb_[:, 0:128], Pb[:, :], C.ident_bf[:, :]), p + "ptp")
        C.wait(v, p + "ptp")
        if c >= 1:
            C.wait(v, p + "imm", c)
        C.inc(v.tensor_copy(PT[:, :], pb_[:, 0:128]), p + "ptc")
        C.wait(te, p + "ptc")
        C.wait(te, p + "ve", C.count(p + "ve"))
        if c >= 1:
            C.wait(te, p + "shc", 8 * c)
        for half in range(2):
            for j in range(8):
                te.matmul(C.psF[1][:, half * 512:(half + 1) * 512],
                          qsfm[:, j * 128:(j + 1) * 128],
                          Ssh[:, j * 1024 + half * 512:j * 1024 + (half + 1) * 512],
                          start=(j == 0), stop=False)
            mm = te.matmul(C.psF[1][:, half * 512:(half + 1) * 512],
                           PT[:, 0:128],
                           vt[:, half * 512:(half + 1) * 512],
                           start=False, stop=True)
        C.inc(mm, p + "imm")
        for j in range(8):
            if j == 0:
                C.wait(te, p + "ke", C.count(p + "ke"))
            if c >= 1 or j >= 1:
                C.wait(te, p + "sup", 8 * c + j)
            m1 = te.matmul(C.psF[2][:, 0:512], kst[:, j * 128:(j + 1) * 128],
                           vt[:, 0:512], start=True, stop=True)
            m2 = te.matmul(C.psF[2][:, 512:1024], kst[:, j * 128:(j + 1) * 128],
                           vt[:, 512:1024], start=True, stop=True)
            C.inc(m2, p + "dsm", 2)
            C.wait(v, p + "dsm")
            if c >= 1:
                C.wait(v, p + "shc", 8 * c)
            C.inc(v.scalar_tensor_tensor(Sst[:, j * 1024:(j + 1) * 1024],
                                         Sst[:, j * 1024:(j + 1) * 1024],
                                         0.99 ** 128, C.psF[2][:, :],
                                         ALU.mult, ALU.add), p + "sup")
            ecp = gp if j % 2 == 0 else v
            C.wait(ecp, p + "sup")
            C.inc(ecp.tensor_copy(Ssh[:, j * 1024:(j + 1) * 1024],
                                  Sst[:, j * 1024:(j + 1) * 1024]), p + "shc")
        C.wait(v, p + "imm")
        if c >= 2:
            C.wait(v, p + "res", 16 * (c - 1))
        C.inc(v.tensor_tensor(xo[b][:, :], C.psF[1][:, :], xt[c % 3][:, :],
                              ALU.add), p + "ep")
        C.wait(nc.gpsimd, p + "ep", C.count(p + "ep"))
        C.inc(nc.gpsimd.dma_start(x_out[c * 128:(c + 1) * 128, :], xo[b][:, :]),
              p + "res", 16)
    C.wait(nc.sync, p + "res")



def emit_l0attn_b(C, x_in, x_out, parts=("A", "B"), blevel=4):
    nc = C.nc
    v, s, te, gp = nc.vector, nc.scalar, nc.tensor, nc.gpsimd
    p = "sb"

    A_sb = C.sbuf(p + "_A", [32, T], F32)

    if "A" not in parts:
        C.frame()
        C.end_frame()
    C.frame()
    pT = C.psum(p + "a_pT", [128, 1024], BF16)
    pA = [C.psum(f"{p}a_pA{i}", [128, 1024], F32) for i in range(2)]
    pB = C.psum(p + "a_pB", [128, 1024], F32)

    W_in = C.sbuf(p + "_Win", [128, 8 * 6176], BF16)
    win_sem = load_weight_fm(C, W_in, C.in_ap["L0_attn_b_W_in"], D, 6176)
    winA = C.sbuf(p + "_WinA", [128, 8 * 32], F32)
    apA = C.in_ap["L0_attn_b_W_in"][:, 6144:6176].rearrange("(c p) o -> p c o", p=128)
    C.inc(nc.sync.dma_start(winA[:, :].rearrange("p (c o) -> p c o", c=8), apA),
          p + "aw", 16)
    diag = C.sbuf(p + "_diag", [128, 160 * 128], BF16)
    C.inc(nc.sync.dma_start(
        diag[:, :].rearrange("p (d j) -> p d j", d=160),
        C.in_ap["c_conv_diag"].rearrange("d p j -> p d j")), p + "aw", 16)
    convb = C.sbuf(p + "_convb", [128, 5120], BF16)
    C.inc(nc.sync.dma_start(convb[:, :], C.in_ap["c_convb_fm_rep"][:, :]), p + "aw", 16)

    xt = C.sbuf(p + "_xt", [128, D], F32)
    lnb = [C.sbuf(f"{p}_ln{i}", [128, D], BF16) for i in range(2)]
    lnf = C.sbuf(p + "_lnf", [128, D], F32)
    fm = [C.sbuf(f"{p}_fm{i}", [128, D], BF16) for i in range(2)]
    fm32 = C.sbuf(p + "_fm32", [128, D], F32)
    xpad = C.sbuf(p + "_xpad", [128, 40 * 131], BF16)
    halo = C.sbuf(p + "_halo", [128, 120], BF16)
    cvo = [C.sbuf(f"{p}_cvo{i}", [128, 1024], BF16) for i in range(2)]
    zsb = C.sbuf(p + "_zsb", [128, 1024], F32)
    stats = C.sbuf(p + "_st", [128, 16], F32)
    scr = C.sbuf(p + "_scr", [128, 1], F32)

    xpadv = xpad[:, 0:40 * 131].rearrange("p (cb t) -> p cb t", cb=40)

    C.inc(v.memset(xpadv[:, :, 0:3], 0.0), p + "ahl")

    for c in range(NT if "A" in parts else 0):
        b = c % 2
        if c >= 1:
            C.wait(nc.sync, p + "aln", 7 * c)
        C.inc(nc.sync.dma_start(xt[:, :], x_in[c * 128:(c + 1) * 128, :]),
              p + "ald", 16)
        C.wait(v, p + "ald")
        C.wait(s, p + "ald")
        if c >= 2:
            C.wait(v, p + "astp", c - 1)
        if c >= 1:
            C.wait(v, p + "aftp", c)
        emit_ln_tile(C, xt, lnb[b], stats, scr, want_f32=lnf, sem=p + "aln")
        C.wait(te, p + "aln")
        if c >= 1:
            C.wait(te, p + "aev", c)
        C.inc(emit_transpose_tile(C, lnb[b], pT), p + "astp")
        C.wait(v, p + "astp")
        if c >= 2:
            C.wait(v, p + "azm", c - 1)
        C.inc(v.tensor_copy(fm[b][:, :], pT[:, :]), p + "aev")
        if c >= 1:
            C.wait(te, p + "aze", c)
        tr = None
        for j in range(8):
            tr = te.transpose(pB[:, j * 128:(j + 1) * 128],
                              lnf[:, j * 128:(j + 1) * 128], C.ident_f32[:, :])
        C.inc(tr, p + "aftp")
        C.wait(v, p + "aftp")
        if c >= 1:
            C.wait(v, p + "aam", c)
        C.inc(v.tensor_copy(fm32[:, :], pB[:, :]), p + "af32e")
        C.wait(te, p + "af32e")
        C.wait(te, p + "aw")
        mm = None
        for j in range(8):
            mm = te.matmul(pB[0:32, 0:128], winA[:, j * 32:(j + 1) * 32],
                           fm32[:, j * 128:(j + 1) * 128],
                           start=(j == 0), stop=(j == 7))
        C.inc(mm, p + "aam")
        C.wait(v, p + "aam")
        C.inc(v.tensor_copy(A_sb[:, c * 128:(c + 1) * 128], pB[0:32, 0:128]),
              p + "aae")
        C.wait(te, p + "aev", C.count(p + "aev"))
        C.wait(te, win_sem)
        for g in range(5):
            if c >= 1 and g == 0:
                C.wait(te, p + "acv", 5 * c)
            if c >= 1 or g >= 2:
                C.wait(te, p + "axe", 5 * c + g - 1)
            for cbl in range(8):
                cb = g * 8 + cbl
                for j in range(8):
                    mm = te.matmul(pA[g % 2][:, cbl * 128:(cbl + 1) * 128],
                                   W_in[:, j * 6176 + cb * 128:j * 6176 + (cb + 1) * 128],
                                   fm[b][:, j * 128:(j + 1) * 128],
                                   start=(j == 0), stop=(j == 7))
            C.inc(mm, p + "apm")
            C.wait(v, p + "apm")
            if c >= 1 and g == 0:
                C.wait(v, p + "acm", 5 * c)
                C.wait(v, p + "ahl", c + 1)
            C.inc(v.tensor_copy(xpadv[:, g * 8:(g + 1) * 8, 3:131],
                                pA[g % 2][:, :].rearrange("p (cb t) -> p cb t", cb=8)),
                  p + "axe")
            C.wait(v, p + "axe")
            if c >= 1:
                C.inc(v.tensor_copy(xpadv[:, g * 8:(g + 1) * 8, 0:3],
                                    halo[:, 0:120].rearrange("p (cb e) -> p cb e", cb=40)
                                    [:, g * 8:(g + 1) * 8, :]), p + "axp")
            else:
                C.inc(v.engine_nop(), p + "axp")
        for g in range(5):
            C.wait(te, p + "axp", 5 * c + g + 1)
            C.wait(te, p + "axe", 5 * (c + 1))
            C.wait(te, p + "acv", max(0, 5 * c + g - 1))
            for cbl in range(8):
                cb = g * 8 + cbl
                for e in range(4):
                    mm = te.matmul(pA[g % 2][:, cbl * 128:(cbl + 1) * 128],
                                   diag[:, (cb * 4 + e) * 128:(cb * 4 + e + 1) * 128],
                                   xpadv[:, cb, e:e + 128],
                                   start=(e == 0), stop=(e == 3))
            C.inc(mm, p + "acm")
            C.wait(v, p + "acm")
            if 5 * c + g >= 2:
                C.wait(v, p + "acw", 16 * (5 * c + g - 1))
            C.inc(v.tensor_tensor(cvo[g % 2][:, :], pA[g % 2][:, :],
                                  convb[:, g * 1024:(g + 1) * 1024], ALU.add),
                  p + "acv")
            C.wait(nc.gpsimd, p + "acv", C.count(p + "acv"))
            dst = C.xbc_hbm.ap()[:, g * 8:(g + 1) * 8, c * 128:(c + 1) * 128]
            C.inc(nc.gpsimd.dma_start(
                dst, cvo[g % 2][:, :].rearrange("p (cb t) -> p cb t", cb=8)),
                p + "acw", 16)
        C.wait(v, p + "axe")
        C.inc(v.tensor_copy(halo[:, 0:120].rearrange("p (cb e) -> p cb e", cb=40),
                            xpadv[:, :, 128:131]), p + "ahl")
        C.wait(te, p + "aae")
        for half in range(2):
            for j in range(8):
                mm = te.matmul(pB[:, half * 512:(half + 1) * 512],
                               fm[b][:, j * 128:(j + 1) * 128],
                               W_in[:, j * 6176 + 5120 + half * 512:
                                    j * 6176 + 5120 + (half + 1) * 512],
                               start=(j == 0), stop=(j == 7))
        C.inc(mm, p + "azm")
        C.wait(v, p + "azm")
        if c >= 1:
            C.wait(v, p + "azw", 16 * c)
        C.inc(v.tensor_copy(zsb[:, :], pB[:, :]), p + "aze")
        C.wait(nc.gpsimd, p + "aze", C.count(p + "aze"))
        C.inc(nc.gpsimd.dma_start(C.z_hbm.ap()[c * 128:(c + 1) * 128, :], zsb[:, :]),
              p + "azw", 16)
    C.wait(nc.sync, p + "acw")
    C.wait(nc.sync, p + "azw")
    nc.all_engine_barrier()
    C.end_frame()

    if getattr(C, "dbg_ssd", None) is not None:
        C.frame()
        dsem = C.sem("dbgssd")
        C.inc(nc.sync.dma_start(C.dbg_ssd["xbc"],
                                C.xbc_hbm.ap().rearrange("P cb t -> P (cb t)")),
              "dbgssd", 16)
        C.inc(nc.sync.dma_start(C.dbg_ssd["z"], C.z_hbm.ap()[:, :]), "dbgssd", 16)
        C.wait(nc.sync, "dbgssd")
        C.end_frame()

    C.frame()
    pT = C.psum(p + "b_pT", [128, 1024], BF16)
    pTC = C.psum(p + "b_pTC", [128, 512], F32)
    pY = C.psum(p + "b_pY", [128, 1024], F32)
    pS = C.psum(p + "b_pS", [64, 1024], F32)
    pE = [C.psum(f"{p}b_pE{i}", [128, 512], F32) for i in range(2)]

    W_out = C.sbuf(p + "_Wout", [128, 8 * 1024], BF16)
    wout_sem = load_weight_fm(C, W_out, C.in_ap["L0_attn_b_W_out"], D, 1024)
    dpr = C.sbuf(p + "_dpr", [128, 1024], F32)
    dp_bf = C.sbuf(p + "_dpb", [128, 1024], BF16)
    big = C.sbuf(p + "_big", [128, 128], BF16)
    indic = C.sbuf(p + "_ind", [1, 256], BF16)
    indicf32 = C.sbuf(p + "_indf32", [32, 4096], F32)
    identx4 = C.sbuf(p + "_idx4", [128, 512], BF16)
    sel = C.sbuf(p + "_sel", [32, 1024], F32)
    C.inc(nc.sync.dma_start(dpr[:, :], C.in_ap["c_dp_rep"][:, :]), p + "bw", 16)
    C.inc(nc.sync.dma_start(big[:, :], C.in_ap["c_biglowT"][:, :]), p + "bw", 16)
    C.inc(nc.sync.dma_start(indic[:, :], C.in_ap["c_indic"][:, :]), p + "bw", 16)
    C.inc(nc.sync.dma_start(indicf32[:, :], C.in_ap["c_indicf32"][:, :]), p + "bw", 16)
    C.inc(nc.sync.dma_start(identx4[:, :], C.in_ap["c_identx4"][:, :]), p + "bw", 16)
    C.inc(nc.sync.dma_start(sel[:, :], C.in_ap["c_selector"][:, :]), p + "bw", 16)
    C.wait(v, p + "bw")
    C.inc(v.tensor_copy(dp_bf[:, :], dpr[:, :]), p + "bw", 1)

    xbc = C.sbuf(p + "_xbc", [128, 5120], BF16)
    x_tm = C.sbuf(p + "_xtm", [128, 1024], BF16)
    B_tm = C.sbuf(p + "_Btm", [128, 2048], BF16)
    Bw_tm = C.sbuf(p + "_Bw", [128, 2048], BF16)
    dts = C.sbuf(p + "_dts", [32, 4 * 128], F32)
    dt_fm = C.sbuf(p + "_dt", [32, 128], F32)
    cum_fm = C.sbuf(p + "_cum", [32, 128], F32)
    zeros32 = C.sbuf(p + "_z32", [32, 128], F32)
    ncum_fm = C.sbuf(p + "_ncfm", [32, 128], F32)
    erow_fm = C.sbuf(p + "_erfm", [32, 128], BF16)
    ncum_row = C.sbuf(p + "_ncrow", [1, 4096], F32)
    erow_bf = C.sbuf(p + "_erow", [1, 4096], BF16)
    totcol = C.sbuf(p + "_totc", [32, 1], F32)
    totb = C.sbuf(p + "_totb", [32, 64], F32)
    ones3264 = C.sbuf(p + "_o64", [32, 64], F32)
    Em_sb = [C.sbuf(f"{p}_Em{i}", [128, 512], F32) for i in range(2)]
    P4 = [C.sbuf(f"{p}_P4{i}", [128, 512], BF16) for i in range(2)]
    Csc_all = C.sbuf(p + "_Csc", [64, 8 * 512], BF16)
    Sst = C.sbuf(p + "_Sst", [64, 1024], F32)
    Ssh = C.sbuf(p + "_Ssh", [64, 1024], BF16)
    t1s = C.sbuf(p + "_t1", [64, 512], F32)
    xdp = C.sbuf(p + "_xdp", [128, 1024], F32)
    ysum = C.sbuf(p + "_ys", [128, 1024], F32)
    gsb = C.sbuf(p + "_g", [128, 1024], F32)
    zb = C.sbuf(p + "_zbl", [128, 1024], F32)
    ygated = C.sbuf(p + "_yg", [128, 1024], BF16)
    yg_fm = C.sbuf(p + "_ygf", [128, 1024], BF16)
    xr = C.sbuf(p + "_xr", [128, 1024], F32)
    xo = [C.sbuf(f"{p}_xo{i}", [128, 1024], F32) for i in range(2)]
    zbias = C.sbuf(p + "_zbias", [128, 1024], F32)
    C.inc(nc.sync.dma_start(zbias[:, :], C.in_ap["c_zbias_rep"][:, :]), p + "bw", 16)

    C.inc(v.memset(zeros32[:, :], 0.0), p + "bw", 1)
    C.inc(v.memset(ones3264[:, :], 1.0), p + "bw", 1)
    C.inc(v.memset(Sst[:, :], 0.0), p + "bw", 1)
    C.inc(gp.memset(Ssh[:, :], 0.0), p + "bw", 1)

    xbcv = xbc[:, 0:5120].rearrange("p (cb t) -> p cb t", cb=40)

    nb = (C.ssd_nchunks if getattr(C, "ssd_nchunks", None) else NT)
    for c in range(nb if "B" in parts else 0):
        b = c % 2
        for g in range(5):
            if c >= 1 and g == 0:
                C.wait(nc.sync, p + "bgm", 8 * c)
                C.wait(nc.sync, p + "bcsv", 32 * c)
                C.wait(nc.sync, p + "bxe", c)
                C.wait(nc.sync, p + "bnm", 8 * c)
            src_ = C.xbc_hbm.ap()[:, g * 8:(g + 1) * 8, c * 128:(c + 1) * 128]
            C.inc(nc.sync.dma_start(
                xbc[:, g * 1024:(g + 1) * 1024].rearrange("p (cb t) -> p cb t", cb=8),
                src_), p + "bxl", 16)
        if c >= 1:
            C.wait(nc.sync, p + "bsg2", 2 * c)
        C.inc(nc.sync.dma_start(zb[:, :], C.z_hbm.ap()[c * 128:(c + 1) * 128, :]),
              p + "bzl", 16)
        if c >= 1:
            C.wait(nc.sync, p + "bep", c)
        C.inc(nc.sync.dma_start(xr[:, :], x_in[c * 128:(c + 1) * 128, :]),
              p + "bxt", 16)
        C.wait(te, p + "bxl", 16 * (5 * c + 1))
        if c >= 1:
            C.wait(te, p + "byge", c)
        C.inc(emit_transpose_tile(C, xbc, pT), p + "bxtp")
        C.wait(v, p + "bxtp")
        if c >= 1:
            C.wait(v, p + "bslm", 8 * c)
        C.inc(v.tensor_copy(x_tm[:, :], pT[:, :]), p + "bxe")
        C.wait(te, p + "bxl", 16 * (5 * c + 3))
        C.wait(te, p + "bxe")
        C.inc(emit_transpose_tile(C, xbc, pT, src_off=8 * 128), p + "bbtp")
        C.wait(v, p + "bbtp")
        if c >= 1:
            C.wait(v, p + "bbwk", 32 * c)
        C.inc(v.tensor_copy(B_tm[:, 0:1024], pT[:, :]), p + "bbe")
        C.wait(te, p + "bbe")
        C.inc(emit_transpose_tile(C, xbc, pT, src_off=16 * 128), p + "bbtp")
        C.wait(v, p + "bbtp")
        C.inc(v.tensor_copy(B_tm[:, 1024:2048], pT[:, :]), p + "bbe")
        if getattr(C, "ssd_cut", None) == 1:
            continue
        C.wait(s, p + "bdt", C.count(p + "bdt"))
        C.inc(s.activation(dts[:, 0:128], A_sb[:, c * 128:(c + 1) * 128], AF.Abs),
              p + "bdt")
        C.wait(s, p + "bdt")
        C.inc(s.activation(dts[:, 128:256], dts[:, 0:128], AF.Exp, scale=-1.0),
              p + "bdt")
        C.wait(s, p + "bdt")
        C.inc(s.activation(dts[:, 256:384], dts[:, 128:256], AF.Ln,
                           bias=C.one_col[0:32, 0:1]), p + "bdt")
        C.wait(s, p + "bdt")
        C.inc(s.activation(dts[:, 384:512], A_sb[:, c * 128:(c + 1) * 128], AF.Relu),
              p + "bdt")
        C.wait(v, p + "bdt")
        if c >= 1:
            C.wait(v, p + "bcum", 2 * c)
        C.inc(v.tensor_tensor(dt_fm[:, :], dts[:, 256:384], dts[:, 384:512], ALU.add),
              p + "bcum")
        C.wait(v, p + "bcum")
        if c >= 1:
            C.wait(v, p + "bemm", 24 * c)
            C.wait(v, p + "berw", c)
            C.wait(v, p + "btot", c)
        C.inc(v.tensor_tensor_scan(cum_fm[:, :], dt_fm[:, :], zeros32[:, :], 0.0,
                                   ALU.add, ALU.add), p + "bcum")
        C.wait(v, p + "bcum")
        if c >= 1:
            C.wait(v, p + "bcw", 32 * c)
        C.inc(v.tensor_scalar(ncum_fm[:, :], cum_fm[:, :], -1.0, None, ALU.mult),
              p + "bncm")
        C.wait(s, p + "bcum", C.count(p + "bcum"))
        if c >= 1:
            C.wait(s, p + "bcw", 32 * c)
            C.wait(s, p + "btt", c)
        C.inc(s.activation(erow_fm[:, :], cum_fm[:, :], AF.Exp, scale=-1.0),
              p + "berw")
        C.inc(s.activation(totcol[:, :], cum_fm[:, 127:128], AF.Exp, scale=-1.0),
              p + "btot")
        C.wait(nc.sync, p + "bncm")
        C.wait(nc.sync, p + "berw")
        if c >= 1:
            C.wait(nc.sync, p + "bcrl", 32 * c)
        C.inc(nc.sync.dma_start(C.bounce_hbm.ap()[:, :], ncum_fm[:, :]),
              p + "bcw", 16)
        C.inc(nc.sync.dma_start(C.bounce2_hbm.ap()[:, :], erow_fm[:, :]),
              p + "bcw", 16)
        C.wait(nc.sync, p + "bcw")
        if c >= 1:
            C.wait(nc.sync, p + "bemm", 24 * c)
            C.wait(nc.sync, p + "btcm", 8 * c)
        C.inc(nc.sync.dma_start(ncum_row[0:1, :],
                                C.bounce_hbm.ap().rearrange("h t -> (h t)")[None, :]),
              p + "bcrl", 16)
        C.inc(nc.sync.dma_start(erow_bf[0:1, :],
                                C.bounce2_hbm.ap().rearrange("h t -> (h t)")[None, :]),
              p + "bcrl", 16)
        if getattr(C, "ssd_cut", None) == 2:
            continue
        for qq in range(8):
            if c == 0 and qq == 0:
                C.wait(te, p + "bw")
            C.wait(te, p + "bcrl", 32 * (c + 1))
            C.wait(te, p + "bcsv", C.count(p + "bcsv"))
            if c >= 1 and qq == 0:
                C.wait(te, p + "bgm", 8 * c)
            for ml in range(2):
                he = 2 * (2 * qq + ml)
                te.matmul(pTC[:, 256 + ml * 128:256 + (ml + 1) * 128],
                          indic[0:1, 0:128], erow_bf[0:1, he * 128:(he + 1) * 128],
                          start=True, stop=False)
                mm = te.matmul(pTC[:, 256 + ml * 128:256 + (ml + 1) * 128],
                               indic[0:1, 128:256],
                               erow_bf[0:1, (he + 1) * 128:(he + 2) * 128],
                               start=False, stop=True)
            C.inc(mm, p + "btcm")
            C.wait(v, p + "btcm")
            C.wait(v, p + "bxl", 16 * (5 * c + 5))
            if c >= 1 and qq == 0:
                C.wait(v, p + "bnm", 8 * c)
            for hh in range(4):
                base = 64 * (hh % 2)
                cb_ = 24 + 2 * qq + hh // 2
                C.inc(v.tensor_tensor(
                    Csc_all[0:64, qq * 512 + hh * 128:qq * 512 + (hh + 1) * 128],
                    xbcv[:, cb_, :][base:base + 64, :],
                    pTC[base + 0:base + 64, 256 + (hh // 2) * 128:
                        256 + (hh // 2) * 128 + 128],
                    ALU.mult), p + "bcsv")
        for q in range(9):
            gq = 8 * c + q
            if q < 8:
                C.wait(te, p + "bcrl", 32 * (c + 1))
                C.wait(te, p + "bcum", C.count(p + "bcum"))
                if c >= 1 and q == 0:
                    C.wait(te, p + "bst", 2 * c)
                if gq >= 1:
                    C.wait(te, p + "bex", gq)
                te.matmul(pE[0][:, 0:512], C.ones_f32[0:1, 0:128],
                          ncum_row[0:1, 4 * q * 128:(4 * q + 4) * 128],
                          start=True, stop=False)
                te.matmul(pE[0][:, 0:512], cum_fm[0:32, :],
                          indicf32[0:32, q * 512:(q + 1) * 512],
                          start=False, stop=False)
                mm = te.matmul(pE[0][:, 0:512], big[:, :], identx4[:, 0:512],
                               start=False, stop=True)
                C.inc(mm, p + "bemm", 3)
                C.wait(s, p + "bemm")
                if gq >= 2:
                    C.wait(s, p + "bpq", 2 * (gq - 1))
                    C.wait(s, p + "bbwk", 4 * (gq - 1))
                C.inc(s.activation(Em_sb[q % 2][:, :], pE[0][:, :], AF.Exp), p + "bex")
                C.wait(s, p + "bex", C.count(p + "bex"))
                C.wait(s, p + "bbe", C.count(p + "bbe"))
                if c >= 1 and q == 0:
                    C.wait(s, p + "bslm", 8 * c)
                for hh in range(4):
                    h = 4 * q + hh
                    C.inc(s.activation(Bw_tm[:, h * 64:(h + 1) * 64],
                                       B_tm[:, h * 64:(h + 1) * 64], AF.Identity,
                                       scale=Em_sb[q % 2][:, hh * 128 + 127:
                                                          hh * 128 + 128]),
                          p + "bbwk")
                C.wait(te, p + "bex", C.count(p + "bex"))
                C.wait(te, p + "bbe", C.count(p + "bbe"))
                C.wait(te, p + "bxl", 16 * (5 * c + 5))
                if gq >= 1:
                    C.wait(te, p + "bpq", 2 * gq)
                C.wait(te, p + "bcsv", C.count(p + "bcsv"))
                for ml in range(2):
                    m = 2 * q + ml
                    bcb = xbcv[:, 8 + m, :]
                    ccb = xbcv[:, 24 + m, :]
                    te.matmul(pE[1][:, ml * 128:(ml + 1) * 128],
                              bcb[0:64, :], ccb[0:64, :], start=True, stop=True,
                              tile_position=(0, 0))
                    mm = te.matmul(pTC[:, ml * 128:(ml + 1) * 128],
                                   bcb[64:128, :], ccb[64:128, :], start=True, stop=True,
                                   tile_position=(64, 0))
                C.inc(mm, p + "bgm")
                C.wait(v, p + "bgm")
                C.wait(v, p + "bex", C.count(p + "bex"))
                if gq >= 2:
                    C.wait(v, p + "bim", gq - 1)
                p4v = P4[q % 2][:, 0:512].rearrange("a (sl t) -> a sl t", sl=4)
                emv = Em_sb[q % 2][:, 0:512].rearrange("a (sl t) -> a sl t", sl=4)
                C.inc(v.tensor_tensor(p4v[:, 0:4:2, :],
                                      pE[1][:, 0:256].rearrange("a (sl t) -> a sl t", sl=2),
                                      emv[:, 0:4:2, :], ALU.mult), p + "bpq")
                C.inc(v.tensor_tensor(p4v[:, 1:4:2, :],
                                      pTC[:, 0:256].rearrange("a (sl t) -> a sl t", sl=2),
                                      emv[:, 1:4:2, :], ALU.mult), p + "bpq")
            if q >= 1:
                qq1 = q - 1
                C.wait(te, p + "bpq", 2 * (8 * c + qq1) + 2)
                C.wait(te, p + "bbwk", 4 * (8 * c + qq1) + 4)
                C.wait(te, p + "bxe", C.count(p + "bxe"))
                if c >= 1 and qq1 == 0:
                    C.wait(te, p + "bye", c)
                    C.wait(te, p + "bep", c)
                    C.wait(te, p + "bsu", 2 * c)
                    C.wait(te, p + "bsh", 2 * c)
                for hh in range(4):
                    h = 4 * qq1 + hh
                    te.matmul(pY[:, h * 32:(h + 1) * 32],
                              P4[qq1 % 2][:, hh * 128:(hh + 1) * 128],
                              x_tm[:, h * 32:(h + 1) * 32], start=True, stop=False)
                    mm = te.matmul(pY[:, h * 32:(h + 1) * 32],
                                   Csc_all[0:64, qq1 * 512 + hh * 128:
                                           qq1 * 512 + (hh + 1) * 128],
                                   Ssh[0:64, h * 32:(h + 1) * 32], start=False, stop=True)
                C.inc(mm, p + "bnm")
                for hh in range(0, 4):
                    h = 4 * qq1 + hh
                    mm = te.matmul(pS[0:64, h * 32:(h + 1) * 32],
                                   Bw_tm[:, h * 64:(h + 1) * 64],
                                   x_tm[:, h * 32:(h + 1) * 32], start=True, stop=True)
                C.inc(mm, p + "bim")
                C.inc(v.engine_nop(), p + "bslm")
        if blevel < 3:
            continue
        C.wait(v, p + "btot")
        if c >= 1:
            C.wait(v, p + "bsm2", 2 * c)
        C.inc(v.tensor_scalar(totb[:, :], ones3264[:, :], totcol[:, 0:1], None,
                              ALU.mult), p + "btt")
        C.wait(te, p + "btt")
        C.wait(te, p + "bpq", C.count(p + "bpq"))
        C.wait(te, p + "bex", C.count(p + "bex"))
        C.wait(te, p + "bim", C.count(p + "bim"))
        for H in range(2):
            C.inc(te.matmul(pE[H][0:64, 0:512], totb[:, :],
                            sel[:, H * 512:(H + 1) * 512], start=True, stop=True),
                  p + "bsm2")
            C.wait(v, p + "bsm2")
            C.inc(v.tensor_tensor(t1s[:, :], Sst[:, H * 512:(H + 1) * 512],
                                  pE[H][0:64, 0:512], ALU.mult),
                  p + "bst")
            C.wait(v, p + "bst")
            C.inc(v.tensor_tensor(Sst[:, H * 512:(H + 1) * 512], t1s[:, :],
                                  pS[0:64, H * 512:(H + 1) * 512], ALU.add),
                  p + "bsu")
            C.wait(v, p + "bsu")
            C.inc(v.tensor_copy(Ssh[0:64, H * 512:(H + 1) * 512],
                                Sst[:, H * 512:(H + 1) * 512]), p + "bsh")
        if getattr(C, "ssd_cut", None) == 4:
            continue
        if blevel < 4:
            continue
        C.wait(v, p + "bnm", C.count(p + "bnm"))
        C.wait(v, p + "bxe", C.count(p + "bxe"))
        C.wait(v, p + "bw")
        C.inc(v.tensor_tensor(xdp[:, :], x_tm[:, :], dp_bf[:, :], ALU.mult), p + "bxd")
        C.wait(v, p + "bxd")
        C.inc(v.tensor_tensor(ysum[:, :], pY[:, :], xdp[:, :], ALU.add), p + "bye")
        C.wait(v, p + "bzl", 16 * (c + 1))
        if c >= 1:
            C.wait(v, p + "bsg", 2 * c)
        C.inc(v.tensor_tensor(zb[:, :], zb[:, :], zbias[:, :], ALU.add), p + "bsg")
        C.wait(s, p + "bsg")
        if c >= 1:
            C.wait(s, p + "byg", c)
        C.inc(s.activation(gsb[:, :], zb[:, :], AF.Tanh, scale=0.5), p + "bsg")
        C.wait(v, p + "bsg")
        C.inc(v.tensor_scalar(gsb[:, :], gsb[:, :], 0.5, 0.5, ALU.mult, ALU.add),
              p + "bsg2")
        C.wait(v, p + "bsg2")
        C.inc(v.tensor_tensor(gsb[:, :], gsb[:, :], zb[:, :], ALU.mult), p + "bsg2")
        C.wait(v, p + "bsg2")
        C.wait(v, p + "bye")
        if c >= 1:
            C.wait(v, p + "bygt", c)
        C.inc(v.tensor_tensor(ygated[:, :], ysum[:, :], gsb[:, :], ALU.mult), p + "byg")
        C.wait(te, p + "byg")
        C.wait(te, p + "bbe", C.count(p + "bbe"))
        C.inc(emit_transpose_tile(C, ygated, pT), p + "bygt")
        C.wait(v, p + "bygt")
        if c >= 1:
            C.wait(v, p + "bwm", c)
        C.inc(v.tensor_copy(yg_fm[:, :], pT[:, :]), p + "byge")
        C.wait(te, p + "byge")
        C.wait(te, wout_sem)
        C.wait(te, p + "bye", C.count(p + "bye"))
        for half in range(2):
            for j in range(8):
                mm = te.matmul(pY[:, half * 512:(half + 1) * 512],
                               yg_fm[:, j * 128:(j + 1) * 128],
                               W_out[:, j * 1024 + half * 512:j * 1024 + (half + 1) * 512],
                               start=(j == 0), stop=(j == 7))
        C.inc(mm, p + "bwm")
        C.wait(v, p + "bwm")
        C.wait(v, p + "bxt", 16 * (c + 1))
        if c >= 2:
            C.wait(v, p + "bres", 16 * (c - 1))
        C.inc(v.tensor_tensor(xo[b][:, :], pY[:, :], xr[:, :], ALU.add), p + "bep")
        C.wait(nc.gpsimd, p + "bep", C.count(p + "bep"))
        C.inc(nc.gpsimd.dma_start(x_out[c * 128:(c + 1) * 128, :], xo[b][:, :]),
              p + "bres", 16)
    if "B" in parts and blevel >= 4 and getattr(C, "ssd_cut", None) is None:
        C.wait(nc.sync, p + "bres")
    C.end_frame()



def build_program(passes=ALL_PASSES, debug_wT=False, ssd_parts=("A", "B"),
                  debug_ssd=False, ssd_blevel=4, ssd_nchunks=None, ssd_cut=None,
                  ssd_qcut=5):
    nc = bass.Bass("TRN2", target_bir_lowering=False)
    C = Ctx(nc)
    C.ssd_nchunks = ssd_nchunks
    C.ssd_cut = ssd_cut
    C.ssd_qcut = ssd_qcut
    if debug_ssd:
        C.dbg_ssd = {
            "xbc": nc.dram_tensor("dbg_xbc", [128, 40 * T], BF16,
                                  kind="ExternalOutput").ap(),
            "z": nc.dram_tensor("dbg_z", [T, D], F32, kind="ExternalOutput").ap(),
        }
    if debug_wT:
        C.dbg_wT = nc.dram_tensor("dbg_wT", [128, 8 * 1024], BF16,
                                  kind="ExternalOutput").ap()
        C.dbg_dcol = nc.dram_tensor("dbg_dcol", [128, 2], F32,
                                    kind="ExternalOutput").ap()
        C.dbg_asum = nc.dram_tensor("dbg_asum", [128, 9], F32,
                                    kind="ExternalOutput").ap()
        C.dbg_ones = nc.dram_tensor("dbg_ones", [1, 128], F32,
                                    kind="ExternalOutput").ap()

    names_shapes = {
        "L0_attn_a_x": [T, D],
        "L0_attn_a_Wq": [D, D], "L0_attn_a_Wk": [D, D], "L0_attn_a_Wv": [D, D],
        "L0_attn_b_W_in": [D, 6176], "L0_attn_b_conv_w": [5120, 4],
        "L0_attn_b_conv_b": [5120], "L0_attn_b_z_bias": [D],
        "L0_attn_b_Dp": [32], "L0_attn_b_W_out": [D, D],
        "L0_ffn_W_r": [D, D], "L0_ffn_W_k": [D, D], "L0_ffn_W_v": [D, D],
        "L0_ffn_mu_r": [D], "L0_ffn_mu_k": [D],
        "L1_attn_a_qkv_w": [D, 3 * D], "L1_attn_a_qkv_b": [3 * D],
        "L1_attn_a_out_w": [D, D], "L1_attn_a_out_b": [D],
        "L1_attn_b_W_qkv": [3 * D, D], "L1_attn_b_b_qkv": [3 * D],
        "L1_ffn_weight": [D, D], "L1_ffn_alpha": [1], "L1_ffn_bias": [D],
        "c_ident_bf": ([128, 128], BF16),
        "c_ident_f32": [128, 128],
        "c_ones_f32": [1, 128],
        "c_mask_ut": [128, 128],
        "c_rope_cos": [T, 32], "c_rope_sin": [T, 32],
        "c_rope_cos_rep": [T, 512], "c_rope_sin_rep": [T, 512],
        "c_mloc": [128, 128],
        "c_qdec": [128, 1], "c_kdec": [128, 1],
        "c_alpha_rep": [128, 1],
        "c_l1ffn_bias_rep": [128, 1024],
        "c_l1a_qkvb_rep": [128, 3 * D],
        "c_l1a_outb_rep": [128, D],
        "c_l1b_bqkv_rep": [128, 3 * D],
        "c_zbias_rep": [128, D],
        "c_dp_rep": [128, D],
        "c_mu_fm": [128, 16],
        "c_convw_fm": [128, 160],
        "c_convb_fm": [128, 40],
        "c_conv_diag": ([160, 128, 128], BF16),
        "c_convb_fm_rep": ([128, 5120], BF16),
        "c_biglowT": ([128, 128], BF16),
        "c_indic": ([1, 256], BF16),
        "c_indicf32": [32, 4096],
        "c_identx4": ([128, 512], BF16),
        "c_selector": [32, 1024],
    }
    C.in_ap = {}
    for name, spec in names_shapes.items():
        if isinstance(spec, tuple):
            shape, dt = spec
        else:
            shape, dt = spec, F32
        C.in_ap[name] = nc.dram_tensor(name, shape, dt, kind="ExternalInput").ap()

    out_ap = nc.dram_tensor("out", [T, D], F32, kind="ExternalOutput").ap()

    C.xs = [nc.dram_tensor(f"xs{i}", [T, D], F32, kind="Internal").ap()
            for i in range(2)]
    C.tern_scratch = nc.dram_tensor("tern_scratch", [128, 8 * 1024], BF16,
                                    kind="Internal")
    C.l0ffn_scratch = {nm: nc.dram_tensor(f"l0ffn_w{nm}", [128, 8 * 1024], BF16,
                                          kind="Internal") for nm in ("r", "k", "v")}
    C.krs_hbm = nc.dram_tensor("krs", [8, 128, T], BF16, kind="Internal")
    C.xbc_hbm = nc.dram_tensor("xbc_scr", [128, 40, T], BF16, kind="Internal")
    C.z_hbm = nc.dram_tensor("z_scr", [T, D], F32, kind="Internal")
    C.bounce_hbm = nc.dram_tensor("bounce", [32, 128], F32, kind="Internal")
    C.bounce2_hbm = nc.dram_tensor("bounce2", [32, 128], BF16, kind="Internal")

    C.ident_bf = nc.alloc_sbuf_tensor("ident_bf", [128, 128], BF16)
    C.ident_f32 = nc.alloc_sbuf_tensor("ident_f32", [128, 128], F32)
    C.ones_f32 = nc.alloc_sbuf_tensor("ones_f32", [1, 128], F32)
    C.eps_col = nc.alloc_sbuf_tensor("eps_col", [128, 1], F32)
    C.negc = nc.alloc_sbuf_tensor("negc", [128, 1], F32)
    C.one_col = nc.alloc_sbuf_tensor("one_col", [128, 1], F32)
    C.inc(nc.sync.dma_start(C.ident_bf[:, :], C.in_ap["c_ident_bf"][:, :]), "init", 16)
    C.inc(nc.sync.dma_start(C.ident_f32[:, :], C.in_ap["c_ident_f32"][:, :]), "init", 16)
    C.inc(nc.sync.dma_start(C.ones_f32[:, :], C.in_ap["c_ones_f32"][:, :]), "init", 16)
    C.inc(nc.vector.memset(C.eps_col[:, :], EPS), "init", 1)
    C.inc(nc.vector.memset(C.negc[:, :], -20.0), "init", 1)
    C.inc(nc.vector.memset(C.one_col[:, :], 1.0), "init", 1)
    for eng in (nc.vector, nc.scalar, nc.tensor, nc.gpsimd, nc.sync):
        C.wait(eng, "init")
    nc.all_engine_barrier()

    cur = C.in_ap["L0_attn_a_x"]
    bufi = 0
    emitters = {
        "l1ffn_1": lambda ci, co: emit_l1ffn(C, ci, co, first=True),
        "l1ffn_2": lambda ci, co: emit_l1ffn(C, ci, co, first=False),
        "l0attn_a": lambda ci, co: emit_l0attn_a(C, ci, co),
        "l1attn_a": lambda ci, co: emit_l1attn_a(C, ci, co),
        "l1attn_b": lambda ci, co: emit_l1attn_b(C, ci, co),
        "l0attn_b": lambda ci, co: emit_l0attn_b(C, ci, co, parts=ssd_parts,
                                                 blevel=ssd_blevel),
        "l0ffn_1": lambda ci, co: emit_l0ffn(C, ci, co, first=True),
        "l0ffn_2": lambda ci, co: emit_l0ffn(C, ci, co, first=False),
    }
    for i, pname in enumerate(passes):
        dest = out_ap if i == len(passes) - 1 else C.xs[bufi]
        C.frame()
        if pname != "l0attn_b":
            C.psT = [C.psum(f"psT{i}_{pname}", [128, 1024], BF16) for i in range(2)]
            C.psF = [C.psum(f"psF{i}_{pname}", [128, 1024], F32) for i in range(3)]
        emitters[pname](cur, dest)
        nc.all_engine_barrier()
        C.end_frame()
        cur = dest
        bufi ^= 1
    return nc



def make_const_inputs():
    import ml_dtypes
    c = {}
    c["c_ident_bf"] = np.eye(128, dtype=ml_dtypes.bfloat16)
    c["c_ident_f32"] = np.eye(128, dtype=np.float32)
    c["c_ones_f32"] = np.ones((1, 128), dtype=np.float32)
    tt = np.arange(128)
    c["c_mask_ut"] = (tt[None, :] >= tt[:, None]).astype(np.float32)
    hd = 64
    freqs = 10000.0 ** (np.arange(0, hd, 2.0, dtype=np.float32) / hd)
    pos = np.arange(T, dtype=np.float32)
    ang = pos[:, None] * freqs[None, :]
    c["c_rope_cos"] = np.cos(ang).astype(np.float32)
    c["c_rope_sin"] = np.sin(ang).astype(np.float32)
    c["c_rope_cos_rep"] = np.ascontiguousarray(np.tile(c["c_rope_cos"], (1, 16)))
    c["c_rope_sin_rep"] = np.ascontiguousarray(np.tile(c["c_rope_sin"], (1, 16)))
    alpha = np.float32(0.99)
    dif = tt[:, None] - tt[None, :]
    c["c_mloc"] = np.where(dif >= 0, alpha ** dif, 0.0).astype(np.float32) / 32.0
    c["c_qdec"] = (alpha ** (tt + 1) / 32.0).astype(np.float32).reshape(128, 1)
    c["c_kdec"] = (alpha ** (127 - tt)).astype(np.float32).reshape(128, 1)
    blt = np.where(tt[None, :] > tt[:, None], -16384.0, 0.0)
    c["c_biglowT"] = blt.astype(ml_dtypes.bfloat16)
    ind = np.zeros((1, 256), np.float32)
    ind[0, 0:64] = 1.0
    ind[0, 192:256] = 1.0
    c["c_indic"] = ind.astype(ml_dtypes.bfloat16)
    ind32 = np.zeros((32, 4096), np.float32)
    for q in range(8):
        for hh in range(4):
            ind32[4 * q + hh, q * 512 + hh * 128:q * 512 + (hh + 1) * 128] = 1.0
    c["c_indicf32"] = ind32
    c["c_identx4"] = np.tile(np.eye(128, dtype=np.float32), (1, 4)).astype(
        ml_dtypes.bfloat16)
    selm = np.zeros((32, 1024), np.float32)
    for h in range(32):
        selm[h, h * 32:(h + 1) * 32] = 1.0
    c["c_selector"] = selm
    return c


def make_in_map(inputs, b, consts):
    import ml_dtypes
    m = {}
    for k, v in inputs.items():
        if k == "L0_attn_a_x":
            m[k] = np.ascontiguousarray(np.asarray(v)[b])
        else:
            m[k] = np.ascontiguousarray(np.asarray(v))
    m.update(consts)
    rep = lambda vec, n: np.ascontiguousarray(
        np.broadcast_to(np.asarray(vec, np.float32).reshape(1, -1), (128, n)))
    m["c_alpha_rep"] = rep(np.broadcast_to(np.asarray(inputs["L1_ffn_alpha"]), (1,)), 1)
    m["c_l1ffn_bias_rep"] = rep(inputs["L1_ffn_bias"], 1024)
    m["c_l1a_qkvb_rep"] = rep(inputs["L1_attn_a_qkv_b"], 3 * D)
    m["c_l1a_outb_rep"] = rep(inputs["L1_attn_a_out_b"], D)
    m["c_l1b_bqkv_rep"] = rep(inputs["L1_attn_b_b_qkv"], 3 * D)
    m["c_zbias_rep"] = rep(inputs["L0_attn_b_z_bias"], D)
    m["c_dp_rep"] = rep(np.repeat(np.asarray(inputs["L0_attn_b_Dp"], np.float32), 32), D)
    mu_fm = np.concatenate([np.asarray(inputs["L0_ffn_mu_r"], np.float32).reshape(8, 128).T,
                            np.asarray(inputs["L0_ffn_mu_k"], np.float32).reshape(8, 128).T],
                           axis=1)
    m["c_mu_fm"] = np.ascontiguousarray(mu_fm)
    cw = np.asarray(inputs["L0_attn_b_conv_w"], np.float32).reshape(40, 128, 4)
    m["c_convw_fm"] = np.ascontiguousarray(cw.transpose(1, 0, 2).reshape(128, 160))
    cb = np.asarray(inputs["L0_attn_b_conv_b"], np.float32).reshape(40, 128)
    m["c_convb_fm"] = np.ascontiguousarray(cb.T)
    cbr = np.asarray(inputs["L0_attn_b_conv_b"], np.float32).reshape(40, 128)
    m["c_convb_fm_rep"] = np.ascontiguousarray(
        np.broadcast_to(cbr.T[:, :, None], (128, 40, 128)).reshape(128, 5120)
        .astype(ml_dtypes.bfloat16))
    import ml_dtypes as _md
    cw = np.asarray(inputs["L0_attn_b_conv_w"], np.float32)
    dg = np.zeros((160, 128, 128), np.float32)
    for cbi in range(40):
        for e in range(4):
            np.fill_diagonal(dg[cbi * 4 + e], cw[cbi * 128:(cbi + 1) * 128, e])
    m["c_conv_diag"] = dg.astype(_md.bfloat16)
    return m


def kernel(**inputs):
    nc = build_program()
    consts = make_const_inputs()
    in_maps = [make_in_map(inputs, b, consts) for b in range(8)]
    res = run_bass_kernel_spmd(nc, in_maps, core_ids=list(range(8)))
    out = np.stack([np.asarray(r["out"]) for r in res.results], axis=0)
    return out.astype(np.float32)


# version 17
# speedup vs baseline: 1.0017x; 1.0017x over previous
import contextlib
import numpy as np
import concourse.bass as bass
import concourse.mybir as mybir
from concourse.bass_utils import run_bass_kernel_spmd

F32 = mybir.dt.float32
BF16 = mybir.dt.bfloat16
AF = mybir.ActivationFunctionType
ALU = mybir.AluOpType
AX = mybir.AxisListType

T = 4096
D = 1024
NT = T // 128
EPS = 1e-5

ALL_PASSES = ("l0attn_a", "l0ffn_1", "l0attn_b", "l0ffn_2",
              "l1attn_a", "l1ffn_1", "l1attn_b", "l1ffn_2")


class Ctx:

    def __init__(self, nc):
        self.nc = nc
        self._guards = []
        self._frames = []
        self._frame_sems = []
        self.sems = {}
        self.counts = {}

    def frame(self):
        self._frames.append(contextlib.ExitStack())
        self._frame_sems.append([])

    def tframe(self):
        self._tframes = getattr(self, "_tframes", [])
        self._tframes.append(contextlib.ExitStack())

    def end_tframe(self):
        self._tframes.pop().close()

    def end_frame(self):
        self.drain_all()
        self.nc.all_engine_barrier()
        self._frames.pop().close()
        names = self._frame_sems.pop()
        if names:
            self.nc.clear_and_free_semaphores([self.sems[n] for n in names])
            self.nc.all_engine_barrier()
        for name in names:
            del self.sems[name]
            del self.counts[name]

    def sbuf(self, name, shape, dtype):
        stack = (self._tframes[-1] if getattr(self, "_tframes", None)
                 else self._frames[-1])
        return stack.enter_context(self.nc.sbuf_tensor(name, shape, dtype))

    def psum(self, name, shape, dtype):
        return self._frames[-1].enter_context(self.nc.psum_tensor(name, shape, dtype))

    def sem(self, name):
        if name not in self.sems:
            h = self.nc.alloc_semaphore(name=name)
            if self._frames:
                self._frame_sems[-1].append(name)
            self.sems[name] = h
            self.counts[name] = 0
        return self.sems[name]

    def inc(self, inst, name, by=1):
        inst.then_inc(self.sem(name), by)
        self.counts[name] += by
        return self.counts[name]

    def wait(self, eng, name, val=None):
        eng.wait_ge(self.sem(name), self.counts[name] if val is None else val)

    def count(self, name):
        self.sem(name)
        return self.counts[name]

    def drain_all(self):
        if self._frames:
            names = self._frame_sems[-1]
        else:
            names = list(self.counts)
        for name in names:
            if self.counts.get(name, 0) > 0:
                self.nc.sync.wait_ge(self.sems[name], self.counts[name])



def emit_ln_tile(C, xt, ln_bf, stats, scr, want_f32=None, sem="lnp", eng_sem_val=None):
    nc = C.nc
    v, s = nc.vector, nc.scalar
    v.bn_stats(stats[:, 0:6], xt[:, 0:512])
    C.inc(v.bn_stats(stats[:, 6:12], xt[:, 512:1024]), sem)
    C.wait(v, sem)
    C.inc(v.bn_aggr(stats[:, 12:14], stats[:, 0:12]), sem)
    C.wait(v, sem)
    C.inc(v.tensor_scalar(stats[:, 14:15], stats[:, 12:13], -1.0, None, ALU.mult), sem)
    C.wait(s, sem)
    C.inc(s.activation(scr[:, 0:1], stats[:, 13:14], AF.Ln, bias=C.eps_col[:, 0:1]), sem)
    C.wait(s, sem)
    C.inc(s.activation(stats[:, 15:16], scr[:, 0:1], AF.Exp, scale=-0.5), sem)
    C.wait(v, sem)
    C.inc(v.tensor_scalar(ln_bf[:, :], xt[:, :], stats[:, 14:15], stats[:, 15:16],
                          ALU.add, ALU.mult), sem)
    if want_f32 is not None:
        C.inc(v.tensor_scalar(want_f32[:, :], xt[:, :], stats[:, 14:15], stats[:, 15:16],
                              ALU.add, ALU.mult), sem)
    return C.count(sem)


def emit_transpose_tile(C, src_bf, dst_psum_bf, n=8, src_off=0):
    nc = C.nc
    t = None
    for j in range(n):
        t = nc.tensor.transpose(dst_psum_bf[:, j * 128:(j + 1) * 128],
                                src_bf[:, src_off + j * 128:src_off + (j + 1) * 128],
                                C.ident_bf[:, :])
    return t


def load_weight_fm(C, dst_sbuf, src_hbm, din, dout, sem=None, queue=None):
    nc = C.nc
    q = queue or nc.gpsimd
    nchunk = din // 128
    ap = src_hbm.rearrange("(c p) o -> p c o", p=128)
    dst = dst_sbuf[:, 0:nchunk * dout].rearrange("p (c o) -> p c o", c=nchunk)
    C._gld = getattr(C, "_gld", 0) + 1
    name = f"gld{C._gld}"
    C.inc(q.dma_start(dst, ap), name, 16)
    return name


def prep_weight_T(C, p, wT, src_ap, sem, prev_cp=None, stage=None, prev_tp=None):
    nc = C.nc
    v, te = nc.vector, nc.tensor
    if stage is None:
        stage = C.sbuf(p + "_stage", [128, 8 * 1024], BF16)
    ap = src_ap.rearrange("(c p) o -> p c o", p=128)
    if prev_tp is not None:
        C.wait(nc.gpsimd, prev_tp)
    C.inc(nc.gpsimd.dma_start(stage[:, :].rearrange("p (c o) -> p c o", c=8), ap),
          sem + "ld", 16)
    C.wait(te, sem + "ld")
    for j in range(8):
        if j > 0:
            C.wait(te, sem + "cp")
        elif prev_cp is not None:
            C.wait(te, prev_cp)
        C.inc(emit_transpose_tile(C, stage, C.psT[0], src_off=j * 1024), sem + "tp")
        C.wait(v, sem + "tp")
        for i in range(8):
            C.inc(v.tensor_copy(wT[:, i * 1024 + j * 128:i * 1024 + (j + 1) * 128],
                                C.psT[0][:, i * 128:(i + 1) * 128]), sem + "cp")



def emit_l1ffn(C, x_in, x_out, first, final_out=None):
    nc = C.nc
    v, s, te = nc.vector, nc.scalar, nc.tensor
    p = "tf1" if first else "tf2"

    wT = C.sbuf(p + "_wT", [128, 8 * 1024], BF16)
    if first:
        wload = C.sbuf(p + "_wload", [128, 8 * 1024], F32)
        sgn = C.sbuf(p + "_sgn", [128, 3 * 1024], F32)
        wt_bf = C.sbuf(p + "_wtbf", [128, 1024], BF16)
        asum = C.sbuf(p + "_asum", [128, 9], F32)
        dcol = C.sbuf(p + "_dcol", [128, 2], F32)
        ones_col = C.sbuf(p + "_ones", [128, 1], F32)
        dps = C.psF[2][0:1, 0:1]
        bps = C.psF[2][:, 512:513]
        tps = C.psT[0]

        ap = C.in_ap["L1_ffn_weight"].rearrange("(c p) o -> p c o", p=128)
        C.inc(nc.sync.dma_start(wload[:, :].rearrange("p (c o) -> p c o", c=8), ap),
              p + "w", 16)
        C.wait(v, p + "w")
        for j in range(8):
            C.inc(v.tensor_reduce(asum[:, j:j + 1], wload[:, j * 1024:(j + 1) * 1024],
                                  AX.X, ALU.add, apply_absolute_value=True), p + "a")
        C.wait(v, p + "a")
        C.inc(v.tensor_reduce(asum[:, 8:9], asum[:, 0:8], AX.X, ALU.add), p + "a")
        C.inc(v.memset(ones_col[:, :], 1.0), p + "a")
        C.wait(te, p + "a")
        C.inc(te.matmul(dps, asum[:, 8:9], ones_col[:, :], start=True, stop=True),
              p + "a")
        C.wait(s, p + "a")
        C.inc(s.activation(dcol[0:1, 0:1], dps, AF.Identity,
                           scale=0.7 / (1024.0 * 1024.0)), p + "a")
        C.wait(te, p + "a")
        C.inc(te.matmul(bps, C.ones_f32[0:1, 0:128], dcol[0:1, 0:1],
                        start=True, stop=True), p + "a")
        C.wait(v, p + "a")
        C.inc(v.tensor_copy(dcol[:, 0:1], bps), p + "a")
        C.wait(v, p + "a")
        C.inc(v.tensor_scalar(dcol[:, 1:2], dcol[:, 0:1], -1.0, None, ALU.mult), p + "a")
        C.wait(s, p + "a")
        for j in range(8):
            wj = wload[:, j * 1024:(j + 1) * 1024]
            if j > 0:
                C.wait(s, p + "t")
            C.inc(s.sign(sgn[:, 0:1024], wj, bias=dcol[:, 1:2]), p + "s")
            C.inc(s.sign(sgn[:, 1024:2048], wj, bias=dcol[:, 0:1]), p + "s")
            C.wait(v, p + "s")
            if j > 0:
                C.wait(v, p + "tp", j)
            C.inc(v.tensor_tensor(sgn[:, 2048:3072], sgn[:, 0:1024], sgn[:, 1024:2048],
                                  ALU.add), p + "g")
            C.inc(v.tensor_scalar(wt_bf[:, :], sgn[:, 2048:3072], 0.5, None, ALU.mult),
                  p + "g")
            C.wait(te, p + "g")
            if j > 0:
                C.wait(te, p + "t")
            C.inc(emit_transpose_tile(C, wt_bf, tps), p + "tp")
            C.wait(v, p + "tp")
            for i in range(8):
                C.inc(v.tensor_copy(wT[:, i * 1024 + j * 128:i * 1024 + (j + 1) * 128],
                                    tps[:, i * 128:(i + 1) * 128]), p + "t")
        C.wait(nc.sync, p + "t")
        C.inc(nc.sync.dma_start(C.tern_scratch.ap()[:, :], wT[:, :]), p + "w", 16)
        if getattr(C, "dbg_wT", None) is not None:
            C.inc(nc.sync.dma_start(C.dbg_wT[:, :], wT[:, :]), p + "w", 16)
            C.inc(nc.sync.dma_start(C.dbg_dcol[:, :], dcol[:, :]), p + "w", 16)
            C.inc(nc.sync.dma_start(C.dbg_asum[:, :], asum[:, :]), p + "w", 16)
            C.inc(nc.sync.dma_start(C.dbg_ones[:, :], C.ones_f32[:, :]), p + "w", 16)
    else:
        C.inc(nc.sync.dma_start(wT[:, :], C.tern_scratch.ap()[:, :]), p + "w", 16)

    alpha_col = C.sbuf(p + "_alpha", [128, 1], F32)
    bias_rep = C.sbuf(p + "_bias", [128, 1024], F32)
    C.inc(nc.sync.dma_start(alpha_col[:, :], C.in_ap["c_alpha_rep"][:, :]), p + "w", 16)
    C.inc(nc.sync.dma_start(bias_rep[:, :], C.in_ap["c_l1ffn_bias_rep"][:, :]), p + "w", 16)

    xt = [C.sbuf(f"{p}_xt{i}", [128, D], F32) for i in range(3)]
    lnb = [C.sbuf(f"{p}_ln{i}", [128, D], BF16) for i in range(2)]
    fm = [C.sbuf(f"{p}_fm{i}", [128, D], BF16) for i in range(2)]
    xo = [C.sbuf(f"{p}_xo{i}", [128, D], F32) for i in range(2)]
    stats = C.sbuf(p + "_st", [128, 16], F32)
    scr = C.sbuf(p + "_scr", [128, 1], F32)
    psT = C.psT
    pso = C.psF

    gp = nc.gpsimd

    def ldx(i):
        if i >= 3:
            C.wait(nc.sync, p + "ep", 2 * (i - 3) + 2)
        C.inc(nc.sync.dma_start(xt[i % 3][:, :], x_in[i * 128:(i + 1) * 128, :]),
              p + "ld", 16)

    def ln_tile(i):
        b = i % 2
        C.wait(v, p + "ld", 16 * (i + 1))
        C.wait(s, p + "ld", 16 * (i + 1))
        C.wait(v, p + "w")
        if i >= 2:
            C.wait(v, p + "stp", i - 1)
        emit_ln_tile(C, xt[i % 3], lnb[b], stats, scr, sem=p + "ln")
        C.wait(te, p + "ln", 6 * (i + 1))
        if i >= 2:
            C.wait(te, p + "ev", i - 1)
        elif first and i == 0:
            C.wait(te, p + "t")
        C.inc(emit_transpose_tile(C, lnb[b], psT[b]), p + "stp")
        C.wait(v, p + "stp", i + 1)
        if i >= 2:
            C.wait(v, p + "mm", i - 1)
        C.inc(v.tensor_copy(fm[b][:, :], psT[b][:, :]), p + "ev")

    ldx(0)
    ldx(1)
    ln_tile(0)
    for i in range(NT):
        b = i % 2
        if i + 2 < NT:
            ldx(i + 2)
        if i + 1 < NT:
            ln_tile(i + 1)
        C.wait(te, p + "ev", i + 1)
        C.wait(te, p + "w")
        if i >= 2:
            C.wait(te, p + "ep", 2 * (i - 2) + 1)
        mm = None
        for half in range(2):
            for j in range(8):
                mm = te.matmul(pso[b][:, half * 512:(half + 1) * 512],
                               fm[b][:, j * 128:(j + 1) * 128],
                               wT[:, j * 1024 + half * 512:j * 1024 + (half + 1) * 512],
                               start=(j == 0), stop=(j == 7))
        C.inc(mm, p + "mm")
        C.wait(v, p + "mm", i + 1)
        if i >= 2:
            C.wait(v, p + "res", 16 * (i - 1))
        C.inc(v.scalar_tensor_tensor(xo[b][:, :], pso[b][:, :], alpha_col[:, 0:1],
                                     bias_rep[:, :], ALU.mult, ALU.add), p + "ep")
        C.inc(v.tensor_tensor(xo[b][:, :], xo[b][:, :], xt[i % 3][:, :], ALU.add),
              p + "ep")
        C.wait(gp, p + "ep", 2 * (i + 1))
        C.inc(gp.dma_start(x_out[i * 128:(i + 1) * 128, :], xo[b][:, :]),
              p + "res", 16)
    C.wait(nc.sync, p + "res")



def emit_l0ffn(C, x_in, x_out, first):
    nc = C.nc
    v, s, te = nc.vector, nc.scalar, nc.tensor
    p = "cf1" if first else "cf2"

    wrT = C.sbuf(p + "_wrT", [128, 8 * 1024], BF16)
    wkT = C.sbuf(p + "_wkT", [128, 8 * 1024], BF16)
    wvT = C.sbuf(p + "_wvT", [128, 8 * 1024], BF16)
    mu = C.sbuf(p + "_mu", [128, 16], F32)
    C.inc(nc.sync.dma_start(mu[:, :], C.in_ap["c_mu_fm"][:, :]), p + "w", 16)
    if first:
        prep_weight_T(C, p + "wr", wrT, C.in_ap["L0_ffn_W_r"], p + "r")
        prep_weight_T(C, p + "wk", wkT, C.in_ap["L0_ffn_W_k"], p + "k",
                      prev_cp=p + "rcp")
        prep_weight_T(C, p + "wv", wvT, C.in_ap["L0_ffn_W_v"], p + "v",
                      prev_cp=p + "kcp")
        for nm, wt in (("r", wrT), ("k", wkT), ("v", wvT)):
            C.wait(nc.sync, p + nm + "cp")
            C.inc(nc.sync.dma_start(C.l0ffn_scratch[nm].ap()[:, :], wt[:, :]),
                  p + "w", 16)
    else:
        for nm, wt in (("r", wrT), ("k", wkT), ("v", wvT)):
            C.inc(nc.sync.dma_start(wt[:, :], C.l0ffn_scratch[nm].ap()[:, :]),
                  p + "w", 16)

    xt = [C.sbuf(f"{p}_xt{i}", [128, D], F32) for i in range(2)]
    lnb = [C.sbuf(f"{p}_ln{i}", [128, D], BF16) for i in range(2)]
    fm = [C.sbuf(f"{p}_fm{i}", [128, D], BF16) for i in range(2)]
    shf = C.sbuf(p + "_shf", [128, D], BF16)
    dfm = C.sbuf(p + "_dfm", [128, D], BF16)
    xr = C.sbuf(p + "_xr", [128, D], BF16)
    xk = C.sbuf(p + "_xk", [128, D], BF16)
    rr = C.sbuf(p + "_rr", [128, D], F32)
    krel = C.sbuf(p + "_krel", [128, D], F32)
    kact = C.sbuf(p + "_kact", [128, D], BF16)
    kfm = C.sbuf(p + "_kfm", [128, D], BF16)
    xo = [C.sbuf(f"{p}_xo{i}", [128, D], F32) for i in range(2)]
    stats = C.sbuf(p + "_st", [128, 16], F32)
    scr = C.sbuf(p + "_scr", [128, 1], F32)

    def chunked(t, off, w):
        return t[:, 0:D].rearrange("p (c q) -> p c q", c=8)[:, :, off:off + w]

    for i in range(NT):
        b = i % 2
        if i >= 2:
            C.wait(nc.sync, p + "ep", 2 * (i - 2) + 2)
        C.inc(nc.sync.dma_start(xt[b][:, :], x_in[i * 128:(i + 1) * 128, :]),
              p + "ld", 16)
        C.wait(v, p + "ld")
        C.wait(s, p + "ld")
        C.wait(v, p + "w")
        if i >= 2:
            C.wait(v, p + "stp", i - 1)
        emit_ln_tile(C, xt[b], lnb[b], stats, scr, sem=p + "ln")
        C.wait(te, p + "ln")
        if i >= 2:
            C.wait(te, p + "kev", i - 1)
        elif first and i == 0 and C.count(p + "vcp") > 0:
            C.wait(te, p + "vcp")
        C.inc(emit_transpose_tile(C, lnb[b], psTb(C, b)), p + "stp")
        C.wait(v, p + "stp")
        if i >= 1:
            C.wait(v, p + "mm", 3 * i - 2)
        C.inc(v.tensor_copy(fm[b][:, :], psTb(C, b)[:, :]), p + "ev")
        C.wait(v, p + "ev")
        C.inc(v.tensor_copy(chunked(shf, 1, 127), chunked(fm[b], 0, 127)), p + "sh")
        if i == 0:
            C.inc(v.memset(chunked(shf, 0, 1), 0.0), p + "sh")
        else:
            C.inc(v.tensor_copy(chunked(shf, 0, 1), chunked(fm[1 - b], 127, 1)),
                  p + "sh")
        C.wait(v, p + "sh")
        C.inc(v.tensor_tensor(dfm[:, :], fm[b][:, :], shf[:, :], ALU.subtract), p + "mx")
        C.wait(v, p + "mx")
        if i >= 1:
            C.wait(v, p + "mm", 3 * (i - 1) + 2)
        for j in range(8):
            sl = slice(j * 128, (j + 1) * 128)
            C.inc(v.scalar_tensor_tensor(xr[:, sl], dfm[:, sl], mu[:, j:j + 1],
                                         shf[:, sl], ALU.mult, ALU.add), p + "mxr")
        for j in range(8):
            sl = slice(j * 128, (j + 1) * 128)
            C.inc(v.scalar_tensor_tensor(xk[:, sl], dfm[:, sl], mu[:, 8 + j:9 + j],
                                         shf[:, sl], ALU.mult, ALU.add), p + "mxk")
        C.wait(te, p + "mxr")
        C.wait(te, p + "w")
        if i >= 1:
            C.wait(te, p + "sg", i)
            C.wait(te, p + "kv", 2 * i - 1)
        mm = None
        for half in range(2):
            for j in range(8):
                mm = te.matmul(C.psF[0][:, half * 512:(half + 1) * 512],
                               xr[:, j * 128:(j + 1) * 128],
                               wrT[:, j * 1024 + half * 512:j * 1024 + (half + 1) * 512],
                               start=(j == 0), stop=(j == 7))
        C.inc(mm, p + "mm")
        C.wait(te, p + "mxk")
        for half in range(2):
            for j in range(8):
                mm = te.matmul(C.psF[1][:, half * 512:(half + 1) * 512],
                               xk[:, j * 128:(j + 1) * 128],
                               wkT[:, j * 1024 + half * 512:j * 1024 + (half + 1) * 512],
                               start=(j == 0), stop=(j == 7))
        C.inc(mm, p + "mm")
        C.wait(s, p + "mm", C.count(p + "mm") - 1)
        if i >= 1:
            C.wait(s, p + "ep", 2 * (i - 1) + 1)
        C.inc(s.activation(rr[:, :], C.psF[0][:, :], AF.Tanh, scale=0.5), p + "sg")
        C.wait(s, p + "mm")
        C.inc(s.activation(krel[:, :], C.psF[1][:, :], AF.Relu), p + "kv")
        if i >= 1:
            C.wait(s, p + "ktp", i)
        C.inc(s.activation(kact[:, :], krel[:, :], AF.Square), p + "kv")
        C.wait(te, p + "kv")
        C.wait(te, p + "ev", C.count(p + "ev"))
        C.inc(emit_transpose_tile(C, kact, psTb(C, b)), p + "ktp")
        C.wait(v, p + "ktp")
        if i >= 1:
            C.wait(v, p + "mm", 3 * (i - 1) + 3)
        C.inc(v.tensor_copy(kfm[:, :], psTb(C, b)[:, :]), p + "kev")
        C.wait(te, p + "kev")
        if i >= 1:
            C.wait(te, p + "ep", 2 * (i - 1) + 1)
        for half in range(2):
            for j in range(8):
                mm = te.matmul(C.psF[2][:, half * 512:(half + 1) * 512],
                               kfm[:, j * 128:(j + 1) * 128],
                               wvT[:, j * 1024 + half * 512:j * 1024 + (half + 1) * 512],
                               start=(j == 0), stop=(j == 7))
        C.inc(mm, p + "mm")
        C.wait(v, p + "mm")
        C.wait(v, p + "sg")
        if i >= 2:
            C.wait(v, p + "res", 16 * (i - 1))
        C.inc(v.tensor_scalar(rr[:, :], rr[:, :], 0.5, 0.5, ALU.mult, ALU.add), p + "sg2")
        C.wait(v, p + "sg2")
        C.inc(v.tensor_tensor(xo[b][:, :], C.psF[2][:, :], rr[:, :], ALU.mult), p + "ep")
        C.inc(v.tensor_tensor(xo[b][:, :], xo[b][:, :], xt[b][:, :], ALU.add), p + "ep")
        C.wait(nc.gpsimd, p + "ep", C.count(p + "ep"))
        C.inc(nc.gpsimd.dma_start(x_out[i * 128:(i + 1) * 128, :], xo[b][:, :]),
              p + "res", 16)
    C.wait(nc.sync, p + "res")


def psTb(C, b):
    return C.psT[b]



def emit_l0attn_a(C, x_in, x_out):
    nc = C.nc
    v, s, te = nc.vector, nc.scalar, nc.tensor
    p = "aa"

    K_fm = C.sbuf(p + "_Kfm", [128, 8 * T], BF16)
    V_tm = C.sbuf(p + "_Vtm", [128, NT * 1024], BF16)

    C.frame()
    wkT = C.sbuf(p + "_wkT", [128, 8 * 1024], BF16)
    wvT = C.sbuf(p + "_wvT", [128, 8 * 1024], BF16)
    stageA = C.sbuf(p + "_stageA", [128, 8 * 1024], BF16)
    prep_weight_T(C, p + "wk", wkT, C.in_ap["L0_attn_a_Wk"], p + "k", stage=stageA)
    prep_weight_T(C, p + "wv", wvT, C.in_ap["L0_attn_a_Wv"], p + "v",
                  prev_cp=p + "kcp", stage=stageA, prev_tp=p + "ktp")
    xt = [C.sbuf(f"{p}_xt{i}", [128, D], F32) for i in range(2)]
    lnb = [C.sbuf(f"{p}_ln{i}", [128, D], BF16) for i in range(2)]
    fm = [C.sbuf(f"{p}_fm{i}", [128, D], BF16) for i in range(2)]
    stats = C.sbuf(p + "_st", [128, 16], F32)
    scr = C.sbuf(p + "_scr", [128, 1], F32)

    for i in range(NT):
        b = i % 2
        if i >= 2:
            C.wait(nc.sync, p + "aln", 6 * (i - 1))
        C.inc(nc.sync.dma_start(xt[b][:, :], x_in[i * 128:(i + 1) * 128, :]),
              p + "ald", 16)
        C.wait(v, p + "ald")
        C.wait(s, p + "ald")
        if i >= 2:
            C.wait(v, p + "astp", i - 1)
        emit_ln_tile(C, xt[b], lnb[b], stats, scr, sem=p + "aln")
        C.wait(te, p + "aln")
        if i >= 2:
            C.wait(te, p + "aev", i - 1)
        elif i == 0:
            C.wait(te, p + "vcp")
        C.inc(emit_transpose_tile(C, lnb[b], C.psT[b]), p + "astp")
        C.wait(v, p + "astp")
        if i >= 2:
            C.wait(v, p + "avm", i - 1)
        C.inc(v.tensor_copy(fm[b][:, :], C.psT[b][:, :]), p + "aev")
        C.wait(te, p + "aev")
        C.wait(te, p + "kld")
        C.wait(te, p + "vld")
        if i >= 1:
            C.wait(te, p + "akc", i)
        mm = None
        for dblk in range(8):
            for j in range(8):
                mm = te.matmul(C.psF[0][:, dblk * 128:(dblk + 1) * 128],
                               wkT[:, j * 1024 + dblk * 128:j * 1024 + (dblk + 1) * 128],
                               fm[b][:, j * 128:(j + 1) * 128],
                               start=(j == 0), stop=(j == 7))
        C.inc(mm, p + "akm")
        C.wait(v, p + "akm")
        kdst = K_fm[:, 0:8 * T].rearrange("q (c t) -> q c t", c=8)[:, :, i * 128:(i + 1) * 128]
        ksrc = C.psF[0][:, 0:1024].rearrange("q (c t) -> q c t", c=8)
        C.inc(v.tensor_copy(kdst, ksrc), p + "akc")
        if i >= 1:
            C.wait(te, p + "avc", i)
        for half in range(2):
            for j in range(8):
                mm = te.matmul(C.psF[1][:, half * 512:(half + 1) * 512],
                               fm[b][:, j * 128:(j + 1) * 128],
                               wvT[:, j * 1024 + half * 512:j * 1024 + (half + 1) * 512],
                               start=(j == 0), stop=(j == 7))
        C.inc(mm, p + "avm")
        C.wait(v, p + "avm")
        C.inc(v.tensor_copy(V_tm[:, i * 1024:(i + 1) * 1024], C.psF[1][:, :]), p + "avc")
    nc.all_engine_barrier()
    C.end_frame()

    C.frame()
    wqT = C.sbuf(p + "_wqT", [128, 8 * 1024], BF16)
    C.tframe()
    prep_weight_T(C, p + "wq", wqT, C.in_ap["L0_attn_a_Wq"], p + "q")
    C.end_tframe()
    xt2 = [C.sbuf(f"{p}_x2{i}", [128, D], F32) for i in range(3)]
    lnb2 = [C.sbuf(f"{p}_l2{i}", [128, D], BF16) for i in range(2)]
    fm2 = [C.sbuf(f"{p}_f2{i}", [128, D], BF16) for i in range(2)]
    qfm = C.sbuf(p + "_qfm", [128, 1024], BF16)
    Ssc = C.sbuf(p + "_Ssc", [128, 4096], F32)
    Pb = C.sbuf(p + "_P", [128, 4096], BF16)
    PT = C.sbuf(p + "_PT", [128, 4096], BF16)
    mx = C.sbuf(p + "_mx", [128, 12], F32)
    lsum = C.sbuf(p + "_ls", [128, 10], F32)
    xo = [C.sbuf(f"{p}_xo{i}", [128, D], F32) for i in range(2)]
    stats2 = C.sbuf(p + "_st2", [128, 16], F32)
    scr2 = C.sbuf(p + "_sc2", [128, 1], F32)

    def ldxB(j):
        if j >= 3:
            C.wait(nc.sync, p + "bep", j - 2)
        elif j <= 1:
            C.wait(nc.sync, p + "qtp")
        C.inc(nc.sync.dma_start(xt2[j % 3][:, :], x_in[j * 128:(j + 1) * 128, :]),
              p + "bld", 16)

    ldxB(0)
    ldxB(1)
    for i in range(NT):
        b = i % 2
        if i + 2 < NT:
            ldxB(i + 2)
        C.wait(v, p + "bld", 16 * (i + 1))
        C.wait(s, p + "bld", 16 * (i + 1))
        if i >= 2:
            C.wait(v, p + "bstp", i - 1)
        emit_ln_tile(C, xt2[i % 3], lnb2[b], stats2, scr2, sem=p + "bln")
        C.wait(te, p + "bln")
        if i >= 2:
            C.wait(te, p + "bev", i - 1)
        elif i == 0:
            C.wait(te, p + "qcp")
        if i >= 1:
            C.wait(te, p + "bptc", 4 * i)
        C.inc(emit_transpose_tile(C, lnb2[b], C.psT[b]), p + "bstp")
        C.wait(v, p + "bstp")
        if i >= 2:
            C.wait(v, p + "bqm", i - 1)
        C.inc(v.tensor_copy(fm2[b][:, :], C.psT[b][:, :]), p + "bev")
        C.wait(te, p + "bev")
        C.wait(te, p + "qld")
        if i >= 1:
            C.wait(te, p + "bqc", i)
        for dblk in range(8):
            for j in range(8):
                mm = te.matmul(C.psF[2][:, dblk * 128:(dblk + 1) * 128],
                               wqT[:, j * 1024 + dblk * 128:j * 1024 + (dblk + 1) * 128],
                               fm2[b][:, j * 128:(j + 1) * 128],
                               start=(j == 0), stop=(j == 7))
        C.inc(mm, p + "bqm")
        C.wait(v, p + "bqm")
        if i >= 1:
            C.wait(v, p + "bsm", 8 * i)
        C.inc(v.tensor_copy(qfm[:, :], C.psF[2][:, :]), p + "bqc")
        C.wait(te, p + "bqc")
        for kb in range(8):
            half = kb % 2
            C.wait(te, p + "bse", max(0, 8 * i + kb - 1))
            for j in range(8):
                mm = te.matmul(C.psF[0][:, half * 512:(half + 1) * 512],
                               qfm[:, j * 128:(j + 1) * 128],
                               K_fm[:, j * T + kb * 512:j * T + (kb + 1) * 512],
                               start=(j == 0), stop=(j == 7))
            C.inc(mm, p + "bsm")
            C.wait(s, p + "bsm")
            if i >= 1 and kb == 0:
                C.wait(s, p + "bmx", 10 * i)
                C.wait(s, p + "bpt", 4 * i)
            C.inc(s.copy(Ssc[:, kb * 512:(kb + 1) * 512],
                         C.psF[0][:, half * 512:(half + 1) * 512]), p + "bse")
            C.wait(v, p + "bse")
            C.inc(v.reduce_max(mx[:, kb:kb + 1], Ssc[:, kb * 512:(kb + 1) * 512],
                               AX.X), p + "bmx")
        C.wait(v, p + "bmx")
        C.inc(v.reduce_max(mx[:, 9:10], mx[:, 0:8], AX.X), p + "bmx")
        C.wait(v, p + "bmx")
        C.inc(v.tensor_scalar(mx[:, 10:11], mx[:, 9:10], -1.0, None, ALU.mult),
              p + "bmx")
        C.wait(s, p + "bmx")
        for kb in range(8):
            C.inc(s.activation(Pb[:, kb * 512:(kb + 1) * 512],
                               Ssc[:, kb * 512:(kb + 1) * 512], AF.Exp,
                               bias=mx[:, 10:11],
                               accum_out=lsum[:, kb:kb + 1]), p + "bex")
        C.wait(v, p + "bex")
        C.inc(v.tensor_reduce(mx[:, 11:12], lsum[:, 0:8], AX.X, ALU.add), p + "blr")
        C.wait(v, p + "blr")
        C.inc(v.tensor_scalar(lsum[:, 8:9], mx[:, 11:12], 32.0, None, ALU.mult),
              p + "blr")
        C.wait(v, p + "blr")
        C.inc(v.reciprocal(lsum[:, 9:10], lsum[:, 8:9]), p + "blr")
        C.wait(te, p + "bex")
        for n in range(4):
            bank = C.psT[n % 2]
            if n == 0:
                C.wait(te, p + "bev", C.count(p + "bev"))
                if i >= 1:
                    C.wait(te, p + "bptc", 4 * i)
            elif n >= 2:
                C.wait(te, p + "bptc", 4 * i + n - 1)
            C.inc(emit_transpose_tile(C, Pb, bank, src_off=n * 1024), p + "bpt")
            C.wait(v, p + "bpt")
            C.inc(v.tensor_copy(PT[:, n * 1024:(n + 1) * 1024], bank[:, :]), p + "bptc")
        C.wait(te, p + "bptc")
        if i >= 1:
            C.wait(te, p + "bep", i)
        for half in range(2):
            for kk in range(32):
                mm = te.matmul(C.psF[1][:, half * 512:(half + 1) * 512],
                               PT[:, kk * 128:(kk + 1) * 128],
                               V_tm[:, kk * 1024 + half * 512:kk * 1024 + (half + 1) * 512],
                               start=(kk == 0), stop=(kk == 31))
        C.inc(mm, p + "bvm")
        C.wait(v, p + "bvm")
        C.wait(v, p + "blr")
        if i >= 2:
            C.wait(v, p + "bres", 16 * (i - 1))
        C.inc(v.scalar_tensor_tensor(xo[b][:, :], C.psF[1][:, :], lsum[:, 9:10],
                                     xt2[i % 3][:, :], ALU.mult, ALU.add), p + "bep")
        C.wait(nc.gpsimd, p + "bep", C.count(p + "bep"))
        C.inc(nc.gpsimd.dma_start(x_out[i * 128:(i + 1) * 128, :], xo[b][:, :]),
              p + "bres", 16)
    C.wait(nc.sync, p + "bres")
    C.end_frame()



def emit_rope(C, p, qk_tm, out_bf, cs, sem):
    v = C.nc.vector
    scr = C.rp_scr
    def hv(t, off, width, stride, base):
        stop = off + (width - 1) * stride + 1
        return t[:, base:base + 1024].rearrange(
            "p (h d) -> p h d", h=16)[:, :, off:stop:stride]
    x1 = hv(qk_tm, 0, 32, 2, 0)
    x2 = hv(qk_tm, 1, 32, 2, 0)
    cos = cs[:, 0:512].rearrange("p (h j) -> p h j", h=16)
    sin = cs[:, 512:1024].rearrange("p (h j) -> p h j", h=16)
    s1 = scr[:, 0:512].rearrange("p (h j) -> p h j", h=16)
    s2 = scr[:, 512:1024].rearrange("p (h j) -> p h j", h=16)
    o_lo = hv(out_bf, 0, 32, 1, 0)
    o_hi = hv(out_bf, 32, 32, 1, 0)
    v.tensor_tensor(s1, x1, cos, ALU.mult)
    v.tensor_tensor(s2, x2, sin, ALU.mult)
    C.inc(v.tensor_tensor(o_lo, s1, s2, ALU.subtract), sem)
    v.tensor_tensor(s1, x2, cos, ALU.mult)
    v.tensor_tensor(s2, x1, sin, ALU.mult)
    C.inc(v.tensor_tensor(o_hi, s1, s2, ALU.add), sem)


def emit_l1attn_a(C, x_in, x_out):
    nc = C.nc
    v, s, te = nc.vector, nc.scalar, nc.tensor
    p = "la"
    NSUP = 8
    pT = C.psT[0]
    pA = [C.psF[0], C.psF[1]]
    pB = C.psF[2]

    V_tm = C.sbuf(p + "_Vtm", [128, NT * 16 * 65], BF16)
    C.rp_scr = C.sbuf(p + "_rps", [128, 1024], F32)
    qkvb_q = C.sbuf(p + "_qkvbq", [128, D], F32)
    C.inc(nc.sync.dma_start(qkvb_q[:, :], C.in_ap["c_l1a_qkvb_rep"][:, 0:1024]),
          p + "w", 16)
    von = V_tm[:, 0:NT * 16 * 65].rearrange("p (t h e) -> p t h e", t=NT, h=16)
    C.inc(v.memset(von[:, :, :, 64:65], 1.0), p + "w", 1)

    C.frame()
    wkv = C.sbuf(p + "_wkv", [128, 8 * 2048], BF16)
    wkv_sem = load_weight_fm(C, wkv, C.in_ap["L1_attn_a_qkv_w"][:, 1024:3072], D, 2048)
    qkvb_kv = C.sbuf(p + "_qkvkv", [128, 2 * D], F32)
    C.inc(nc.sync.dma_start(qkvb_kv[:, :], C.in_ap["c_l1a_qkvb_rep"][:, 1024:3072]),
          p + "w", 16)
    xt = [C.sbuf(f"{p}_xt{i}", [128, D], F32) for i in range(2)]
    lnb = [C.sbuf(f"{p}_ln{i}", [128, D], BF16) for i in range(2)]
    fm = [C.sbuf(f"{p}_fm{i}", [128, D], BF16) for i in range(2)]
    ktm = C.sbuf(p + "_ktm", [128, D], F32)
    krb = C.sbuf(p + "_krb", [128, D], BF16)
    krs = C.sbuf(p + "_krs", [128, D], BF16)
    cs = [C.sbuf(f"{p}_cs{i}", [128, 1024], F32) for i in range(2)]
    stats = C.sbuf(p + "_st", [128, 16], F32)
    scr = C.sbuf(p + "_scr", [128, 1], F32)

    cos_rep = C.in_ap["c_rope_cos_rep"]
    sin_rep = C.in_ap["c_rope_sin_rep"]
    for i in range(NT):
        b = i % 2
        if i >= 2:
            C.wait(nc.sync, p + "aln", 6 * (i - 1))
        C.inc(nc.sync.dma_start(xt[b][:, :], x_in[i * 128:(i + 1) * 128, :]),
              p + "ald", 16)
        if i >= 2:
            C.wait(nc.sync, p + "arp", 2 * (i - 1))
        C.inc(nc.sync.dma_start(cs[b][:, 0:512], cos_rep[i * 128:(i + 1) * 128, :]),
              p + "acs", 16)
        C.inc(nc.sync.dma_start(cs[b][:, 512:1024], sin_rep[i * 128:(i + 1) * 128, :]),
              p + "acs", 16)
        C.wait(v, p + "ald")
        C.wait(s, p + "ald")
        if i >= 2:
            C.wait(v, p + "astp", i - 1)
        emit_ln_tile(C, xt[b], lnb[b], stats, scr, sem=p + "aln")
        C.wait(te, p + "aln")
        if i >= 1:
            C.wait(te, p + "akr", i)
        C.inc(emit_transpose_tile(C, lnb[b], pT), p + "astp")
        C.wait(v, p + "astp")
        if i >= 2:
            C.wait(v, p + "avm", 2 * (i - 1))
        C.inc(v.tensor_copy(fm[b][:, :], pT[:, :]), p + "aev")
        C.wait(te, p + "aev")
        C.wait(te, p + "w")
        C.wait(te, wkv_sem)
        if i >= 1:
            C.wait(te, p + "ake", i)
        mm = None
        for j in range(8):
            for half in range(2):
                mm = te.matmul(pA[0][:, half * 512:(half + 1) * 512],
                               fm[b][:, j * 128:(j + 1) * 128],
                               wkv[:, j * 2048 + half * 512:j * 2048 + (half + 1) * 512],
                               start=(j == 0), stop=(j == 7))
        C.inc(mm, p + "avm")
        if i >= 1:
            C.wait(te, p + "ave", i)
        for j in range(8):
            for half in range(2):
                mm = te.matmul(pA[1][:, half * 512:(half + 1) * 512],
                               fm[b][:, j * 128:(j + 1) * 128],
                               wkv[:, j * 2048 + 1024 + half * 512:
                                   j * 2048 + 1024 + (half + 1) * 512],
                               start=(j == 0), stop=(j == 7))
        C.inc(mm, p + "avm")
        C.wait(v, p + "avm", C.count(p + "avm") - 1)
        C.inc(v.tensor_tensor(ktm[:, :], pA[0][:, :], qkvb_kv[:, 0:1024], ALU.add),
              p + "ake")
        C.wait(v, p + "acs", 32 * (i + 1))
        C.wait(v, p + "ake")
        emit_rope(C, p, ktm, krb, cs[b], p + "arp")
        C.wait(te, p + "arp")
        C.inc(emit_transpose_tile(C, krb, pT), p + "akt")
        C.wait(v, p + "akt")
        if i >= 1:
            C.wait(v, p + "akw", 16 * i)
        C.inc(v.tensor_copy(krs[:, :], pT[:, :]), p + "akr")
        C.wait(nc.gpsimd, p + "akr", C.count(p + "akr"))
        kdst = C.krs_hbm.ap()[:, :, i * 128:(i + 1) * 128].rearrange("h p k -> p h k")
        C.inc(nc.gpsimd.dma_start(kdst, krs[:, :].rearrange("p (h k) -> p h k", h=8)),
              p + "akw", 16)
        C.wait(v, p + "avm")
        vdst = V_tm[:, i * 1040:(i + 1) * 1040].rearrange(
            "p (h e) -> p h e", h=16)[:, :, 0:64]
        vsrc = pA[1][:, :].rearrange("p (h e) -> p h e", h=16)
        bsr = qkvb_kv[:, 1024:2048].rearrange("p (h e) -> p h e", h=16)
        C.inc(v.tensor_tensor(vdst, vsrc, bsr, ALU.add), p + "ave")
    nc.all_engine_barrier()
    C.end_frame()

    C.frame()
    wq = C.sbuf(p + "_wq", [128, 8 * 1024], BF16)
    wo = C.sbuf(p + "_wo", [128, 8 * 1024], BF16)
    wq_sem = load_weight_fm(C, wq, C.in_ap["L1_attn_a_qkv_w"][:, 0:1024], D, 1024)
    wo_sem = load_weight_fm(C, wo, C.in_ap["L1_attn_a_out_w"], D, 1024)
    outb = C.sbuf(p + "_outb", [128, D], F32)
    C.inc(nc.sync.dma_start(outb[:, :], C.in_ap["c_l1a_outb_rep"][:, :]), p + "bw", 16)
    xt4 = [C.sbuf(f"{p}_x2{i}", [128, D], F32) for i in range(4)]
    lnb2 = [C.sbuf(f"{p}_l2{i}", [128, D], BF16) for i in range(2)]
    fm2 = [C.sbuf(f"{p}_f2{i}", [128, D], BF16) for i in range(2)]
    qtm = C.sbuf(p + "_qtm", [128, D], F32)
    qrb = C.sbuf(p + "_qrb", [128, D], BF16)
    Qs = C.sbuf(p + "_Qs", [128, 8 * 512], BF16)
    Kr = [C.sbuf(f"{p}_Kr{i}", [128, 4096], BF16) for i in range(2)]
    Pb = [C.sbuf(f"{p}_Pb{i}", [128, 1024], BF16) for i in range(2)]
    lrow = C.sbuf(p + "_lr", [1, 2048], F32)
    Osb = C.sbuf(p + "_Osb", [128, 512], F32)
    cs2 = [C.sbuf(f"{p}_c2{i}", [128, 1024], F32) for i in range(2)]
    xo2 = [C.sbuf(f"{p}_xo{i}", [128, D], F32) for i in range(2)]
    stats2 = C.sbuf(p + "_st2", [128, 16], F32)
    scr2 = C.sbuf(p + "_sc2", [128, 1], F32)

    for sg in range(NSUP):
        if sg > 0:
            C.wait(nc.sync, p + "bep", 4 * sg)
        for tau in range(4):
            i = sg * 4 + tau
            C.inc(nc.sync.dma_start(xt4[tau][:, :], x_in[i * 128:(i + 1) * 128, :]),
                  p + "bld", 16)
        for tau in range(4):
            i = sg * 4 + tau
            b = i % 2
            if i >= 2:
                C.wait(nc.sync, p + "brp", 2 * (i - 1))
            C.inc(nc.sync.dma_start(cs2[b][:, 0:512],
                                    cos_rep[i * 128:(i + 1) * 128, :]), p + "bcs", 16)
            C.inc(nc.sync.dma_start(cs2[b][:, 512:1024],
                                    sin_rep[i * 128:(i + 1) * 128, :]), p + "bcs", 16)
            C.wait(v, p + "bld", 16 * (4 * sg + tau + 1))
            C.wait(s, p + "bld", 16 * (4 * sg + tau + 1))
            if i >= 2:
                C.wait(v, p + "bstp", i - 1)
            emit_ln_tile(C, xt4[tau], lnb2[b], stats2, scr2, sem=p + "bln")
            C.wait(te, p + "bln")
            if i >= 1:
                C.wait(te, p + "bqc", i)
            C.inc(emit_transpose_tile(C, lnb2[b], pT), p + "bstp")
            C.wait(v, p + "bstp")
            if i >= 2:
                C.wait(v, p + "bqm", i - 1)
            C.inc(v.tensor_copy(fm2[b][:, :], pT[:, :]), p + "bev")
            C.wait(te, p + "bev")
            C.wait(te, p + "bw")
            C.wait(te, wq_sem)
            C.wait(te, wo_sem)
            if sg > 0 and tau == 0:
                C.wait(te, p + "laos", 16 * sg)
                C.wait(te, p + "laex", 256 * sg)
                C.wait(te, p + "bep0", 4 * sg)
            if i >= 1:
                C.wait(te, p + "bqe", i)
            for j in range(8):
                for half in range(2):
                    mm = te.matmul(pA[0][:, half * 512:(half + 1) * 512],
                                   fm2[b][:, j * 128:(j + 1) * 128],
                                   wq[:, j * 1024 + half * 512:j * 1024 + (half + 1) * 512],
                                   start=(j == 0), stop=(j == 7))
            C.inc(mm, p + "bqm")
            C.wait(v, p + "bqm")
            C.inc(v.tensor_tensor(qtm[:, :], pA[0][:, :], qkvb_q[:, :], ALU.add),
                  p + "bqe")
            C.wait(v, p + "bcs", 32 * (4 * sg + tau + 1))
            C.wait(v, p + "bqe")
            emit_rope(C, p, qtm, qrb, cs2[b], p + "brp")
            C.wait(te, p + "brp")
            C.wait(te, p + "bev", C.count(p + "bev"))
            C.inc(emit_transpose_tile(C, qrb, pT), p + "bqt")
            C.wait(v, p + "bqt")
            if sg > 0 and tau == 0:
                C.wait(v, p + "bom", 4 * sg)
            qdst = Qs[:, 0:8 * 512].rearrange("p (h q) -> p h q", h=8)[
                :, :, tau * 128:(tau + 1) * 128]
            C.inc(v.tensor_copy(qdst, pT[:, :].rearrange("p (h q) -> p h q", h=8)),
                  p + "bqc")
        if sg == 0:
            C.inc(nc.sync.dma_start(Kr[0][:, :], C.krs_hbm.ap()[0, :, :]),
                  p + "lakl", 16)
        for hp in range(8):
            ghp = sg * 8 + hp
            kb = ghp % 2
            if ghp + 1 < 8 * NSUP:
                C.wait(nc.sync, p + "lasm", max(0, 32 * ghp))
                C.inc(nc.sync.dma_start(Kr[1 - kb][:, :],
                                        C.krs_hbm.ap()[(hp + 1) % 8, :, :]),
                      p + "lakl", 16)
            C.wait(te, p + "lakl", 16 * (ghp + 1))
            C.wait(te, p + "bqc", C.count(p + "bqc"))
            for k in range(33):
                if k < 32:
                    bank = pA[k % 2]
                    C.wait(te, p + "laex", max(0, 32 * ghp + k - 1))
                    m1 = te.matmul(bank[:, 0:512], Kr[kb][0:64, k * 128:(k + 1) * 128],
                                   Qs[0:64, hp * 512:(hp + 1) * 512],
                                   start=True, stop=True, tile_position=(0, 0))
                    m2 = te.matmul(bank[:, 512:1024], Kr[kb][64:128, k * 128:(k + 1) * 128],
                                   Qs[64:128, hp * 512:(hp + 1) * 512],
                                   start=True, stop=True, tile_position=(64, 0))
                    C.inc(m2, p + "lasm")
                    C.wait(s, p + "lasm")
                    if 32 * ghp + k >= 2:
                        C.wait(s, p + "lapv", 32 * ghp + k - 1)
                    C.inc(s.activation(Pb[k % 2][:, :], bank[:, :], AF.Exp,
                                       bias=C.negc[:, 0:1], scale=0.125), p + "laex")
                if k > 0:
                    kk = k - 1
                    pb = Pb[kk % 2]
                    C.wait(te, p + "laex", 32 * ghp + kk + 1)
                    ve = V_tm[:, kk * 1040 + (2 * hp) * 65:kk * 1040 + (2 * hp) * 65 + 65]
                    vo = V_tm[:, kk * 1040 + (2 * hp + 1) * 65:
                              kk * 1040 + (2 * hp + 1) * 65 + 65]
                    if kk == 0 and ghp > 0:
                        C.wait(te, p + "laos", 2 * ghp)
                        C.wait(te, p + "lalb", ghp)
                    te.matmul(pB[0:65, 0:512], ve, pb[:, 0:512],
                              start=(kk == 0), stop=(kk == 31))
                    m4 = te.matmul(pB[0:65, 512:1024], vo, pb[:, 512:1024],
                                   start=(kk == 0), stop=(kk == 31))
                    C.inc(m4, p + "lapv")
            C.wait(v, p + "lapv")
            C.inc(v.tensor_copy(lrow[0:1, 0:512], pB[64:65, 0:512]), p + "lalr")
            C.inc(v.tensor_copy(lrow[0:1, 512:1024], pB[64:65, 512:1024]), p + "lalr")
            C.wait(v, p + "lalr")
            C.inc(v.reciprocal(lrow[0:1, 1024:2048], lrow[0:1, 0:1024]), p + "lalr")
            C.wait(v, p + "lalr")
            if ghp > 0:
                C.wait(v, p + "laos", 2 * ghp)
            C.inc(v.tensor_copy(Osb[0:64, 0:512], pB[0:64, 0:512]), p + "laoc")
            C.inc(v.tensor_copy(Osb[64:128, 0:512], pB[0:64, 512:1024]), p + "laoc")
            C.wait(te, p + "laoc")
            m5 = te.matmul(pB[0:64, 0:512], C.ones_f32[0:1, 0:64],
                           lrow[0:1, 1024:1536], start=True, stop=True)
            m6 = te.matmul(pB[0:64, 512:1024], C.ones_f32[0:1, 0:64],
                           lrow[0:1, 1536:2048], start=True, stop=True)
            C.inc(m6, p + "lalb")
            C.wait(v, p + "lalb")
            C.inc(v.tensor_tensor(Qs[0:64, hp * 512:(hp + 1) * 512], Osb[0:64, 0:512],
                                  pB[0:64, 0:512], ALU.mult), p + "laos")
            C.inc(v.tensor_tensor(Qs[64:128, hp * 512:(hp + 1) * 512], Osb[64:128, 0:512],
                                  pB[0:64, 512:1024], ALU.mult), p + "laos")
        C.wait(te, p + "laos")
        for tau in range(4):
            i = sg * 4 + tau
            bank = pA[tau % 2]
            if tau >= 2:
                C.wait(te, p + "bep0", 4 * sg + tau - 1)
            for j in range(8):
                for half in range(2):
                    mm = te.matmul(bank[:, half * 512:(half + 1) * 512],
                                   Qs[:, j * 512 + tau * 128:j * 512 + (tau + 1) * 128],
                                   wo[:, j * 1024 + half * 512:j * 1024 + (half + 1) * 512],
                                   start=(j == 0), stop=(j == 7))
            C.inc(mm, p + "bom")
            C.wait(v, p + "bom")
            if tau >= 2 or sg > 0:
                C.wait(v, p + "bres", 16 * (4 * sg + tau - 1))
            C.inc(v.tensor_tensor(xo2[tau % 2][:, :], bank[:, :], outb[:, :],
                                  ALU.add), p + "bep0")
            C.wait(v, p + "bep0")
            C.inc(v.tensor_tensor(xo2[tau % 2][:, :], xo2[tau % 2][:, :],
                                  xt4[tau][:, :], ALU.add), p + "bep")
            C.wait(nc.gpsimd, p + "bep", C.count(p + "bep"))
            C.inc(nc.gpsimd.dma_start(x_out[i * 128:(i + 1) * 128, :],
                                      xo2[tau % 2][:, :]), p + "bres", 16)
    C.wait(nc.sync, p + "bres")
    C.end_frame()



def emit_l1attn_b(C, x_in, x_out):
    nc = C.nc
    v, s, te, gp = nc.vector, nc.scalar, nc.tensor, nc.gpsimd
    p = "lb"

    wqT = C.sbuf(p + "_wqT", [128, 8 * 1024], BF16)
    wkT = C.sbuf(p + "_wkT", [128, 8 * 1024], BF16)
    wvT = C.sbuf(p + "_wvT", [128, 8 * 1024], BF16)
    stage = C.sbuf(p + "_stage", [128, 8 * 1024], BF16)
    W = C.in_ap["L1_attn_b_W_qkv"]
    prep_weight_T(C, p + "wq", wqT, W[0:1024, :], p + "q", stage=stage)
    prep_weight_T(C, p + "wk", wkT, W[1024:2048, :], p + "k", stage=stage,
                  prev_tp=p + "qtp", prev_cp=p + "qcp")
    prep_weight_T(C, p + "wv", wvT, W[2048:3072, :], p + "v", stage=stage,
                  prev_tp=p + "ktp", prev_cp=p + "kcp")

    bq = C.sbuf(p + "_bq", [128, 3 * D], F32)
    qdec = C.sbuf(p + "_qd", [128, 1], F32)
    kdec = C.sbuf(p + "_kd", [128, 1], F32)
    mloc = C.sbuf(p + "_ml", [128, 128], F32)
    bqs = C.sbuf(p + "_bqs", [128, D], F32)
    bks = C.sbuf(p + "_bks", [128, D], F32)
    C.inc(nc.sync.dma_start(bq[:, :], C.in_ap["c_l1b_bqkv_rep"][:, :]), p + "w", 16)
    C.inc(nc.sync.dma_start(qdec[:, :], C.in_ap["c_qdec"][:, :]), p + "w", 16)
    C.inc(nc.sync.dma_start(kdec[:, :], C.in_ap["c_kdec"][:, :]), p + "w", 16)
    C.inc(nc.sync.dma_start(mloc[:, :], C.in_ap["c_mloc"][:, :]), p + "w", 16)
    C.wait(v, p + "w")
    C.inc(v.tensor_scalar(bqs[:, :], bq[:, 0:1024], qdec[:, 0:1], None, ALU.mult),
          p + "w", 1)
    C.inc(v.tensor_scalar(bks[:, :], bq[:, 1024:2048], kdec[:, 0:1], None, ALU.mult),
          p + "w", 1)

    Sst = C.sbuf(p + "_Sst", [128, 8 * 1024], F32)
    Ssh = C.sbuf(p + "_Ssh", [128, 8 * 1024], BF16)
    C.inc(v.memset(Sst[:, :], 0.0), p + "w", 1)
    C.inc(gp.memset(Ssh[:, :], 0.0), p + "w", 1)

    xt = [C.sbuf(f"{p}_xt{i}", [128, D], F32) for i in range(3)]
    lnb = [C.sbuf(f"{p}_ln{i}", [128, D], BF16) for i in range(2)]
    fm = [C.sbuf(f"{p}_fm{i}", [128, D], BF16) for i in range(2)]
    qpt = C.sbuf(p + "_qpt", [128, D], BF16)
    qst = C.sbuf(p + "_qst", [128, D], BF16)
    kpt = C.sbuf(p + "_kpt", [128, D], BF16)
    kst = C.sbuf(p + "_kst", [128, D], BF16)
    vt = C.sbuf(p + "_vt", [128, D], BF16)
    qfm = C.sbuf(p + "_qfm", [128, D], BF16)
    qsfm = C.sbuf(p + "_qsfm", [128, D], BF16)
    kfm = C.sbuf(p + "_kfm", [128, D], BF16)
    Pb = C.sbuf(p + "_P", [128, 128], BF16)
    PT = C.sbuf(p + "_PT", [128, 128], BF16)
    xo = [C.sbuf(f"{p}_xo{i}", [128, D], F32) for i in range(2)]
    stats = C.sbuf(p + "_st", [128, 16], F32)
    scr = C.sbuf(p + "_scr", [128, 1], F32)

    def ldx7(j):
        if j >= 3:
            C.wait(nc.sync, p + "ep", (j - 3) + 1)
        C.inc(nc.sync.dma_start(xt[j % 3][:, :], x_in[j * 128:(j + 1) * 128, :]),
              p + "ld", 16)

    ldx7(0)
    ldx7(1)
    for c in range(NT):
        b = c % 2
        pa, pb_ = C.psT[0], C.psT[1]
        if c + 2 < NT:
            ldx7(c + 2)
        C.wait(v, p + "ld", 16 * (c + 1))
        C.wait(s, p + "ld", 16 * (c + 1))
        C.wait(v, p + "w")
        if c >= 2:
            C.wait(v, p + "stp", c - 1)
        emit_ln_tile(C, xt[c % 3], lnb[b], stats, scr, sem=p + "ln")
        C.wait(te, p + "ln")
        if c >= 1:
            C.wait(te, p + "ev", c)
            C.wait(te, p + "qsc", c)
        elif c == 0:
            C.wait(te, p + "vcp")
        C.inc(emit_transpose_tile(C, lnb[b], pa), p + "stp")
        C.wait(v, p + "stp")
        if c >= 2:
            C.wait(v, p + "vmm", c - 1)
        C.inc(v.tensor_copy(fm[b][:, :], pa[:, :]), p + "ev")
        C.wait(te, p + "ev")
        if c >= 1:
            C.wait(te, p + "qe", 2 * c)
            C.wait(te, p + "pmk", c)
        mm = None
        for half in range(2):
            for j in range(8):
                mm = te.matmul(C.psF[0][:, half * 512:(half + 1) * 512],
                               fm[b][:, j * 128:(j + 1) * 128],
                               wqT[:, j * 1024 + half * 512:j * 1024 + (half + 1) * 512],
                               start=(j == 0), stop=(j == 7))
        C.inc(mm, p + "qmm")
        C.wait(v, p + "qmm")
        if c >= 1:
            C.wait(v, p + "qtp", 2 * c)
        C.inc(v.tensor_tensor(qpt[:, :], C.psF[0][:, :], bq[:, 0:1024], ALU.add),
              p + "qe")
        C.inc(v.scalar_tensor_tensor(qst[:, :], C.psF[0][:, :], qdec[:, 0:1],
                                     bqs[:, :], ALU.mult, ALU.add), p + "qe")
        if c >= 1:
            C.wait(te, p + "sup", 8 * c)
        for half in range(2):
            for j in range(8):
                mm = te.matmul(C.psF[2][:, half * 512:(half + 1) * 512],
                               fm[b][:, j * 128:(j + 1) * 128],
                               wkT[:, j * 1024 + half * 512:j * 1024 + (half + 1) * 512],
                               start=(j == 0), stop=(j == 7))
        C.inc(mm, p + "kmm")
        C.wait(v, p + "kmm")
        if c >= 1:
            C.wait(v, p + "ktp", c)
            C.wait(v, p + "dsm", 16 * c)
        C.inc(v.tensor_tensor(kpt[:, :], C.psF[2][:, :], bq[:, 1024:2048], ALU.add),
              p + "ke")
        C.inc(v.scalar_tensor_tensor(kst[:, :], C.psF[2][:, :], kdec[:, 0:1],
                                     bks[:, :], ALU.mult, ALU.add), p + "ke")
        if c >= 1:
            C.wait(te, p + "ep", c)
        for half in range(2):
            for j in range(8):
                mm = te.matmul(C.psF[1][:, half * 512:(half + 1) * 512],
                               fm[b][:, j * 128:(j + 1) * 128],
                               wvT[:, j * 1024 + half * 512:j * 1024 + (half + 1) * 512],
                               start=(j == 0), stop=(j == 7))
        C.inc(mm, p + "vmm")
        C.wait(v, p + "vmm")
        if c >= 1:
            C.wait(v, p + "imm", c)
            C.wait(v, p + "dsm", 16 * c)
        C.inc(v.tensor_tensor(vt[:, :], C.psF[1][:, :], bq[:, 2048:3072], ALU.add),
              p + "ve")
        C.wait(te, p + "qe")
        if c >= 1:
            C.wait(te, p + "ptc", c)
        C.inc(emit_transpose_tile(C, qpt, pb_), p + "qtp")
        C.wait(v, p + "qtp")
        if c >= 1:
            C.wait(v, p + "smm", c)
            C.wait(v, p + "imm", c)
        C.inc(v.tensor_copy(qfm[:, :], pb_[:, :]), p + "qpc")
        C.inc(emit_transpose_tile(C, qst, pa), p + "qtp")
        C.wait(v, p + "qtp")
        C.inc(v.tensor_copy(qsfm[:, :], pa[:, :]), p + "qsc")
        C.wait(te, p + "qpc")
        C.wait(te, p + "ve")
        C.inc(emit_transpose_tile(C, kpt, pb_), p + "ktp")
        C.wait(v, p + "ktp")
        C.inc(v.tensor_copy(kfm[:, :], pb_[:, :]), p + "kfc")
        C.wait(te, p + "kfc")
        C.wait(te, p + "qe", C.count(p + "qe"))
        for j in range(8):
            mm = te.matmul(C.psF[0][:, 0:128], qfm[:, j * 128:(j + 1) * 128],
                           kfm[:, j * 128:(j + 1) * 128],
                           start=(j == 0), stop=(j == 7))
        C.inc(mm, p + "smm")
        C.wait(v, p + "smm")
        if c >= 1:
            C.wait(v, p + "ptp", c)
        C.inc(v.tensor_tensor(Pb[:, :], C.psF[0][:, 0:128], mloc[:, :], ALU.mult),
              p + "pmk")
        C.wait(te, p + "pmk")
        C.inc(te.transpose(pb_[:, 0:128], Pb[:, :], C.ident_bf[:, :]), p + "ptp")
        C.wait(v, p + "ptp")
        if c >= 1:
            C.wait(v, p + "imm", c)
        C.inc(v.tensor_copy(PT[:, :], pb_[:, 0:128]), p + "ptc")
        C.wait(te, p + "ptc")
        C.wait(te, p + "ve", C.count(p + "ve"))
        if c >= 1:
            C.wait(te, p + "shc", 8 * c)
        for half in range(2):
            for j in range(8):
                te.matmul(C.psF[1][:, half * 512:(half + 1) * 512],
                          qsfm[:, j * 128:(j + 1) * 128],
                          Ssh[:, j * 1024 + half * 512:j * 1024 + (half + 1) * 512],
                          start=(j == 0), stop=False)
            mm = te.matmul(C.psF[1][:, half * 512:(half + 1) * 512],
                           PT[:, 0:128],
                           vt[:, half * 512:(half + 1) * 512],
                           start=False, stop=True)
        C.inc(mm, p + "imm")
        for j in range(8):
            if j == 0:
                C.wait(te, p + "ke", C.count(p + "ke"))
            if c >= 1 or j >= 1:
                C.wait(te, p + "sup", 8 * c + j)
            m1 = te.matmul(C.psF[2][:, 0:512], kst[:, j * 128:(j + 1) * 128],
                           vt[:, 0:512], start=True, stop=True)
            m2 = te.matmul(C.psF[2][:, 512:1024], kst[:, j * 128:(j + 1) * 128],
                           vt[:, 512:1024], start=True, stop=True)
            C.inc(m2, p + "dsm", 2)
            C.wait(v, p + "dsm")
            if c >= 1:
                C.wait(v, p + "shc", 8 * c)
            C.inc(v.scalar_tensor_tensor(Sst[:, j * 1024:(j + 1) * 1024],
                                         Sst[:, j * 1024:(j + 1) * 1024],
                                         0.99 ** 128, C.psF[2][:, :],
                                         ALU.mult, ALU.add), p + "sup")
            ecp = gp if j % 2 == 0 else v
            C.wait(ecp, p + "sup")
            C.inc(ecp.tensor_copy(Ssh[:, j * 1024:(j + 1) * 1024],
                                  Sst[:, j * 1024:(j + 1) * 1024]), p + "shc")
        C.wait(v, p + "imm")
        if c >= 2:
            C.wait(v, p + "res", 16 * (c - 1))
        C.inc(v.tensor_tensor(xo[b][:, :], C.psF[1][:, :], xt[c % 3][:, :],
                              ALU.add), p + "ep")
        C.wait(nc.gpsimd, p + "ep", C.count(p + "ep"))
        C.inc(nc.gpsimd.dma_start(x_out[c * 128:(c + 1) * 128, :], xo[b][:, :]),
              p + "res", 16)
    C.wait(nc.sync, p + "res")



def emit_l0attn_b(C, x_in, x_out, parts=("A", "B"), blevel=4):
    nc = C.nc
    v, s, te, gp = nc.vector, nc.scalar, nc.tensor, nc.gpsimd
    p = "sb"

    A_sb = C.sbuf(p + "_A", [32, T], F32)

    if "A" not in parts:
        C.frame()
        C.end_frame()
    C.frame()
    pT = C.psum(p + "a_pT", [128, 1024], BF16)
    pA = [C.psum(f"{p}a_pA{i}", [128, 1024], F32) for i in range(2)]
    pB = C.psum(p + "a_pB", [128, 1024], F32)

    W_in = C.sbuf(p + "_Win", [128, 8 * 6176], BF16)
    win_sem = load_weight_fm(C, W_in, C.in_ap["L0_attn_b_W_in"], D, 6176)
    winA = C.sbuf(p + "_WinA", [128, 8 * 32], F32)
    apA = C.in_ap["L0_attn_b_W_in"][:, 6144:6176].rearrange("(c p) o -> p c o", p=128)
    C.inc(nc.sync.dma_start(winA[:, :].rearrange("p (c o) -> p c o", c=8), apA),
          p + "aw", 16)
    diag = C.sbuf(p + "_diag", [128, 160 * 128], BF16)
    C.inc(nc.sync.dma_start(
        diag[:, :].rearrange("p (d j) -> p d j", d=160),
        C.in_ap["c_conv_diag"].rearrange("d p j -> p d j")), p + "aw", 16)
    convb = C.sbuf(p + "_convb", [128, 5120], BF16)
    C.inc(nc.sync.dma_start(convb[:, :], C.in_ap["c_convb_fm_rep"][:, :]), p + "aw", 16)

    xt = C.sbuf(p + "_xt", [128, D], F32)
    lnb = [C.sbuf(f"{p}_ln{i}", [128, D], BF16) for i in range(2)]
    lnf = C.sbuf(p + "_lnf", [128, D], F32)
    fm = [C.sbuf(f"{p}_fm{i}", [128, D], BF16) for i in range(2)]
    fm32 = C.sbuf(p + "_fm32", [128, D], F32)
    xpad = C.sbuf(p + "_xpad", [128, 40 * 131], BF16)
    halo = C.sbuf(p + "_halo", [128, 120], BF16)
    cvo = [C.sbuf(f"{p}_cvo{i}", [128, 1024], BF16) for i in range(2)]
    zsb = C.sbuf(p + "_zsb", [128, 1024], F32)
    stats = C.sbuf(p + "_st", [128, 16], F32)
    scr = C.sbuf(p + "_scr", [128, 1], F32)

    xpadv = xpad[:, 0:40 * 131].rearrange("p (cb t) -> p cb t", cb=40)

    C.inc(v.memset(xpadv[:, :, 0:3], 0.0), p + "ahl")

    for c in range(NT if "A" in parts else 0):
        b = c % 2
        if c >= 1:
            C.wait(nc.sync, p + "aln", 7 * c)
        C.inc(nc.sync.dma_start(xt[:, :], x_in[c * 128:(c + 1) * 128, :]),
              p + "ald", 16)
        C.wait(v, p + "ald")
        C.wait(s, p + "ald")
        if c >= 2:
            C.wait(v, p + "astp", c - 1)
        if c >= 1:
            C.wait(v, p + "aftp", c)
        emit_ln_tile(C, xt, lnb[b], stats, scr, want_f32=lnf, sem=p + "aln")
        C.wait(te, p + "aln")
        if c >= 1:
            C.wait(te, p + "aev", c)
        C.inc(emit_transpose_tile(C, lnb[b], pT), p + "astp")
        C.wait(v, p + "astp")
        if c >= 2:
            C.wait(v, p + "azm", c - 1)
        C.inc(v.tensor_copy(fm[b][:, :], pT[:, :]), p + "aev")
        if c >= 1:
            C.wait(te, p + "aze", c)
        tr = None
        for j in range(8):
            tr = te.transpose(pB[:, j * 128:(j + 1) * 128],
                              lnf[:, j * 128:(j + 1) * 128], C.ident_f32[:, :])
        C.inc(tr, p + "aftp")
        C.wait(v, p + "aftp")
        if c >= 1:
            C.wait(v, p + "aam", c)
        C.inc(v.tensor_copy(fm32[:, :], pB[:, :]), p + "af32e")
        C.wait(te, p + "af32e")
        C.wait(te, p + "aw")
        mm = None
        for j in range(8):
            mm = te.matmul(pB[0:32, 0:128], winA[:, j * 32:(j + 1) * 32],
                           fm32[:, j * 128:(j + 1) * 128],
                           start=(j == 0), stop=(j == 7))
        C.inc(mm, p + "aam")
        C.wait(v, p + "aam")
        C.inc(v.tensor_copy(A_sb[:, c * 128:(c + 1) * 128], pB[0:32, 0:128]),
              p + "aae")
        C.wait(te, p + "aev", C.count(p + "aev"))
        C.wait(te, win_sem)
        for g in range(5):
            if c >= 1 and g == 0:
                C.wait(te, p + "acv", 5 * c)
            if c >= 1 or g >= 2:
                C.wait(te, p + "axe", 5 * c + g - 1)
            for cbl in range(8):
                cb = g * 8 + cbl
                for j in range(8):
                    mm = te.matmul(pA[g % 2][:, cbl * 128:(cbl + 1) * 128],
                                   W_in[:, j * 6176 + cb * 128:j * 6176 + (cb + 1) * 128],
                                   fm[b][:, j * 128:(j + 1) * 128],
                                   start=(j == 0), stop=(j == 7))
            C.inc(mm, p + "apm")
            C.wait(v, p + "apm")
            if c >= 1 and g == 0:
                C.wait(v, p + "acm", 5 * c)
                C.wait(v, p + "ahl", c + 1)
            C.inc(v.tensor_copy(xpadv[:, g * 8:(g + 1) * 8, 3:131],
                                pA[g % 2][:, :].rearrange("p (cb t) -> p cb t", cb=8)),
                  p + "axe")
            C.wait(v, p + "axe")
            if c >= 1:
                C.inc(v.tensor_copy(xpadv[:, g * 8:(g + 1) * 8, 0:3],
                                    halo[:, 0:120].rearrange("p (cb e) -> p cb e", cb=40)
                                    [:, g * 8:(g + 1) * 8, :]), p + "axp")
            else:
                C.inc(v.engine_nop(), p + "axp")
        for g in range(5):
            C.wait(te, p + "axp", 5 * c + g + 1)
            C.wait(te, p + "axe", 5 * (c + 1))
            C.wait(te, p + "acv", max(0, 5 * c + g - 1))
            for cbl in range(8):
                cb = g * 8 + cbl
                for e in range(4):
                    mm = te.matmul(pA[g % 2][:, cbl * 128:(cbl + 1) * 128],
                                   diag[:, (cb * 4 + e) * 128:(cb * 4 + e + 1) * 128],
                                   xpadv[:, cb, e:e + 128],
                                   start=(e == 0), stop=(e == 3))
            C.inc(mm, p + "acm")
            C.wait(v, p + "acm")
            if 5 * c + g >= 2:
                C.wait(v, p + "acw", 16 * (5 * c + g - 1))
            C.inc(v.tensor_tensor(cvo[g % 2][:, :], pA[g % 2][:, :],
                                  convb[:, g * 1024:(g + 1) * 1024], ALU.add),
                  p + "acv")
            C.wait(nc.gpsimd, p + "acv", C.count(p + "acv"))
            dst = C.xbc_hbm.ap()[:, g * 8:(g + 1) * 8, c * 128:(c + 1) * 128]
            C.inc(nc.gpsimd.dma_start(
                dst, cvo[g % 2][:, :].rearrange("p (cb t) -> p cb t", cb=8)),
                p + "acw", 16)
        C.wait(v, p + "axe")
        C.inc(v.tensor_copy(halo[:, 0:120].rearrange("p (cb e) -> p cb e", cb=40),
                            xpadv[:, :, 128:131]), p + "ahl")
        C.wait(te, p + "aae")
        for half in range(2):
            for j in range(8):
                mm = te.matmul(pB[:, half * 512:(half + 1) * 512],
                               fm[b][:, j * 128:(j + 1) * 128],
                               W_in[:, j * 6176 + 5120 + half * 512:
                                    j * 6176 + 5120 + (half + 1) * 512],
                               start=(j == 0), stop=(j == 7))
        C.inc(mm, p + "azm")
        C.wait(v, p + "azm")
        if c >= 1:
            C.wait(v, p + "azw", 16 * c)
        C.inc(v.tensor_copy(zsb[:, :], pB[:, :]), p + "aze")
        C.wait(nc.gpsimd, p + "aze", C.count(p + "aze"))
        C.inc(nc.gpsimd.dma_start(C.z_hbm.ap()[c * 128:(c + 1) * 128, :], zsb[:, :]),
              p + "azw", 16)
    C.wait(nc.sync, p + "acw")
    C.wait(nc.sync, p + "azw")
    nc.all_engine_barrier()
    C.end_frame()

    if getattr(C, "dbg_ssd", None) is not None:
        C.frame()
        dsem = C.sem("dbgssd")
        C.inc(nc.sync.dma_start(C.dbg_ssd["xbc"],
                                C.xbc_hbm.ap().rearrange("P cb t -> P (cb t)")),
              "dbgssd", 16)
        C.inc(nc.sync.dma_start(C.dbg_ssd["z"], C.z_hbm.ap()[:, :]), "dbgssd", 16)
        C.wait(nc.sync, "dbgssd")
        C.end_frame()

    C.frame()
    pT = C.psum(p + "b_pT", [128, 1024], BF16)
    pTC = C.psum(p + "b_pTC", [128, 512], F32)
    pY = C.psum(p + "b_pY", [128, 1024], F32)
    pS = C.psum(p + "b_pS", [64, 1024], F32)
    pE = [C.psum(f"{p}b_pE{i}", [128, 512], F32) for i in range(2)]

    W_out = C.sbuf(p + "_Wout", [128, 8 * 1024], BF16)
    wout_sem = load_weight_fm(C, W_out, C.in_ap["L0_attn_b_W_out"], D, 1024)
    dpr = C.sbuf(p + "_dpr", [128, 1024], F32)
    dp_bf = C.sbuf(p + "_dpb", [128, 1024], BF16)
    big = C.sbuf(p + "_big", [128, 128], BF16)
    indic = C.sbuf(p + "_ind", [1, 256], BF16)
    indicf32 = C.sbuf(p + "_indf32", [32, 4096], F32)
    identx4 = C.sbuf(p + "_idx4", [128, 512], BF16)
    sel = C.sbuf(p + "_sel", [32, 1024], F32)
    C.inc(nc.sync.dma_start(dpr[:, :], C.in_ap["c_dp_rep"][:, :]), p + "bw", 16)
    C.inc(nc.sync.dma_start(big[:, :], C.in_ap["c_biglowT"][:, :]), p + "bw", 16)
    C.inc(nc.sync.dma_start(indic[:, :], C.in_ap["c_indic"][:, :]), p + "bw", 16)
    C.inc(nc.sync.dma_start(indicf32[:, :], C.in_ap["c_indicf32"][:, :]), p + "bw", 16)
    C.inc(nc.sync.dma_start(identx4[:, :], C.in_ap["c_identx4"][:, :]), p + "bw", 16)
    C.inc(nc.sync.dma_start(sel[:, :], C.in_ap["c_selector"][:, :]), p + "bw", 16)
    C.wait(v, p + "bw")
    C.inc(v.tensor_copy(dp_bf[:, :], dpr[:, :]), p + "bw", 1)

    xbc = C.sbuf(p + "_xbc", [128, 5120], BF16)
    x_tm = C.sbuf(p + "_xtm", [128, 1024], BF16)
    B_tm = C.sbuf(p + "_Btm", [128, 2048], BF16)
    Bw_tm = C.sbuf(p + "_Bw", [128, 2048], BF16)
    dts = C.sbuf(p + "_dts", [32, 4 * 128], F32)
    dt_fm = C.sbuf(p + "_dt", [32, 128], F32)
    cum_fm = C.sbuf(p + "_cum", [32, 128], F32)
    zeros32 = C.sbuf(p + "_z32", [32, 128], F32)
    ncum_fm = C.sbuf(p + "_ncfm", [32, 128], F32)
    erow_fm = C.sbuf(p + "_erfm", [32, 128], BF16)
    ncum_row = C.sbuf(p + "_ncrow", [1, 4096], F32)
    erow_bf = C.sbuf(p + "_erow", [1, 4096], BF16)
    totcol = C.sbuf(p + "_totc", [32, 1], F32)
    totb = C.sbuf(p + "_totb", [32, 64], F32)
    ones3264 = C.sbuf(p + "_o64", [32, 64], F32)
    Em_sb = [C.sbuf(f"{p}_Em{i}", [128, 512], F32) for i in range(2)]
    P4 = [C.sbuf(f"{p}_P4{i}", [128, 512], BF16) for i in range(2)]
    Csc_all = C.sbuf(p + "_Csc", [64, 8 * 512], BF16)
    Sst = C.sbuf(p + "_Sst", [64, 1024], F32)
    Ssh = C.sbuf(p + "_Ssh", [64, 1024], BF16)
    t1s = C.sbuf(p + "_t1", [64, 512], F32)
    xdp = C.sbuf(p + "_xdp", [128, 1024], F32)
    ysum = C.sbuf(p + "_ys", [128, 1024], F32)
    gsb = C.sbuf(p + "_g", [128, 1024], F32)
    zb = C.sbuf(p + "_zbl", [128, 1024], F32)
    ygated = C.sbuf(p + "_yg", [128, 1024], BF16)
    yg_fm = C.sbuf(p + "_ygf", [128, 1024], BF16)
    xr = C.sbuf(p + "_xr", [128, 1024], F32)
    xo = [C.sbuf(f"{p}_xo{i}", [128, 1024], F32) for i in range(2)]
    zbias = C.sbuf(p + "_zbias", [128, 1024], F32)
    C.inc(nc.sync.dma_start(zbias[:, :], C.in_ap["c_zbias_rep"][:, :]), p + "bw", 16)

    C.inc(v.memset(zeros32[:, :], 0.0), p + "bw", 1)
    C.inc(v.memset(ones3264[:, :], 1.0), p + "bw", 1)
    C.inc(v.memset(Sst[:, :], 0.0), p + "bw", 1)
    C.inc(gp.memset(Ssh[:, :], 0.0), p + "bw", 1)

    xbcv = xbc[:, 0:5120].rearrange("p (cb t) -> p cb t", cb=40)

    nb = (C.ssd_nchunks if getattr(C, "ssd_nchunks", None) else NT)
    for c in range(nb if "B" in parts else 0):
        b = c % 2
        for g in range(5):
            if c >= 1 and g == 0:
                C.wait(nc.sync, p + "bgm", 8 * c)
                C.wait(nc.sync, p + "bcsv", 32 * c)
                C.wait(nc.sync, p + "bxe", c)
                C.wait(nc.sync, p + "bnm", 8 * c)
            src_ = C.xbc_hbm.ap()[:, g * 8:(g + 1) * 8, c * 128:(c + 1) * 128]
            C.inc(nc.sync.dma_start(
                xbc[:, g * 1024:(g + 1) * 1024].rearrange("p (cb t) -> p cb t", cb=8),
                src_), p + "bxl", 16)
        if c >= 1:
            C.wait(nc.sync, p + "bsg2", 2 * c)
        C.inc(nc.sync.dma_start(zb[:, :], C.z_hbm.ap()[c * 128:(c + 1) * 128, :]),
              p + "bzl", 16)
        if c >= 1:
            C.wait(nc.sync, p + "bep", c)
        C.inc(nc.sync.dma_start(xr[:, :], x_in[c * 128:(c + 1) * 128, :]),
              p + "bxt", 16)
        C.wait(te, p + "bxl", 16 * (5 * c + 1))
        if c >= 1:
            C.wait(te, p + "byge", c)
        C.inc(emit_transpose_tile(C, xbc, pT), p + "bxtp")
        C.wait(v, p + "bxtp")
        if c >= 1:
            C.wait(v, p + "bslm", 8 * c)
        C.inc(v.tensor_copy(x_tm[:, :], pT[:, :]), p + "bxe")
        C.wait(te, p + "bxl", 16 * (5 * c + 3))
        C.wait(te, p + "bxe")
        C.inc(emit_transpose_tile(C, xbc, pT, src_off=8 * 128), p + "bbtp")
        C.wait(v, p + "bbtp")
        if c >= 1:
            C.wait(v, p + "bbwk", 32 * c)
        C.inc(v.tensor_copy(B_tm[:, 0:1024], pT[:, :]), p + "bbe")
        C.wait(te, p + "bbe")
        C.inc(emit_transpose_tile(C, xbc, pT, src_off=16 * 128), p + "bbtp")
        C.wait(v, p + "bbtp")
        C.inc(v.tensor_copy(B_tm[:, 1024:2048], pT[:, :]), p + "bbe")
        if getattr(C, "ssd_cut", None) == 1:
            continue
        C.wait(s, p + "bdt", C.count(p + "bdt"))
        C.inc(s.activation(dts[:, 0:128], A_sb[:, c * 128:(c + 1) * 128], AF.Abs),
              p + "bdt")
        C.wait(s, p + "bdt")
        C.inc(s.activation(dts[:, 128:256], dts[:, 0:128], AF.Exp, scale=-1.0),
              p + "bdt")
        C.wait(s, p + "bdt")
        C.inc(s.activation(dts[:, 256:384], dts[:, 128:256], AF.Ln,
                           bias=C.one_col[0:32, 0:1]), p + "bdt")
        C.wait(s, p + "bdt")
        C.inc(s.activation(dts[:, 384:512], A_sb[:, c * 128:(c + 1) * 128], AF.Relu),
              p + "bdt")
        C.wait(v, p + "bdt")
        if c >= 1:
            C.wait(v, p + "bcum", 2 * c)
        C.inc(v.tensor_tensor(dt_fm[:, :], dts[:, 256:384], dts[:, 384:512], ALU.add),
              p + "bcum")
        C.wait(v, p + "bcum")
        if c >= 1:
            C.wait(v, p + "bemm", 24 * c)
            C.wait(v, p + "berw", c)
            C.wait(v, p + "btot", c)
        C.inc(v.tensor_tensor_scan(cum_fm[:, :], dt_fm[:, :], zeros32[:, :], 0.0,
                                   ALU.add, ALU.add), p + "bcum")
        C.wait(v, p + "bcum")
        if c >= 1:
            C.wait(v, p + "bcw", 32 * c)
        C.inc(v.tensor_scalar(ncum_fm[:, :], cum_fm[:, :], -1.0, None, ALU.mult),
              p + "bncm")
        C.wait(s, p + "bcum", C.count(p + "bcum"))
        if c >= 1:
            C.wait(s, p + "bcw", 32 * c)
            C.wait(s, p + "btt", c)
        C.inc(s.activation(erow_fm[:, :], cum_fm[:, :], AF.Exp, scale=-1.0),
              p + "berw")
        C.inc(s.activation(totcol[:, :], cum_fm[:, 127:128], AF.Exp, scale=-1.0),
              p + "btot")
        C.wait(nc.sync, p + "bncm")
        C.wait(nc.sync, p + "berw")
        if c >= 1:
            C.wait(nc.sync, p + "bcrl", 32 * c)
        C.inc(nc.sync.dma_start(C.bounce_hbm.ap()[:, :], ncum_fm[:, :]),
              p + "bcw", 16)
        C.inc(nc.sync.dma_start(C.bounce2_hbm.ap()[:, :], erow_fm[:, :]),
              p + "bcw", 16)
        C.wait(nc.sync, p + "bcw")
        if c >= 1:
            C.wait(nc.sync, p + "bemm", 24 * c)
            C.wait(nc.sync, p + "btcm", 8 * c)
        C.inc(nc.sync.dma_start(ncum_row[0:1, :],
                                C.bounce_hbm.ap().rearrange("h t -> (h t)")[None, :]),
              p + "bcrl", 16)
        C.inc(nc.sync.dma_start(erow_bf[0:1, :],
                                C.bounce2_hbm.ap().rearrange("h t -> (h t)")[None, :]),
              p + "bcrl", 16)
        if getattr(C, "ssd_cut", None) == 2:
            continue
        for qq in range(8):
            if c == 0 and qq == 0:
                C.wait(te, p + "bw")
            C.wait(te, p + "bcrl", 32 * (c + 1))
            C.wait(te, p + "bcsv", C.count(p + "bcsv"))
            if c >= 1 and qq == 0:
                C.wait(te, p + "bgm", 8 * c)
            for ml in range(2):
                he = 2 * (2 * qq + ml)
                te.matmul(pTC[:, 256 + ml * 128:256 + (ml + 1) * 128],
                          indic[0:1, 0:128], erow_bf[0:1, he * 128:(he + 1) * 128],
                          start=True, stop=False)
                mm = te.matmul(pTC[:, 256 + ml * 128:256 + (ml + 1) * 128],
                               indic[0:1, 128:256],
                               erow_bf[0:1, (he + 1) * 128:(he + 2) * 128],
                               start=False, stop=True)
            C.inc(mm, p + "btcm")
            C.wait(v, p + "btcm")
            C.wait(v, p + "bxl", 16 * (5 * c + 5))
            if c >= 1 and qq == 0:
                C.wait(v, p + "bnm", 8 * c)
            for hh in range(4):
                base = 64 * (hh % 2)
                cb_ = 24 + 2 * qq + hh // 2
                C.inc(v.tensor_tensor(
                    Csc_all[0:64, qq * 512 + hh * 128:qq * 512 + (hh + 1) * 128],
                    xbcv[:, cb_, :][base:base + 64, :],
                    pTC[base + 0:base + 64, 256 + (hh // 2) * 128:
                        256 + (hh // 2) * 128 + 128],
                    ALU.mult), p + "bcsv")
        for q in range(9):
            gq = 8 * c + q
            if q < 8:
                C.wait(te, p + "bcrl", 32 * (c + 1))
                C.wait(te, p + "bcum", C.count(p + "bcum"))
                if c >= 1 and q == 0:
                    C.wait(te, p + "bst", 2 * c)
                if gq >= 1:
                    C.wait(te, p + "bex", gq)
                te.matmul(pE[0][:, 0:512], C.ones_f32[0:1, 0:128],
                          ncum_row[0:1, 4 * q * 128:(4 * q + 4) * 128],
                          start=True, stop=False)
                te.matmul(pE[0][:, 0:512], cum_fm[0:32, :],
                          indicf32[0:32, q * 512:(q + 1) * 512],
                          start=False, stop=False)
                mm = te.matmul(pE[0][:, 0:512], big[:, :], identx4[:, 0:512],
                               start=False, stop=True)
                C.inc(mm, p + "bemm", 3)
                C.wait(s, p + "bemm")
                if gq >= 2:
                    C.wait(s, p + "bpq", 2 * (gq - 1))
                    C.wait(s, p + "bbwk", 4 * (gq - 1))
                C.inc(s.activation(Em_sb[q % 2][:, :], pE[0][:, :], AF.Exp), p + "bex")
                C.wait(s, p + "bex", C.count(p + "bex"))
                C.wait(s, p + "bbe", C.count(p + "bbe"))
                if c >= 1 and q == 0:
                    C.wait(s, p + "bslm", 8 * c)
                for hh in range(4):
                    h = 4 * q + hh
                    C.inc(s.activation(Bw_tm[:, h * 64:(h + 1) * 64],
                                       B_tm[:, h * 64:(h + 1) * 64], AF.Identity,
                                       scale=Em_sb[q % 2][:, hh * 128 + 127:
                                                          hh * 128 + 128]),
                          p + "bbwk")
                C.wait(te, p + "bex", C.count(p + "bex"))
                C.wait(te, p + "bbe", C.count(p + "bbe"))
                C.wait(te, p + "bxl", 16 * (5 * c + 5))
                if gq >= 1:
                    C.wait(te, p + "bpq", 2 * gq)
                C.wait(te, p + "bcsv", C.count(p + "bcsv"))
                for ml in range(2):
                    m = 2 * q + ml
                    bcb = xbcv[:, 8 + m, :]
                    ccb = xbcv[:, 24 + m, :]
                    te.matmul(pE[1][:, ml * 128:(ml + 1) * 128],
                              bcb[0:64, :], ccb[0:64, :], start=True, stop=True,
                              tile_position=(0, 0))
                    mm = te.matmul(pTC[:, ml * 128:(ml + 1) * 128],
                                   bcb[64:128, :], ccb[64:128, :], start=True, stop=True,
                                   tile_position=(64, 0))
                C.inc(mm, p + "bgm")
                C.wait(v, p + "bgm")
                C.wait(v, p + "bex", C.count(p + "bex"))
                if gq >= 2:
                    C.wait(v, p + "bim", gq - 1)
                p4v = P4[q % 2][:, 0:512].rearrange("a (sl t) -> a sl t", sl=4)
                emv = Em_sb[q % 2][:, 0:512].rearrange("a (sl t) -> a sl t", sl=4)
                C.inc(v.tensor_tensor(p4v[:, 0:4:2, :],
                                      pE[1][:, 0:256].rearrange("a (sl t) -> a sl t", sl=2),
                                      emv[:, 0:4:2, :], ALU.mult), p + "bpq")
                C.inc(v.tensor_tensor(p4v[:, 1:4:2, :],
                                      pTC[:, 0:256].rearrange("a (sl t) -> a sl t", sl=2),
                                      emv[:, 1:4:2, :], ALU.mult), p + "bpq")
            if q >= 1:
                qq1 = q - 1
                C.wait(te, p + "bpq", 2 * (8 * c + qq1) + 2)
                C.wait(te, p + "bbwk", 4 * (8 * c + qq1) + 4)
                C.wait(te, p + "bxe", C.count(p + "bxe"))
                if c >= 1 and qq1 == 0:
                    C.wait(te, p + "bye", c)
                    C.wait(te, p + "bep", c)
                    C.wait(te, p + "bsu", 2 * c)
                    C.wait(te, p + "bsh", 2 * c)
                for hh in range(4):
                    h = 4 * qq1 + hh
                    te.matmul(pY[:, h * 32:(h + 1) * 32],
                              P4[qq1 % 2][:, hh * 128:(hh + 1) * 128],
                              x_tm[:, h * 32:(h + 1) * 32], start=True, stop=False)
                    mm = te.matmul(pY[:, h * 32:(h + 1) * 32],
                                   Csc_all[0:64, qq1 * 512 + hh * 128:
                                           qq1 * 512 + (hh + 1) * 128],
                                   Ssh[0:64, h * 32:(h + 1) * 32], start=False, stop=True)
                C.inc(mm, p + "bnm")
                for hh in range(0, 4):
                    h = 4 * qq1 + hh
                    mm = te.matmul(pS[0:64, h * 32:(h + 1) * 32],
                                   Bw_tm[:, h * 64:(h + 1) * 64],
                                   x_tm[:, h * 32:(h + 1) * 32], start=True, stop=True)
                C.inc(mm, p + "bim")
                C.inc(v.engine_nop(), p + "bslm")
        if blevel < 3:
            continue
        C.wait(v, p + "btot")
        if c >= 1:
            C.wait(v, p + "bsm2", 2 * c)
        C.inc(v.tensor_scalar(totb[:, :], ones3264[:, :], totcol[:, 0:1], None,
                              ALU.mult), p + "btt")
        C.wait(te, p + "btt")
        C.wait(te, p + "bpq", C.count(p + "bpq"))
        C.wait(te, p + "bex", C.count(p + "bex"))
        C.wait(te, p + "bim", C.count(p + "bim"))
        for H in range(2):
            C.inc(te.matmul(pE[H][0:64, 0:512], totb[:, :],
                            sel[:, H * 512:(H + 1) * 512], start=True, stop=True),
                  p + "bsm2")
            C.wait(v, p + "bsm2")
            C.inc(v.tensor_tensor(t1s[:, :], Sst[:, H * 512:(H + 1) * 512],
                                  pE[H][0:64, 0:512], ALU.mult),
                  p + "bst")
            C.wait(v, p + "bst")
            C.inc(v.tensor_tensor(Sst[:, H * 512:(H + 1) * 512], t1s[:, :],
                                  pS[0:64, H * 512:(H + 1) * 512], ALU.add),
                  p + "bsu")
            C.wait(v, p + "bsu")
            C.inc(v.tensor_copy(Ssh[0:64, H * 512:(H + 1) * 512],
                                Sst[:, H * 512:(H + 1) * 512]), p + "bsh")
        if getattr(C, "ssd_cut", None) == 4:
            continue
        if blevel < 4:
            continue
        C.wait(v, p + "bnm", C.count(p + "bnm"))
        C.wait(v, p + "bxe", C.count(p + "bxe"))
        C.wait(v, p + "bw")
        C.inc(v.tensor_tensor(xdp[:, :], x_tm[:, :], dp_bf[:, :], ALU.mult), p + "bxd")
        C.wait(v, p + "bxd")
        C.inc(v.tensor_tensor(ysum[:, :], pY[:, :], xdp[:, :], ALU.add), p + "bye")
        C.wait(v, p + "bzl", 16 * (c + 1))
        if c >= 1:
            C.wait(v, p + "bsg", 2 * c)
        C.inc(v.tensor_tensor(zb[:, :], zb[:, :], zbias[:, :], ALU.add), p + "bsg")
        C.wait(s, p + "bsg")
        if c >= 1:
            C.wait(s, p + "byg", c)
        C.inc(s.activation(gsb[:, :], zb[:, :], AF.Tanh, scale=0.5), p + "bsg")
        C.wait(v, p + "bsg")
        C.inc(v.tensor_scalar(gsb[:, :], gsb[:, :], 0.5, 0.5, ALU.mult, ALU.add),
              p + "bsg2")
        C.wait(v, p + "bsg2")
        C.inc(v.tensor_tensor(gsb[:, :], gsb[:, :], zb[:, :], ALU.mult), p + "bsg2")
        C.wait(v, p + "bsg2")
        C.wait(v, p + "bye")
        if c >= 1:
            C.wait(v, p + "bygt", c)
        C.inc(v.tensor_tensor(ygated[:, :], ysum[:, :], gsb[:, :], ALU.mult), p + "byg")
        C.wait(te, p + "byg")
        C.wait(te, p + "bbe", C.count(p + "bbe"))
        C.inc(emit_transpose_tile(C, ygated, pT), p + "bygt")
        C.wait(v, p + "bygt")
        if c >= 1:
            C.wait(v, p + "bwm", c)
        C.inc(v.tensor_copy(yg_fm[:, :], pT[:, :]), p + "byge")
        C.wait(te, p + "byge")
        C.wait(te, wout_sem)
        C.wait(te, p + "bye", C.count(p + "bye"))
        for half in range(2):
            for j in range(8):
                mm = te.matmul(pY[:, half * 512:(half + 1) * 512],
                               yg_fm[:, j * 128:(j + 1) * 128],
                               W_out[:, j * 1024 + half * 512:j * 1024 + (half + 1) * 512],
                               start=(j == 0), stop=(j == 7))
        C.inc(mm, p + "bwm")
        C.wait(v, p + "bwm")
        C.wait(v, p + "bxt", 16 * (c + 1))
        if c >= 2:
            C.wait(v, p + "bres", 16 * (c - 1))
        C.inc(v.tensor_tensor(xo[b][:, :], pY[:, :], xr[:, :], ALU.add), p + "bep")
        C.wait(nc.gpsimd, p + "bep", C.count(p + "bep"))
        C.inc(nc.gpsimd.dma_start(x_out[c * 128:(c + 1) * 128, :], xo[b][:, :]),
              p + "bres", 16)
    if "B" in parts and blevel >= 4 and getattr(C, "ssd_cut", None) is None:
        C.wait(nc.sync, p + "bres")
    C.end_frame()



def build_program(passes=ALL_PASSES, debug_wT=False, ssd_parts=("A", "B"),
                  debug_ssd=False, ssd_blevel=4, ssd_nchunks=None, ssd_cut=None,
                  ssd_qcut=5):
    nc = bass.Bass("TRN2", target_bir_lowering=False)
    C = Ctx(nc)
    C.ssd_nchunks = ssd_nchunks
    C.ssd_cut = ssd_cut
    C.ssd_qcut = ssd_qcut
    if debug_ssd:
        C.dbg_ssd = {
            "xbc": nc.dram_tensor("dbg_xbc", [128, 40 * T], BF16,
                                  kind="ExternalOutput").ap(),
            "z": nc.dram_tensor("dbg_z", [T, D], F32, kind="ExternalOutput").ap(),
        }
    if debug_wT:
        C.dbg_wT = nc.dram_tensor("dbg_wT", [128, 8 * 1024], BF16,
                                  kind="ExternalOutput").ap()
        C.dbg_dcol = nc.dram_tensor("dbg_dcol", [128, 2], F32,
                                    kind="ExternalOutput").ap()
        C.dbg_asum = nc.dram_tensor("dbg_asum", [128, 9], F32,
                                    kind="ExternalOutput").ap()
        C.dbg_ones = nc.dram_tensor("dbg_ones", [1, 128], F32,
                                    kind="ExternalOutput").ap()

    names_shapes = {
        "L0_attn_a_x": [T, D],
        "L0_attn_a_Wq": [D, D], "L0_attn_a_Wk": [D, D], "L0_attn_a_Wv": [D, D],
        "L0_attn_b_W_in": [D, 6176], "L0_attn_b_conv_w": [5120, 4],
        "L0_attn_b_conv_b": [5120], "L0_attn_b_z_bias": [D],
        "L0_attn_b_Dp": [32], "L0_attn_b_W_out": [D, D],
        "L0_ffn_W_r": [D, D], "L0_ffn_W_k": [D, D], "L0_ffn_W_v": [D, D],
        "L0_ffn_mu_r": [D], "L0_ffn_mu_k": [D],
        "L1_attn_a_qkv_w": [D, 3 * D], "L1_attn_a_qkv_b": [3 * D],
        "L1_attn_a_out_w": [D, D], "L1_attn_a_out_b": [D],
        "L1_attn_b_W_qkv": [3 * D, D], "L1_attn_b_b_qkv": [3 * D],
        "L1_ffn_weight": [D, D], "L1_ffn_alpha": [1], "L1_ffn_bias": [D],
        "c_ident_bf": ([128, 128], BF16),
        "c_ident_f32": [128, 128],
        "c_ones_f32": [1, 128],
        "c_mask_ut": [128, 128],
        "c_rope_cos": [T, 32], "c_rope_sin": [T, 32],
        "c_rope_cos_rep": [T, 512], "c_rope_sin_rep": [T, 512],
        "c_mloc": [128, 128],
        "c_qdec": [128, 1], "c_kdec": [128, 1],
        "c_alpha_rep": [128, 1],
        "c_l1ffn_bias_rep": [128, 1024],
        "c_l1a_qkvb_rep": [128, 3 * D],
        "c_l1a_outb_rep": [128, D],
        "c_l1b_bqkv_rep": [128, 3 * D],
        "c_zbias_rep": [128, D],
        "c_dp_rep": [128, D],
        "c_mu_fm": [128, 16],
        "c_convw_fm": [128, 160],
        "c_convb_fm": [128, 40],
        "c_conv_diag": ([160, 128, 128], BF16),
        "c_convb_fm_rep": ([128, 5120], BF16),
        "c_biglowT": ([128, 128], BF16),
        "c_indic": ([1, 256], BF16),
        "c_indicf32": [32, 4096],
        "c_identx4": ([128, 512], BF16),
        "c_selector": [32, 1024],
    }
    C.in_ap = {}
    for name, spec in names_shapes.items():
        if isinstance(spec, tuple):
            shape, dt = spec
        else:
            shape, dt = spec, F32
        C.in_ap[name] = nc.dram_tensor(name, shape, dt, kind="ExternalInput").ap()

    out_ap = nc.dram_tensor("out", [T, D], F32, kind="ExternalOutput").ap()

    C.xs = [nc.dram_tensor(f"xs{i}", [T, D], F32, kind="Internal").ap()
            for i in range(2)]
    C.tern_scratch = nc.dram_tensor("tern_scratch", [128, 8 * 1024], BF16,
                                    kind="Internal")
    C.l0ffn_scratch = {nm: nc.dram_tensor(f"l0ffn_w{nm}", [128, 8 * 1024], BF16,
                                          kind="Internal") for nm in ("r", "k", "v")}
    C.krs_hbm = nc.dram_tensor("krs", [8, 128, T], BF16, kind="Internal")
    C.xbc_hbm = nc.dram_tensor("xbc_scr", [128, 40, T], BF16, kind="Internal")
    C.z_hbm = nc.dram_tensor("z_scr", [T, D], F32, kind="Internal")
    C.bounce_hbm = nc.dram_tensor("bounce", [32, 128], F32, kind="Internal")
    C.bounce2_hbm = nc.dram_tensor("bounce2", [32, 128], BF16, kind="Internal")

    C.ident_bf = nc.alloc_sbuf_tensor("ident_bf", [128, 128], BF16)
    C.ident_f32 = nc.alloc_sbuf_tensor("ident_f32", [128, 128], F32)
    C.ones_f32 = nc.alloc_sbuf_tensor("ones_f32", [1, 128], F32)
    C.eps_col = nc.alloc_sbuf_tensor("eps_col", [128, 1], F32)
    C.negc = nc.alloc_sbuf_tensor("negc", [128, 1], F32)
    C.one_col = nc.alloc_sbuf_tensor("one_col", [128, 1], F32)
    C.inc(nc.sync.dma_start(C.ident_bf[:, :], C.in_ap["c_ident_bf"][:, :]), "init", 16)
    C.inc(nc.sync.dma_start(C.ident_f32[:, :], C.in_ap["c_ident_f32"][:, :]), "init", 16)
    C.inc(nc.sync.dma_start(C.ones_f32[:, :], C.in_ap["c_ones_f32"][:, :]), "init", 16)
    C.inc(nc.vector.memset(C.eps_col[:, :], EPS), "init", 1)
    C.inc(nc.vector.memset(C.negc[:, :], -20.0), "init", 1)
    C.inc(nc.vector.memset(C.one_col[:, :], 1.0), "init", 1)
    for eng in (nc.vector, nc.scalar, nc.tensor, nc.gpsimd, nc.sync):
        C.wait(eng, "init")
    nc.all_engine_barrier()

    cur = C.in_ap["L0_attn_a_x"]
    bufi = 0
    emitters = {
        "l1ffn_1": lambda ci, co: emit_l1ffn(C, ci, co, first=True),
        "l1ffn_2": lambda ci, co: emit_l1ffn(C, ci, co, first=False),
        "l0attn_a": lambda ci, co: emit_l0attn_a(C, ci, co),
        "l1attn_a": lambda ci, co: emit_l1attn_a(C, ci, co),
        "l1attn_b": lambda ci, co: emit_l1attn_b(C, ci, co),
        "l0attn_b": lambda ci, co: emit_l0attn_b(C, ci, co, parts=ssd_parts,
                                                 blevel=ssd_blevel),
        "l0ffn_1": lambda ci, co: emit_l0ffn(C, ci, co, first=True),
        "l0ffn_2": lambda ci, co: emit_l0ffn(C, ci, co, first=False),
    }
    for i, pname in enumerate(passes):
        dest = out_ap if i == len(passes) - 1 else C.xs[bufi]
        C.frame()
        if pname != "l0attn_b":
            C.psT = [C.psum(f"psT{i}_{pname}", [128, 1024], BF16) for i in range(2)]
            C.psF = [C.psum(f"psF{i}_{pname}", [128, 1024], F32) for i in range(3)]
        emitters[pname](cur, dest)
        nc.all_engine_barrier()
        C.end_frame()
        cur = dest
        bufi ^= 1
    return nc



def make_const_inputs():
    import ml_dtypes
    c = {}
    c["c_ident_bf"] = np.eye(128, dtype=ml_dtypes.bfloat16)
    c["c_ident_f32"] = np.eye(128, dtype=np.float32)
    c["c_ones_f32"] = np.ones((1, 128), dtype=np.float32)
    tt = np.arange(128)
    c["c_mask_ut"] = (tt[None, :] >= tt[:, None]).astype(np.float32)
    hd = 64
    freqs = 10000.0 ** (np.arange(0, hd, 2.0, dtype=np.float32) / hd)
    pos = np.arange(T, dtype=np.float32)
    ang = pos[:, None] * freqs[None, :]
    c["c_rope_cos"] = np.cos(ang).astype(np.float32)
    c["c_rope_sin"] = np.sin(ang).astype(np.float32)
    c["c_rope_cos_rep"] = np.ascontiguousarray(np.tile(c["c_rope_cos"], (1, 16)))
    c["c_rope_sin_rep"] = np.ascontiguousarray(np.tile(c["c_rope_sin"], (1, 16)))
    alpha = np.float32(0.99)
    dif = tt[:, None] - tt[None, :]
    c["c_mloc"] = np.where(dif >= 0, alpha ** dif, 0.0).astype(np.float32) / 32.0
    c["c_qdec"] = (alpha ** (tt + 1) / 32.0).astype(np.float32).reshape(128, 1)
    c["c_kdec"] = (alpha ** (127 - tt)).astype(np.float32).reshape(128, 1)
    blt = np.where(tt[None, :] > tt[:, None], -16384.0, 0.0)
    c["c_biglowT"] = blt.astype(ml_dtypes.bfloat16)
    ind = np.zeros((1, 256), np.float32)
    ind[0, 0:64] = 1.0
    ind[0, 192:256] = 1.0
    c["c_indic"] = ind.astype(ml_dtypes.bfloat16)
    ind32 = np.zeros((32, 4096), np.float32)
    for q in range(8):
        for hh in range(4):
            ind32[4 * q + hh, q * 512 + hh * 128:q * 512 + (hh + 1) * 128] = 1.0
    c["c_indicf32"] = ind32
    c["c_identx4"] = np.tile(np.eye(128, dtype=np.float32), (1, 4)).astype(
        ml_dtypes.bfloat16)
    selm = np.zeros((32, 1024), np.float32)
    for h in range(32):
        selm[h, h * 32:(h + 1) * 32] = 1.0
    c["c_selector"] = selm
    return c


def make_in_map(inputs, b, consts):
    import ml_dtypes
    m = {}
    for k, v in inputs.items():
        if k == "L0_attn_a_x":
            m[k] = np.ascontiguousarray(np.asarray(v)[b])
        else:
            m[k] = np.ascontiguousarray(np.asarray(v))
    m.update(consts)
    rep = lambda vec, n: np.ascontiguousarray(
        np.broadcast_to(np.asarray(vec, np.float32).reshape(1, -1), (128, n)))
    m["c_alpha_rep"] = rep(np.broadcast_to(np.asarray(inputs["L1_ffn_alpha"]), (1,)), 1)
    m["c_l1ffn_bias_rep"] = rep(inputs["L1_ffn_bias"], 1024)
    m["c_l1a_qkvb_rep"] = rep(inputs["L1_attn_a_qkv_b"], 3 * D)
    m["c_l1a_outb_rep"] = rep(inputs["L1_attn_a_out_b"], D)
    m["c_l1b_bqkv_rep"] = rep(inputs["L1_attn_b_b_qkv"], 3 * D)
    m["c_zbias_rep"] = rep(inputs["L0_attn_b_z_bias"], D)
    m["c_dp_rep"] = rep(np.repeat(np.asarray(inputs["L0_attn_b_Dp"], np.float32), 32), D)
    mu_fm = np.concatenate([np.asarray(inputs["L0_ffn_mu_r"], np.float32).reshape(8, 128).T,
                            np.asarray(inputs["L0_ffn_mu_k"], np.float32).reshape(8, 128).T],
                           axis=1)
    m["c_mu_fm"] = np.ascontiguousarray(mu_fm)
    cw = np.asarray(inputs["L0_attn_b_conv_w"], np.float32).reshape(40, 128, 4)
    m["c_convw_fm"] = np.ascontiguousarray(cw.transpose(1, 0, 2).reshape(128, 160))
    cb = np.asarray(inputs["L0_attn_b_conv_b"], np.float32).reshape(40, 128)
    m["c_convb_fm"] = np.ascontiguousarray(cb.T)
    cbr = np.asarray(inputs["L0_attn_b_conv_b"], np.float32).reshape(40, 128)
    m["c_convb_fm_rep"] = np.ascontiguousarray(
        np.broadcast_to(cbr.T[:, :, None], (128, 40, 128)).reshape(128, 5120)
        .astype(ml_dtypes.bfloat16))
    import ml_dtypes as _md
    cw = np.asarray(inputs["L0_attn_b_conv_w"], np.float32)
    dg = np.zeros((160, 128, 128), np.float32)
    for cbi in range(40):
        for e in range(4):
            np.fill_diagonal(dg[cbi * 4 + e], cw[cbi * 128:(cbi + 1) * 128, e])
    m["c_conv_diag"] = dg.astype(_md.bfloat16)
    return m


def kernel(**inputs):
    nc = build_program()
    consts = make_const_inputs()
    in_maps = [make_in_map(inputs, b, consts) for b in range(8)]
    res = run_bass_kernel_spmd(nc, in_maps, core_ids=list(range(8)))
    out = np.stack([np.asarray(r["out"]) for r in res.results], axis=0)
    return out.astype(np.float32)
